# version 05
# speedup vs baseline: 1.0823x; 1.0222x over previous
import contextlib
import math
import numpy as np

import concourse.bass as bass
import concourse.mybir as mybir
from concourse.bass_utils import run_bass_kernel_spmd

F32 = mybir.dt.float32
BF16 = mybir.dt.bfloat16
AF = mybir.ActivationFunctionType
ALU = mybir.AluOpType
AX = mybir.AxisListType

S = 2048
D = 1024
NT = 16
NC = 8
EPS_LN = 1e-5
LAMBDA_INIT = 0.8 - 0.6 * math.exp(-0.3 * 1)


class Sem:
    __slots__ = ("name", "count", "handle", "_cm")
    def __init__(self, name):
        self.name = name
        self.count = 0
        self.handle = None


class Res:
    __slots__ = ("name", "writer", "readers", "dma_sem", "init_waits")
    def __init__(self, name, dma_sem=None, init_waits=None):
        self.name = name
        self.writer = None
        self.readers = []
        self.dma_sem = dma_sem
        self.init_waits = dict(init_waits) if init_waits else {}


class Queue:
    __slots__ = ("name", "items", "sem", "hw")
    def __init__(self, name, sem):
        self.name = name
        self.items = []
        self.sem = sem
        self.hw = {}


class Task:
    def __init__(self, b, q, milestone_sem):
        self.b = b
        self.q = q
        self.milestone_sem = milestone_sem
        self.n = 0

    def emit(self, fn):
        self.q.items.append(("instr", fn, None, 0))
        self.n += 1

    def dma(self, out, in_, target_res, **kw):
        if target_res.dma_sem is None:
            target_res.dma_sem = self.b.new_sem("d_" + target_res.name)
        sem = target_res.dma_sem
        sem.count += 16
        self.q.items.append(
            ("instr", lambda e: e.dma_start(out=out, in_=in_, **kw), sem, 16))
        self.n += 1
        if not hasattr(self, "_dma_targets"):
            self._dma_targets = {}
        self._dma_targets[target_res] = sem.count
        self._dma_val = sem.count
        self._dma_sem = sem


class B:
    def __init__(self, nc):
        self.nc = nc
        self.sems = []
        self.queues = {}
        for qn in ("pe", "act", "dve", "pool", "sp"):
            self.queues[qn] = Queue(qn, self.new_sem("q_" + qn))
        self.tombstone = {}

    def new_sem(self, name):
        s = Sem(name)
        self.sems.append(s)
        return s

    def res(self, name, dma=False):
        return Res(name, dma_sem=None, init_waits=self.tombstone)

    def _wait(self, q, sem, val):
        if val <= 0:
            return
        if q.hw.get(sem, 0) >= val:
            return
        q.hw[sem] = val
        q.items.append(("wait", sem, val))

    @contextlib.contextmanager
    def task(self, qn, reads=(), writes=(), dma_target=None):
        q = self.queues[qn]
        waits = {}
        def need(sem, val):
            if sem is not None and val > waits.get(sem, 0):
                waits[sem] = val
        for r in reads:
            if r.writer:
                need(*r.writer)
            for sem, val in r.init_waits.items():
                need(sem, val)
            r.init_waits = {}
        for w in writes:
            if w.writer:
                need(*w.writer)
            for rd in w.readers:
                need(*rd)
            for sem, val in w.init_waits.items():
                need(sem, val)
            w.init_waits = {}
        for sem, val in sorted(waits.items(), key=lambda kv: kv[0].name):
            self._wait(q, sem, val)
        t = Task(self, q, q.sem)
        yield t
        if t.n == 0:
            return
        if hasattr(t, "_dma_sem"):
            all_ms = list(t._dma_targets.items())
            for w in writes:
                if w in t._dma_targets:
                    w.writer = (w.dma_sem, t._dma_targets[w])
                else:
                    w.writer = (t._dma_sem, t._dma_val)
                w.readers = []
            for r in reads:
                for res, val in all_ms:
                    r.readers.append((res.dma_sem, val))
        else:
            q.sem.count += 1
            ms = (q.sem, q.sem.count)
            for i in range(len(q.items) - 1, -1, -1):
                kind = q.items[i][0]
                if kind == "instr":
                    _, fn, s0, a0 = q.items[i]
                    assert s0 is None, "engine task mixing with dma incs"
                    q.items[i] = ("instr", fn, q.sem, 1)
                    break
            for w in writes:
                w.writer = ms
                w.readers = []
            for r in reads:
                r.readers.append(ms)

    def set_tombstone(self):
        self.tombstone = {s: s.count for s in self.sems if s.count > 0}

    def final_wait(self, qn):
        q = self.queues[qn]
        for s in self.sems:
            if s.count > 0:
                self._wait(q, s, s.count)

    def replay(self, block):
        handles = {}
        for s in self.sems:
            cm = self.nc.semaphore(s.name)
            s._cm = cm
            s.handle = cm.__enter__()
        def run(engine, q):
            for it in q.items:
                if it[0] == "wait":
                    engine.wait_ge(it[1].handle, it[2])
                else:
                    inst = it[1](engine)
                    if it[2] is not None:
                        inst.then_inc(it[2].handle, it[3])
        @block.tensor
        def _(e):
            run(e, self.queues["pe"])
        @block.scalar
        def _(e):
            run(e, self.queues["act"])
        @block.vector
        def _(e):
            run(e, self.queues["dve"])
        @block.gpsimd
        def _(e):
            run(e, self.queues["pool"])
        @block.sync
        def _(e):
            run(e, self.queues["sp"])


class Arena:
    def __init__(self, b, handle, cols):
        self.b = b
        self.h = handle
        self.cols = cols
        self.off = 0
        self.marks = []

    def alloc(self, name, n):
        assert self.off + n <= self.cols, \
            f"arena overflow: {name} needs {n} at {self.off}/{self.cols}"
        v = self.h[:, self.off:self.off + n]
        self.off += n
        return v, self.b.res(name)

    def mark(self):
        self.marks.append(self.off)

    def reset_to_mark(self):
        self.off = self.marks.pop()
        self.b.set_tombstone()

    def reset(self):
        self.off = 0
        self.marks = []
        self.b.set_tombstone()



def build_program(layers, n_cores=8, debug=None):
    nc = bass.Bass("TRN2", target_bir_lowering=False)
    AP = {}

    def din(name, shape, dtype=F32):
        AP[name] = nc.dram_tensor(name, list(shape), dtype, kind="ExternalInput").ap()
        return AP[name]

    din("x_in", (S, D))
    din("L0_attn_Wq", (D, D)); din("L0_attn_bq", (D,))
    din("L0_attn_Wk", (D, D)); din("L0_attn_bk", (D,))
    din("L0_attn_Wv", (D, D)); din("L0_attn_bv", (D,))
    din("L0_ffn_weight", (D, D))
    din("L1_attn_Wqkv", (D, 3 * D)); din("L1_attn_Wproj", (D, D))
    din("L1_attn_lambda_q1", (32,)); din("L1_attn_lambda_k1", (32,))
    din("L1_attn_lambda_q2", (32,)); din("L1_attn_lambda_k2", (32,))
    din("L1_attn_rms_scale", (64,))
    din("L1_ffn_gate_w", (8, D)); din("L1_ffn_expert_bias", (8,))
    din("L1_ffn_w_up", (8, 1408, D)); din("L1_ffn_w_down", (8, D, 1408))
    din("L1_ffn_shared_w_up", (1408, D)); din("L1_ffn_shared_w_down", (D, 1408))
    din("L2_attn_gamma", (D,)); din("L2_attn_w_qkv", (D, 3 * D)); din("L2_attn_w_out", (D, D))
    din("L2_ffn_Ws1", (D, 4096)); din("L2_ffn_bs1", (4096,))
    din("L2_ffn_Ws2", (2048, D)); din("L2_ffn_bs2", (D,))
    din("L2_ffn_We1", (12, D, 6144)); din("L2_ffn_be1", (12, 6144))
    din("L2_ffn_We2", (12, 3072, D)); din("L2_ffn_be2", (12, D))
    din("L2_ffn_Wr", (D, 12)); din("L2_ffn_br", (12,))
    din("L3_attn_w_qkv", (D, 3 * D)); din("L3_attn_w_proj", (D, D))
    din("L3_ffn_Wg", (D, 8))
    din("L3_ffn_Ws1", (D, 2048)); din("L3_ffn_Ws3", (D, 2048)); din("L3_ffn_Ws2", (2048, D))
    din("L3_ffn_We1", (8, D, 2048)); din("L3_ffn_We3", (8, D, 2048)); din("L3_ffn_We2", (8, 2048, D))
    din("c_eye", (128, 128))
    din("c_tri", (128, 128))
    din("c_alibid", (128, 256))
    din("c_onesrow", (128, 512))
    din("c_misc", (128, 8))
    out = nc.dram_tensor("out", [S, D], F32, kind="ExternalOutput").ap()
    g_spill = nc.dram_tensor("g_spill", [S, D], BF16, kind="Internal").ap()
    dscr = nc.dram_tensor("dscratch", [128, 2048], F32, kind="Internal").ap()

    b = B(nc)
    st = contextlib.ExitStack()
    with st:
        def sb(name, shape, dtype=BF16):
            h = st.enter_context(nc.sbuf_tensor(name, list(shape), dtype))
            r = b.res(name, dma=True)
            return h, r

        x_sb, x_r = sb("x", (128, NT * D), F32)
        x_t = [b.res(f"x{t}") for t in range(NT)]
        xnT_sb, xnT_r = sb("xnT", (128, NC * S), BF16)
        eye_sb, eye_r = sb("eye", (128, 128), BF16)
        eye_f32, eye32_r = sb("eye32", (128, 128), F32)
        tri_sb, tri_r = sb("tri", (128, 128), BF16)
        alibi_sb, alibi_r = sb("alibid", (128, 256), F32)
        ones_sb, ones_r = sb("onesrow", (128, 512), BF16)
        cols_sb, cols_r = sb("cols", (128, 64), F32)
        rows_sb, rows_r = sb("rows", (128, 128), F32)
        mvav_sb, mvav_r = sb("mvav", (128, 16), F32)
        bn6_sb, bn6_r = sb("bn6", (128, 12), F32)
        rstd2_sb, rstd2_r = sb("rstd2", (128, 16), F32)
        gbuf_sb, gbuf_r = sb("gbuf", (128, 2 * D), BF16)
        cmisc_sb, cmisc_r = sb("cmisc", (128, 8), F32)
        gsp_r = b.res("g_spill_dram", dma=True)
        out_r = b.res("out_dram", dma=True)
        af32_h = st.enter_context(nc.sbuf_tensor("af32", [128, 1280], F32))
        abf_h = st.enter_context(nc.sbuf_tensor("abf", [128, 50560], BF16))
        AR = Arena(b, abf_h, 50560)
        AF32 = Arena(b, af32_h, 1280)

        def ab(name, n):
            return AR.alloc(name, n)

        def af(name, n):
            return AF32.alloc(name, n)

        def reset_arenas():
            AR.reset()
            AF32.reset()

        PF = []
        for i in range(7):
            h = st.enter_context(nc.psum_tensor(f"pf{i}", [128, 512], F32))
            PF.append((h, b.res(f"pf{i}")))
        pbf_h = st.enter_context(nc.psum_tensor("pbf", [128, 1024], BF16))
        pbf_r = b.res("pbf")

        with b.task("sp", writes=[x_r] + x_t) as t:
            for half in range(2):
                t.dma(x_sb[:, half * 8 * D:(half + 1) * 8 * D].rearrange(
                          "p (t d) -> p t d", d=D),
                      AP["x_in"].rearrange("(h t p) d -> h p t d", h=2, p=128)[half],
                      x_r)
        with b.task("pool", writes=[eye_r, tri_r, alibi_r, ones_r, cmisc_r]) as t:
            t.dma(eye_sb[:], AP["c_eye"], eye_r)
            t.dma(eye_f32[:], AP["c_eye"], eye32_r)
            t.dma(tri_sb[:], AP["c_tri"], tri_r)
            t.dma(alibi_sb[:], AP["c_alibid"], alibi_r)
            t.dma(ones_sb[:], AP["c_onesrow"], ones_r)
            t.dma(cmisc_sb[:], AP["c_misc"], cmisc_r)

        for t in x_t:
            t.writer = x_r.writer

        PRECAST = [
            ("L0_attn_Wv", D, D), ("L0_attn_Wk", D, D), ("L0_attn_Wq", D, D),
            ("L0_ffn_weight", D, D),
            ("L1_ffn_shared_w_up", 1408, D), ("L1_ffn_shared_w_down", D, 1408),
            ("L1_ffn_w_up", 8 * 1408, D), ("L1_ffn_w_down", 8 * D, 1408),
            ("L1_attn_Wqkv", D, 3 * D), ("L1_attn_Wproj", D, D),
            ("L2_ffn_Ws1", D, 4096), ("L2_ffn_Ws2", 2048, D),
            ("L2_ffn_We1", 12 * D, 6144), ("L2_ffn_We2", 12 * 3072, D),
            ("L2_attn_w_qkv", D, 3 * D), ("L2_attn_w_out", D, D),
            ("L3_ffn_Ws1", D, 2048), ("L3_ffn_Ws3", D, 2048), ("L3_ffn_Ws2", 2048, D),
            ("L3_ffn_We1", 8 * D, 2048), ("L3_ffn_We3", 8 * D, 2048),
            ("L3_ffn_We2", 8 * 2048, D),
            ("L3_attn_w_qkv", D, 3 * D), ("L3_attn_w_proj", D, D),
        ]
        WBF = {}
        WBF_R = {}
        for name, rows, cols in PRECAST:
            WBF[name] = nc.dram_tensor("wbf_" + name, [rows, cols], BF16,
                                       kind="Internal").ap()
            WBF_R[name] = b.res("wbf_" + name, dma=True)
        CHUNK_ROWS = 2 * 1024 * 1024

        def emit_precast(name):
            rows, cols = next((r, c) for n, r, c in PRECAST if n == name)
            src = AP[name]
            if len(src.shape) == 3:
                src = src.rearrange("g n k -> (g n) k")
            step = max(1, CHUNK_ROWS // cols)
            with b.task("pool", writes=[WBF_R[name]], dma_target=WBF_R[name]) as tk:
                for r0 in range(0, rows, step):
                    r1 = min(r0 + step, rows)
                    tk.dma(WBF[name][r0:r1, :], src[r0:r1, :], WBF_R[name])

        for name, _, _ in PRECAST:
            emit_precast(name)

        def ln_phase(lay):
            AR.mark()
            xn_scr, xn_scr_r = ab(f"l{lay}_xnscr", 2 * D)
            for t in range(NT):
                xt = x_sb[:, t * D:(t + 1) * D]
                with b.task("dve", reads=[x_t[t]], writes=[bn6_r]) as tk:
                    tk.emit(lambda e, xt=xt: e.bn_stats(bn6_sb[:, 0:6], xt[:, 0:512]))
                    tk.emit(lambda e, xt=xt: e.bn_stats(bn6_sb[:, 6:12], xt[:, 512:1024]))
                with b.task("dve", reads=[bn6_r], writes=[mvav_r]) as tk:
                    tk.emit(lambda e: e.bn_aggr(mvav_sb[:, 0:2], bn6_sb[:]))
                with b.task("act", reads=[mvav_r, cmisc_r], writes=[mvav_r]) as tk:
                    tk.emit(lambda e: e.activation(mvav_sb[:, 2:3], mvav_sb[:, 1:2],
                                                   AF.Sqrt, bias=cmisc_sb[:, 0:1]))
                with b.task("dve", reads=[mvav_r], writes=[mvav_r]) as tk:
                    tk.emit(lambda e: e.reciprocal(mvav_sb[:, 3:4], mvav_sb[:, 2:3]))
                with b.task("dve", reads=[mvav_r], writes=[mvav_r]) as tk:
                    tk.emit(lambda e: e.tensor_scalar(
                        mvav_sb[:, 4:5], mvav_sb[:, 0:1], mvav_sb[:, 3:4], -1.0,
                        ALU.mult, ALU.mult))
                if lay == 2:
                    with b.task("act", reads=[mvav_r, cmisc_r], writes=[mvav_r]) as tk:
                        tk.emit(lambda e: e.activation(mvav_sb[:, 5:6], mvav_sb[:, 1:2],
                                                       AF.Sqrt, bias=cmisc_sb[:, 1:2]))
                    with b.task("dve", reads=[mvav_r], writes=[mvav_r]) as tk:
                        tk.emit(lambda e: e.reciprocal(mvav_sb[:, 6:7], mvav_sb[:, 5:6]))
                    with b.task("dve", reads=[mvav_r], writes=[rstd2_r]) as tk:
                        tk.emit(lambda e, t=t: e.tensor_scalar(
                            rstd2_sb[:, t:t + 1], mvav_sb[:, 2:3], mvav_sb[:, 6:7], None,
                            ALU.mult))
                slot = t % 2
                xns = xn_scr[:, slot * D:(slot + 1) * D]
                with b.task("act", reads=[mvav_r, x_t[t]], writes=[xn_scr_r]) as tk:
                    tk.emit(lambda e, xt=xt, xns=xns: e.activation(
                        xns, xt, AF.Identity,
                        bias=mvav_sb[:, 4:5], scale=mvav_sb[:, 3:4]))
                with b.task("pe", reads=[xn_scr_r, eye_r], writes=[pbf_r]) as tk:
                    for c in range(NC):
                        tk.emit(lambda e, xns=xns, c=c: e.transpose(
                            pbf_h[:, c * 128:(c + 1) * 128],
                            xns[:, c * 128:(c + 1) * 128], eye_sb[:]))
                with b.task("dve", reads=[pbf_r], writes=[xnT_r]) as tk:
                    dst = xnT_sb.ap().rearrange("p (c n) -> p c n", c=NC)[:, :, t * 128:(t + 1) * 128]
                    src = pbf_h.ap().rearrange("p (c n) -> p c n", c=NC)
                    tk.emit(lambda e, dst=dst, src=src: e.tensor_copy(dst, src))
            AR.reset_to_mark()

        def spill_g(lay, src_sb, src_r, t):
            with b.task("sp", reads=[src_r], writes=[gsp_r], dma_target=gsp_r) as tk:
                tk.dma(AP_gspill_tile(t), src_sb, gsp_r)

        def AP_gspill_tile(t):
            return g_spill.rearrange("(t p) d -> t p d", p=128)[t]

        def mm_group(tk, psum_ap, pairs, extra_first=None, first_start=True,
                     stop=True):
            first = first_start
            if extra_first is not None:
                lh, rh = extra_first
                tk.emit(lambda e, lh=lh, rh=rh, f=first: e.matmul(
                    psum_ap, lh, rh, start=f, stop=False))
                first = False
            for i, (lh, rh) in enumerate(pairs):
                last = stop and (i == len(pairs) - 1)
                tk.emit(lambda e, lh=lh, rh=rh, f=first, l=last: e.matmul(
                    psum_ap, lh, rh, start=f, stop=l))
                first = False

        def layer0():
            AR.mark()
            w_raw, w_raw_r = ab("l0_wraw", D)
            swt, swt_r = ab("l0_swt", NC * D)
            gtile, gtile_r = ab("l0_gt", 2 * D)
            W = WBF["L0_ffn_weight"].rearrange("(c p) i -> c p i", p=128)
            for c in range(NC):
                with b.task("sp", reads=[WBF_R["L0_ffn_weight"]], writes=[w_raw_r],
                            dma_target=w_raw_r) as tk:
                    tk.dma(w_raw[:], W[c], w_raw_r)
                with b.task("act", reads=[w_raw_r], writes=[w_raw_r]) as tk:
                    tk.emit(lambda e: e.activation(w_raw[:], w_raw[:], AF.Sign))
                with b.task("pe", reads=[w_raw_r, eye_r], writes=[pbf_r]) as tk:
                    for ic in range(NC):
                        tk.emit(lambda e, ic=ic: e.transpose(
                            pbf_h[:, ic * 128:(ic + 1) * 128],
                            w_raw[:, ic * 128:(ic + 1) * 128], eye_sb[:]))
                with b.task("dve", reads=[pbf_r], writes=[swt_r]) as tk:
                    dst = swt.rearrange("p (ic n) -> p ic n", ic=NC)[:, :, c * 128:(c + 1) * 128]
                    src = pbf_h.ap().rearrange("p (ic n) -> p ic n", ic=NC)
                    tk.emit(lambda e, dst=dst, src=src: e.tensor_copy(dst, src))
            for t in range(NT):
                slot = t % 2
                pb = [PF[slot * 2], PF[slot * 2 + 1]]
                for half in range(2):
                    ph, pr = pb[half]
                    with b.task("pe", reads=[swt_r, xnT_r], writes=[pr]) as tk:
                        pairs = [(xnT_sb[:, c * S + t * 128: c * S + (t + 1) * 128],
                                  swt[:, c * D + half * 512: c * D + half * 512 + 512])
                                 for c in range(NC)]
                        mm_group(tk, ph[:], pairs)
                for half in range(2):
                    ph, pr = pb[half]
                    with b.task("act", reads=[pr], writes=[gtile_r]) as tk:
                        tk.emit(lambda e, ph=ph, slot=slot, half=half: e.activation(
                            gtile[:, slot * D + half * 512: slot * D + half * 512 + 512],
                            ph[:], AF.Silu))
                with b.task("sp", reads=[gtile_r], dma_target=gsp_r, writes=[gsp_r]) as tk:
                    tk.dma(AP_gspill_tile(t), gtile[:, slot * D:(slot + 1) * D], gsp_r)
            AR.reset_to_mark()


            def transpose_w(name, wt_sb_, wt_r_, raw, raw_r):
                Wsrc = WBF[name].rearrange("(c p) i -> c p i", p=128)
                for c in range(NC):
                    with b.task("sp", reads=[WBF_R[name]], writes=[raw_r],
                                dma_target=raw_r) as tk:
                        tk.dma(raw[:], Wsrc[c], raw_r)
                    with b.task("pe", reads=[raw_r, eye_r], writes=[pbf_r]) as tk:
                        for ic in range(NC):
                            tk.emit(lambda e, ic=ic: e.transpose(
                                pbf_h[:, ic * 128:(ic + 1) * 128],
                                raw[:, ic * 128:(ic + 1) * 128], eye_sb[:]))
                    with b.task("dve", reads=[pbf_r], writes=[wt_r_]) as tk:
                        dst = wt_sb_.rearrange("p (ic n) -> p ic n", ic=NC)[:, :, c * 128:(c + 1) * 128]
                        src = pbf_h.ap().rearrange("p (ic n) -> p ic n", ic=NC)
                        tk.emit(lambda e, dst=dst, src=src: e.tensor_copy(dst, src))

            def load_bias_row(pad, pad_r, name, off, width):
                with b.task("dve", writes=[pad_r]) as tk:
                    tk.emit(lambda e: e.memset(pad[:], 0.0))
                with b.task("pool", writes=[pad_r], dma_target=pad_r) as tk:
                    tk.dma(pad[0:1, 0:width], AP[name].rearrange("(o d) -> o d", o=1), pad_r)

            vt, vt_r = ab("l0_vt", NT * 1040)
            AR.mark()
            wv_t, wv_t_r = ab("l0_wvT", NC * D)
            wraw2, wraw2_r = ab("l0_wraw2", D)
            bv_pad, bv_pad_r = ab("l0_bvpad", D)
            load_bias_row(bv_pad, bv_pad_r, "L0_attn_bv", 0, D)
            transpose_w("L0_attn_Wv", wv_t, wv_t_r, wraw2, wraw2_r)
            with b.task("dve", writes=[vt_r]) as tk:
                dst = vt.rearrange("p (t n) -> p t n", t=NT)[:, :, 1039:1040]
                tk.emit(lambda e, dst=dst: e.memset(dst, 1.0))
            for t in range(NT):
                for half in range(2):
                    ph, pr = PF[2 + half]
                    with b.task("pe", reads=[wv_t_r, xnT_r, ones_r, bv_pad_r], writes=[pr]) as tk:
                        pairs = [(xnT_sb[:, c * S + t * 128: c * S + (t + 1) * 128],
                                  wv_t[:, c * D + half * 512: c * D + half * 512 + 512])
                                 for c in range(NC)]
                        mm_group(tk, ph[:], pairs,
                                 extra_first=(ones_sb[:, 0:128],
                                              bv_pad[:, half * 512: half * 512 + 512]))
                    with b.task("dve", reads=[pr], writes=[vt_r]) as tk:
                        tk.emit(lambda e, ph=ph, t=t, half=half: e.tensor_copy(
                            vt[:, t * 1040 + half * 512: t * 1040 + half * 512 + 512], ph[:]))
            AR.reset_to_mark()

            kt, kt_r = ab("l0_kt", NC * S)
            AR.mark()
            wk_t, wk_t_r = ab("l0_wkT", NC * D)
            wraw3, wraw3_r = ab("l0_wraw3", D)
            bk_pad, bk_pad_r = ab("l0_bkpad", D)
            load_bias_row(bk_pad, bk_pad_r, "L0_attn_bk", 0, D)
            transpose_w("L0_attn_Wk", wk_t, wk_t_r, wraw3, wraw3_r)
            for m in range(NC):
                for ts4 in range(4):
                    ph, pr = PF[ts4 % 2]
                    with b.task("pe", reads=[wk_t_r, xnT_r, ones_r, bk_pad_r], writes=[pr]) as tk:
                        pairs = [(wk_t[:, c * D + m * 128: c * D + (m + 1) * 128],
                                  xnT_sb[:, c * S + ts4 * 512: c * S + ts4 * 512 + 512])
                                 for c in range(NC)]
                        mm_group(tk, ph[:], pairs,
                                 extra_first=(bk_pad[:, m * 128:(m + 1) * 128],
                                              ones_sb[:]))
                    with b.task("act", reads=[pr], writes=[kt_r]) as tk:
                        tk.emit(lambda e, ph=ph, m=m, ts4=ts4: e.activation(
                            kt[:, m * S + ts4 * 512: m * S + ts4 * 512 + 512], ph[:],
                            AF.Identity))
            AR.reset_to_mark()

            wq_t, wq_t_r = ab("l0_wqT", NC * D)
            bq_pad, bq_pad_r = ab("l0_bqpad", D)
            load_bias_row(bq_pad, bq_pad_r, "L0_attn_bq", 0, D)
            AR.mark()
            wraw4, wraw4_r = ab("l0_wraw4", D)
            transpose_w("L0_attn_Wq", wq_t, wq_t_r, wraw4, wraw4_r)
            AR.reset_to_mark()

            ktsq, ktsq_r = ab("l0_ktsq", S)
            onescol, onescol_r = ab("l0_onescol", 16)
            cbf, cbf_r = ab("l0_cbf", 16)
            with b.task("dve", writes=[onescol_r]) as tk:
                tk.emit(lambda e: e.memset(onescol[:], 1.0))

            attn_l0(kt, kt_r, vt, vt_r, wq_t, wq_t_r, bq_pad, bq_pad_r,
                    onescol, onescol_r, ktsq, ktsq_r, cbf, cbf_r)

        def attn_l0(kt, kt_r, vt, vt_r, wq_t, wq_t_r, bias_pad, bias_pad_r,
                    onescol, onescol_r, ktsq, ktsq_r, cbf, cbf_r):
            qt, qt_r = ab("l0_qt", NC * 128)
            qtsq, qtsq_r = ab("l0_qtsq", 128)
            ptb, ptb_r = ab("l0_pt", 2 * 128)
            tmpf, tmpf_r = af("l0_tmp", 512)
            negc, negc_r = af("l0_negc", 16)
            knorm, knorm_r = af("l0_knorm", 16)

            ph_n, pr_n = PF[4]
            for q4 in range(4):
                for c in range(NC):
                    with b.task("act", reads=[kt_r], writes=[ktsq_r]) as tk:
                        tk.emit(lambda e, c=c, q4=q4: e.activation(
                            ktsq[:, 0:512], kt[:, c * S + q4 * 512: c * S + q4 * 512 + 512],
                            AF.Square))
                    with b.task("pe", reads=[ktsq_r, onescol_r], writes=[pr_n]) as tk:
                        tk.emit(lambda e, c=c: e.matmul(
                            ph_n[0:1, 0:512], onescol[:, 0:1], ktsq[:, 0:512],
                            start=(c == 0), stop=(c == NC - 1)))
                with b.task("dve", reads=[pr_n], writes=[knorm_r]) as tk:
                    tk.emit(lambda e, q4=q4: e.tensor_reduce(
                        knorm[0:1, q4:q4 + 1], ph_n[0:1, 0:512], AX.X, ALU.max))
            with b.task("dve", reads=[knorm_r], writes=[knorm_r]) as tk:
                tk.emit(lambda e: e.tensor_reduce(knorm[0:1, 4:5], knorm[0:1, 0:4],
                                                  AX.X, ALU.max))
            with b.task("act", reads=[knorm_r], writes=[knorm_r]) as tk:
                tk.emit(lambda e: e.activation(knorm[0:1, 5:6], knorm[0:1, 4:5], AF.Sqrt))

            for t in range(NT):
                for c in range(NC):
                    ph, pr = PF[5]
                    with b.task("pe", reads=[wq_t_r, xnT_r, ones_r, bias_pad_r], writes=[pr]) as tk:
                        pairs = [(wq_t[:, ci * D + c * 128: ci * D + (c + 1) * 128],
                                  xnT_sb[:, ci * S + t * 128: ci * S + (t + 1) * 128])
                                 for ci in range(NC)]
                        mm_group(tk, ph[:, 0:128], pairs,
                                 extra_first=(bias_pad[:, c * 128:(c + 1) * 128],
                                              ones_sb[:, 0:128]))
                    with b.task("act", reads=[pr], writes=[qt_r]) as tk:
                        tk.emit(lambda e, ph=ph, c=c: e.activation(
                            qt[:, c * 128:(c + 1) * 128], ph[:, 0:128],
                            AF.Identity, scale=1.0 / 32.0))
                ph_n2, pr_n2 = PF[4]
                for c in range(NC):
                    with b.task("act", reads=[qt_r], writes=[qtsq_r]) as tk:
                        tk.emit(lambda e, c=c: e.activation(
                            qtsq[:], qt[:, c * 128:(c + 1) * 128], AF.Square))
                    with b.task("pe", reads=[qtsq_r, onescol_r], writes=[pr_n2]) as tk:
                        tk.emit(lambda e, c=c: e.matmul(
                            ph_n2[0:1, 0:128], onescol[:, 0:1], qtsq[:],
                            start=(c == 0), stop=(c == NC - 1)))
                with b.task("dve", reads=[pr_n2], writes=[knorm_r]) as tk:
                    tk.emit(lambda e: e.tensor_reduce(knorm[0:1, 6:7], ph_n2[0:1, 0:128],
                                                      AX.X, ALU.max))
                with b.task("act", reads=[knorm_r], writes=[knorm_r]) as tk:
                    tk.emit(lambda e: e.activation(knorm[0:1, 7:8], knorm[0:1, 6:7], AF.Sqrt))
                with b.task("dve", reads=[knorm_r], writes=[cbf_r]) as tk:
                    tk.emit(lambda e: e.tensor_tensor(cbf[0:1, 0:1], knorm[0:1, 7:8],
                                                      knorm[0:1, 5:6], ALU.mult))
                ph_b, pr_b = PF[6]
                with b.task("pe", reads=[cbf_r, ones_r], writes=[pr_b]) as tk:
                    tk.emit(lambda e: e.matmul(ph_b[:, 0:1], ones_sb[0:1, 0:128],
                                               cbf[0:1, 0:1], start=True, stop=True))
                with b.task("act", reads=[pr_b], writes=[negc_r]) as tk:
                    tk.emit(lambda e: e.activation(negc[:, 0:1], ph_b[:, 0:1],
                                                   AF.Identity, scale=-1.0))

                aoh = [PF[2], PF[3]]

                def emit_score0(kc):
                    ph_s, pr_s = PF[kc % 2]
                    with b.task("pe", reads=[kt_r, qt_r], writes=[pr_s]) as tk:
                        pairs = [(kt[:, c * S + kc * 128: c * S + (kc + 1) * 128],
                                  qt[:, c * 128:(c + 1) * 128]) for c in range(NC)]
                        mm_group(tk, ph_s[:, 0:128], pairs)

                emit_score0(0)
                for kc in range(t + 1):
                    if kc + 1 < t + 1:
                        emit_score0(kc + 1)
                    ph_s, pr_s = PF[kc % 2]
                    slot = kc % 2
                    with b.task("act", reads=[pr_s, negc_r], writes=[ptb_r]) as tk:
                        tk.emit(lambda e, ph_s=ph_s, slot=slot: e.activation(
                            ptb[:, slot * 128:(slot + 1) * 128], ph_s[:, 0:128],
                            AF.Exp, bias=negc[:, 0:1]))
                    if kc == t:
                        with b.task("dve", reads=[tri_r, ptb_r], writes=[ptb_r]) as tk:
                            tk.emit(lambda e, slot=slot: e.tensor_tensor(
                                ptb[:, slot * 128:(slot + 1) * 128],
                                ptb[:, slot * 128:(slot + 1) * 128], tri_sb[:], ALU.mult))
                    for half in range(2):
                        ph_a, pr_a = aoh[half]
                        with b.task("pe", reads=[ptb_r, vt_r], writes=[pr_a]) as tk:
                            tk.emit(lambda e, kc=kc, slot=slot, half=half, ph_a=ph_a,
                                    t=t: e.matmul(
                                ph_a[:], ptb[:, slot * 128:(slot + 1) * 128],
                                vt[:, kc * 1040 + half * 512: kc * 1040 + half * 512 + 512],
                                start=(kc == 0), stop=(kc == t)))
                    with b.task("pe", reads=[ptb_r, vt_r], writes=[pr_b]) as tk:
                        tk.emit(lambda e, kc=kc, slot=slot, t=t: e.matmul(
                            ph_b[:, 8:9], ptb[:, slot * 128:(slot + 1) * 128],
                            vt[:, kc * 1040 + 1039: kc * 1040 + 1040],
                            start=(kc == 0), stop=(kc == t)))
                with b.task("dve", reads=[pr_b], writes=[negc_r]) as tk:
                    tk.emit(lambda e: e.reciprocal(negc[:, 0:1], ph_b[:, 8:9]))
                gslot = t % 2
                with b.task("sp", reads=[gsp_r], writes=[gbuf_r], dma_target=gbuf_r) as tk:
                    tk.dma(gbuf_sb[:, gslot * D:(gslot + 1) * D], AP_gspill_tile(t), gbuf_r)
                for half in range(2):
                    ph_a, pr_a = aoh[half]
                    with b.task("dve", reads=[pr_a, negc_r, gbuf_r], writes=[tmpf_r]) as tk:
                        tk.emit(lambda e, ph_a=ph_a, half=half, gslot=gslot: e.scalar_tensor_tensor(
                            tmpf[:], ph_a[:], negc[:, 0:1],
                            gbuf_sb[:, gslot * D + half * 512: gslot * D + half * 512 + 512],
                            ALU.mult, ALU.mult))
                    with b.task("dve", reads=[tmpf_r, x_t[t]], writes=[x_t[t]]) as tk:
                        tk.emit(lambda e, t=t, half=half: e.tensor_tensor(
                            x_sb[:, t * D + half * 512: t * D + half * 512 + 512],
                            x_sb[:, t * D + half * 512: t * D + half * 512 + 512],
                            tmpf[:], ALU.add))


        def ffn_generic(lay):
            AR.mark()
            AF32.mark()
            nE = {1: 8, 2: 12, 3: 8}[lay]
            wcomb, wcomb_r = af(f"l{lay}_wcomb", 256)
            ccols, ccols_r = af(f"l{lay}_fc", 64)
            if debug == "w":
                scrdump, scrdump_r = af("dbg_scr", 128)
                rawdump, rawdump_r = af("dbg_raw", 128)
                with b.task("dve", writes=[wcomb_r, scrdump_r, rawdump_r]) as tk:
                    tk.emit(lambda e: e.memset(wcomb[:, 0:256], 0.0))
                    tk.emit(lambda e: e.memset(scrdump[:, 0:128], 0.0))
                    tk.emit(lambda e: e.memset(rawdump[:, 0:128], 0.0))
            if lay == 1:
                gw_raw, gw_raw_r = ab("l1_gwraw", D)
                gwt, gwt_r = ab("l1_gwt", 64)
                eb_bc, eb_bc_r = ab("l1_ebbc", 16)
                ebias_f, ebias_f_r = af("l1_ebias", 16)
                with b.task("dve", writes=[gw_raw_r]) as tk:
                    tk.emit(lambda e: e.memset(gw_raw[:], 0.0))
                with b.task("pool", writes=[gw_raw_r], dma_target=gw_raw_r) as tk:
                    tk.dma(gw_raw[0:8, :], AP["L1_ffn_gate_w"], gw_raw_r)
                with b.task("pe", reads=[gw_raw_r, eye_r], writes=[pbf_r]) as tk:
                    for c in range(NC):
                        tk.emit(lambda e, c=c: e.transpose(
                            pbf_h[:, c * 128:(c + 1) * 128],
                            gw_raw[:, c * 128:(c + 1) * 128], eye_sb[:]))
                with b.task("dve", reads=[pbf_r], writes=[gwt_r]) as tk:
                    src = pbf_h.ap().rearrange("p (c n) -> p c n", c=NC)[:, :, 0:8]
                    dst = gwt.rearrange("p (c n) -> p c n", n=8)
                    tk.emit(lambda e, src=src, dst=dst: e.tensor_copy(dst, src))
                ebrow, ebrow_r = ab("l1_ebrow", 16)
                with b.task("dve", writes=[ebrow_r]) as tk:
                    tk.emit(lambda e: e.memset(ebrow[:], 0.0))
                with b.task("pool", writes=[ebrow_r], dma_target=ebrow_r) as tk:
                    tk.dma(ebrow[0:1, 0:8],
                           AP["L1_ffn_expert_bias"].rearrange("(o d) -> o d", o=1), ebrow_r)
                ph_t, pr_t = PF[6]
                with b.task("pe", reads=[ebrow_r, ones_r], writes=[pr_t]) as tk:
                    tk.emit(lambda e: e.matmul(ph_t[:, 0:8], ones_sb[0:1, 0:128],
                                               ebrow[0:1, 0:8], start=True, stop=True))
                with b.task("dve", reads=[pr_t], writes=[ebias_f_r]) as tk:
                    tk.emit(lambda e: e.tensor_copy(ebias_f[:, 0:8], ph_t[:, 0:8]))
            elif lay == 2:
                wr_bf, wr_bf_r = ab("l2_wr", NC * 16)
                br_pad, br_pad_r = ab("l2_brpad", 16)
                with b.task("pool", writes=[wr_bf_r], dma_target=wr_bf_r) as tk:
                    W = AP["L2_ffn_Wr"].rearrange("(c p) e -> c p e", p=128)
                    for c in range(NC):
                        tk.dma(wr_bf[:, c * 16:c * 16 + 12], W[c], wr_bf_r)
                with b.task("dve", writes=[br_pad_r]) as tk:
                    tk.emit(lambda e: e.memset(br_pad[:], 0.0))
                with b.task("pool", writes=[br_pad_r], dma_target=br_pad_r) as tk:
                    tk.dma(br_pad[0:1, 0:12],
                           AP["L2_ffn_br"].rearrange("(o d) -> o d", o=1), br_pad_r)
            else:
                wg_bf, wg_bf_r = ab("l3_wg", NC * 16)
                with b.task("pool", writes=[wg_bf_r], dma_target=wg_bf_r) as tk:
                    W = AP["L3_ffn_Wg"].rearrange("(c p) e -> c p e", p=128)
                    for c in range(NC):
                        tk.dma(wg_bf[:, c * 16:c * 16 + 8], W[c], wg_bf_r)

            for t in range(NT):
                ph_g, pr_g = PF[4 + (t % 2)]
                ne = nE if lay != 2 else 12
                with b.task("pe", reads=[xnT_r] + (
                        [gwt_r] if lay == 1 else [wr_bf_r, br_pad_r, ones_r] if lay == 2
                        else [wg_bf_r]), writes=[pr_g]) as tk:
                    if lay == 1:
                        pairs = [(xnT_sb[:, c * S + t * 128:c * S + (t + 1) * 128],
                                  gwt[:, c * 8:(c + 1) * 8]) for c in range(NC)]
                        mm_group(tk, ph_g[:, 0:8], pairs)
                    elif lay == 2:
                        pairs = [(xnT_sb[:, c * S + t * 128:c * S + (t + 1) * 128],
                                  wr_bf[:, c * 16:c * 16 + 12]) for c in range(NC)]
                        mm_group(tk, ph_g[:, 0:12], pairs,
                                 extra_first=(ones_sb[:, 0:128], br_pad[:, 0:12]))
                    else:
                        pairs = [(xnT_sb[:, c * S + t * 128:c * S + (t + 1) * 128],
                                  wg_bf[:, c * 16:c * 16 + 8]) for c in range(NC)]
                        mm_group(tk, ph_g[:, 0:8], pairs)
                wc = wcomb[:, 16 * t:16 * t + ne]
                cc = ccols
                if lay == 2:
                    with b.task("act", reads=[pr_g], writes=[ccols_r]) as tk:
                        tk.emit(lambda e, ph_g=ph_g: e.activation(
                            cc[:, 16:28], ph_g[:, 0:12], AF.Identity, scale=1.0 / 1.5))
                    with b.task("dve", reads=[ccols_r], writes=[ccols_r]) as tk:
                        tk.emit(lambda e: e.tensor_reduce(cc[:, 0:1], cc[:, 16:28], AX.X, ALU.max))
                    with b.task("dve", reads=[ccols_r], writes=[ccols_r]) as tk:
                        tk.emit(lambda e: e.tensor_scalar(cc[:, 32:44], cc[:, 16:28], cc[:, 0:1],
                                                          None, ALU.is_ge))
                    with b.task("dve", reads=[ccols_r], writes=[ccols_r]) as tk:
                        tk.emit(lambda e: e.scalar_tensor_tensor(
                            cc[:, 48:60], cc[:, 32:44], -1e30, cc[:, 16:28], ALU.mult, ALU.add))
                    with b.task("dve", reads=[ccols_r], writes=[ccols_r]) as tk:
                        tk.emit(lambda e: e.tensor_reduce(cc[:, 1:2], cc[:, 48:60], AX.X, ALU.max))
                    with b.task("dve", reads=[ccols_r], writes=[ccols_r]) as tk:
                        tk.emit(lambda e: e.tensor_scalar(cc[:, 48:60], cc[:, 48:60], cc[:, 1:2],
                                                          None, ALU.is_ge))
                    with b.task("dve", reads=[ccols_r], writes=[ccols_r]) as tk:
                        tk.emit(lambda e: e.tensor_tensor(cc[:, 2:3], cc[:, 1:2], cc[:, 0:1],
                                                          ALU.subtract))
                    with b.task("act", reads=[ccols_r], writes=[ccols_r]) as tk:
                        tk.emit(lambda e: e.activation(cc[:, 3:4], cc[:, 2:3], AF.Exp))
                    with b.task("dve", reads=[ccols_r], writes=[ccols_r]) as tk:
                        tk.emit(lambda e: e.tensor_scalar(cc[:, 4:5], cc[:, 3:4], 1.0, None,
                                                          ALU.add))
                    with b.task("dve", reads=[ccols_r], writes=[ccols_r]) as tk:
                        tk.emit(lambda e: e.reciprocal(cc[:, 5:6], cc[:, 4:5]))
                    with b.task("dve", reads=[ccols_r], writes=[ccols_r]) as tk:
                        tk.emit(lambda e: e.tensor_tensor(cc[:, 6:7], cc[:, 3:4], cc[:, 5:6],
                                                          ALU.mult))
                    with b.task("dve", reads=[ccols_r], writes=[ccols_r]) as tk:
                        tk.emit(lambda e: e.tensor_scalar(cc[:, 32:44], cc[:, 32:44], cc[:, 5:6],
                                                          None, ALU.mult))
                    with b.task("dve", reads=[ccols_r], writes=[wcomb_r]) as tk:
                        tk.emit(lambda e, wc=wc: e.scalar_tensor_tensor(
                            wc, cc[:, 48:60], cc[:, 6:7], cc[:, 32:44], ALU.mult, ALU.add))
                else:
                    with b.task("act", reads=[pr_g], writes=[ccols_r]) as tk:
                        tk.emit(lambda e, ne=ne, ph_g=ph_g: e.activation(
                            cc[:, 16:16 + ne], ph_g[:, 0:ne], AF.Sigmoid))
                    if debug == "w":
                        with b.task("dve", reads=[ccols_r], writes=[scrdump_r]) as tk:
                            tk.emit(lambda e, t=t: e.tensor_copy(
                                scrdump[:, 8 * t:8 * t + 8], cc[:, 16:24]))
                        with b.task("dve", reads=[pr_g], writes=[rawdump_r]) as tk:
                            tk.emit(lambda e, t=t, ph_g=ph_g: e.tensor_copy(
                                rawdump[:, 8 * t:8 * t + 8], ph_g[:, 0:8]))
                    if lay == 1:
                        with b.task("dve", reads=[ccols_r, ebias_f_r], writes=[ccols_r]) as tk:
                            tk.emit(lambda e: e.tensor_tensor(cc[:, 32:40], cc[:, 16:24],
                                                              ebias_f[:, 0:8], ALU.add))
                        sel_src = (32, 40)
                    else:
                        with b.task("dve", reads=[ccols_r], writes=[ccols_r]) as tk:
                            tk.emit(lambda e: e.tensor_copy(cc[:, 32:40], cc[:, 16:24]))
                        sel_src = (32, 40)
                    a0, a1 = sel_src
                    with b.task("dve", reads=[ccols_r], writes=[ccols_r]) as tk:
                        tk.emit(lambda e: e.tensor_reduce(cc[:, 0:1], cc[:, a0:a1], AX.X, ALU.max))
                    with b.task("dve", reads=[ccols_r], writes=[ccols_r]) as tk:
                        tk.emit(lambda e: e.tensor_scalar(cc[:, 48:56], cc[:, a0:a1], cc[:, 0:1],
                                                          None, ALU.is_ge))
                    with b.task("dve", reads=[ccols_r], writes=[ccols_r]) as tk:
                        tk.emit(lambda e: e.scalar_tensor_tensor(
                            cc[:, 56:64], cc[:, 48:56], -1e30, cc[:, a0:a1], ALU.mult, ALU.add))
                    with b.task("dve", reads=[ccols_r], writes=[ccols_r]) as tk:
                        tk.emit(lambda e: e.tensor_reduce(cc[:, 1:2], cc[:, 56:64], AX.X, ALU.max))
                    with b.task("dve", reads=[ccols_r], writes=[ccols_r]) as tk:
                        tk.emit(lambda e: e.tensor_scalar(cc[:, 56:64], cc[:, 56:64], cc[:, 1:2],
                                                          None, ALU.is_ge))
                    with b.task("dve", reads=[ccols_r], writes=[ccols_r]) as tk:
                        tk.emit(lambda e: e.tensor_tensor(cc[:, 48:56], cc[:, 48:56],
                                                          cc[:, 56:64], ALU.add))
                    if lay == 1:
                        with b.task("dve", reads=[ccols_r], writes=[ccols_r]) as tk:
                            tk.emit(lambda e: e.tensor_tensor(cc[:, 56:64], cc[:, 16:24],
                                                              cc[:, 16:24], ALU.mult))
                        with b.task("dve", reads=[ccols_r], writes=[wcomb_r]) as tk:
                            tk.emit(lambda e, wc=wc: e.tensor_tensor(
                                wc, cc[:, 48:56], cc[:, 56:64], ALU.mult))
                    else:
                        with b.task("dve", reads=[ccols_r], writes=[ccols_r]) as tk:
                            tk.emit(lambda e: e.tensor_tensor(cc[:, 2:3], cc[:, 0:1],
                                                              cc[:, 1:2], ALU.add))
                        with b.task("dve", reads=[ccols_r], writes=[ccols_r]) as tk:
                            tk.emit(lambda e: e.reciprocal(cc[:, 3:4], cc[:, 2:3]))
                        with b.task("dve", reads=[ccols_r], writes=[ccols_r]) as tk:
                            tk.emit(lambda e: e.tensor_tensor(cc[:, 56:64], cc[:, 48:56],
                                                              cc[:, 16:24], ALU.mult))
                        with b.task("dve", reads=[ccols_r], writes=[wcomb_r]) as tk:
                            tk.emit(lambda e, wc=wc: e.tensor_scalar(
                                wc, cc[:, 56:64], cc[:, 3:4], 1.0 / 3.0, ALU.mult, ALU.mult))
            if debug == "w":
                with b.task("sp", reads=[wcomb_r, scrdump_r, rawdump_r],
                            writes=[out_r], dma_target=out_r) as tk:
                    tk.dma(out[0:128, 0:256], wcomb[:, 0:256], out_r)
                    tk.dma(out[0:128, 256:384], scrdump[:, 0:128], out_r)
                    tk.dma(out[0:128, 384:512], rawdump[:, 0:128], out_r)
                return
            if lay == 1:
                AR.reset_to_mark()
                AR.mark()

            if lay == 1:
                ffn_l1_body(wcomb, wcomb_r)
            elif lay == 2:
                ffn_l2_body(wcomb, wcomb_r)
            else:
                ffn_l3_body(wcomb, wcomb_r)
            AR.reset_to_mark()

            AR.mark()
            gt2, gt2_r = ab(f"l{lay}_gsil", 2 * D)
            for t in range(NT):
                slot = t % 2
                with b.task("sp", reads=[gsp_r], writes=[gt2_r], dma_target=gt2_r) as tk:
                    tk.dma(gt2[:, slot * D:(slot + 1) * D], AP_gspill_tile(t), gt2_r)
                with b.task("act", reads=[gt2_r], writes=[gt2_r]) as tk:
                    tk.emit(lambda e, slot=slot: e.activation(
                        gt2[:, slot * D:(slot + 1) * D], gt2[:, slot * D:(slot + 1) * D],
                        AF.Silu))
                with b.task("sp", reads=[gt2_r], writes=[gsp_r], dma_target=gsp_r) as tk:
                    tk.dma(AP_gspill_tile(t), gt2[:, slot * D:(slot + 1) * D], gsp_r)
            AR.reset_to_mark()
            AF32.reset_to_mark()

        def stream_coltiles(tk, name_ap, slot, slot_r, d_rows, col0, ncols):
            for c in range(NC):
                tk.dma(slot[:, c * ncols:(c + 1) * ncols],
                       name_ap[c * 128:(c + 1) * 128, col0:col0 + ncols], slot_r)

        def ffn_l1_body(wcomb, wcomb_r):
            HID = 1408
            NHC = 11
            wup_t, wup_t_r = ab("l1_wupT", NC * HID)
            wdn_t, wdn_t_r = ab("l1_wdnT", NHC * D)
            raw, raw_r = ab("l1_raw", 2 * HID)
            act_s, act_s_r = ab("l1_act", NHC * 512)
            gwin, gwin_r = ab("l1_gwin", 4 * D)

            def transpose_torch2(ap_mat, src_r, rows, cols, dst, dst_r, dst_stride):
                nrc = rows // 128
                ncc = cols // 128
                for rc in range(nrc):
                    slot = rc % 2
                    with b.task("sp", reads=[src_r], writes=[raw_r],
                                dma_target=raw_r) as tk:
                        tk.dma(raw[:, slot * HID:slot * HID + cols],
                               ap_mat[rc * 128:(rc + 1) * 128, :], raw_r)
                    for g0 in range(0, ncc, 8):
                        g1 = min(g0 + 8, ncc)
                        with b.task("pe", reads=[raw_r, eye_r], writes=[pbf_r]) as tk:
                            for cc in range(g0, g1):
                                tk.emit(lambda e, cc=cc, slot=slot: e.transpose(
                                    pbf_h[:, (cc % 8) * 128:(cc % 8) * 128 + 128],
                                    raw[:, slot * HID + cc * 128:slot * HID + (cc + 1) * 128],
                                    eye_sb[:]))
                        with b.task("dve", reads=[pbf_r], writes=[dst_r]) as tk:
                            for cc in range(g0, g1):
                                tk.emit(lambda e, cc=cc, rc=rc: e.tensor_copy(
                                    dst[:, cc * dst_stride + rc * 128:
                                        cc * dst_stride + (rc + 1) * 128],
                                    pbf_h[:, (cc % 8) * 128:(cc % 8) * 128 + 128]))

            def one_unit(eidx, wup_ap, wup_res, wdn_ap, wdn_res, write_mode):
                transpose_torch2(wup_ap, wup_res, HID, D, wup_t, wup_t_r, HID)
                transpose_torch2(wdn_ap, wdn_res, D, HID, wdn_t, wdn_t_r, D)
                for ts4 in range(4):
                    for hc in range(NHC):
                        ph_u, pr_u = PF[hc % 2]
                        with b.task("pe", reads=[wup_t_r, xnT_r], writes=[pr_u]) as tk:
                            pairs = [(wup_t[:, c * HID + hc * 128: c * HID + (hc + 1) * 128],
                                      xnT_sb[:, c * S + ts4 * 512: c * S + ts4 * 512 + 512])
                                     for c in range(NC)]
                            mm_group(tk, ph_u[:], pairs)
                        with b.task("act", reads=[pr_u], writes=[act_s_r]) as tk:
                            tk.emit(lambda e, ph_u=ph_u, hc=hc: e.activation(
                                act_s[:, hc * 512:(hc + 1) * 512], ph_u[:], AF.Relu))
                        with b.task("dve", reads=[act_s_r], writes=[act_s_r]) as tk:
                            tk.emit(lambda e, hc=hc: e.tensor_tensor(
                                act_s[:, hc * 512:(hc + 1) * 512],
                                act_s[:, hc * 512:(hc + 1) * 512],
                                act_s[:, hc * 512:(hc + 1) * 512], ALU.mult))
                    if not write_mode:
                        with b.task("sp", reads=[gsp_r], writes=[gwin_r],
                                    dma_target=gwin_r) as tk:
                            for tt in range(4):
                                tk.dma(gwin[:, tt * D:(tt + 1) * D],
                                       AP_gspill_tile(ts4 * 4 + tt), gwin_r)
                    for oh in range(2):
                        for tt in range(4):
                            ph_y, pr_y = PF[2 + tt]
                            with b.task("pe", reads=[wdn_t_r, act_s_r], writes=[pr_y]) as tk:
                                pairs = []
                                for hc in range(NHC):
                                    pairs.append(
                                        (act_s[:, hc * 512 + tt * 128: hc * 512 + (tt + 1) * 128],
                                         wdn_t[:, hc * D + oh * 512: hc * D + oh * 512 + 512]))
                                mm_group(tk, ph_y[:], pairs)
                            wcol = (wcomb[:, 16 * (ts4 * 4 + tt) + eidx:
                                          16 * (ts4 * 4 + tt) + eidx + 1]
                                    if eidx is not None else None)
                            with b.task("dve",
                                        reads=[pr_y, wcomb_r, gwin_r],
                                        writes=[gwin_r]) as tk:
                                gsl = gwin[:, tt * D + oh * 512: tt * D + oh * 512 + 512]
                                if write_mode:
                                    tk.emit(lambda e, ph_y=ph_y, gsl=gsl: e.tensor_scalar(
                                        gsl, ph_y[:], 1.0, None, ALU.mult))
                                else:
                                    tk.emit(lambda e, ph_y=ph_y, gsl=gsl, wcol=wcol:
                                            e.scalar_tensor_tensor(
                                                gsl, ph_y[:], wcol, gsl, ALU.mult, ALU.add))
                    with b.task("sp", reads=[gwin_r], writes=[gsp_r],
                                dma_target=gsp_r) as tk:
                        for tt in range(4):
                            tk.dma(AP_gspill_tile(ts4 * 4 + tt),
                                   gwin[:, tt * D:(tt + 1) * D], gsp_r)

            one_unit(None, WBF["L1_ffn_shared_w_up"], WBF_R["L1_ffn_shared_w_up"],
                     WBF["L1_ffn_shared_w_down"], WBF_R["L1_ffn_shared_w_down"], True)
            for eidx in range(8):
                one_unit(eidx,
                         WBF["L1_ffn_w_up"][eidx * 1408:(eidx + 1) * 1408, :],
                         WBF_R["L1_ffn_w_up"],
                         WBF["L1_ffn_w_down"][eidx * D:(eidx + 1) * D, :],
                         WBF_R["L1_ffn_w_down"],
                         False)

        def ffn_l2_body(wcomb, wcomb_r):
            HS, HR = 2048, 3072
            bs2bc, bs2bc_r = ab("l2_bs2bc", D)
            biasrow, biasrow_r = ab("l2_biasrow", 2 * HR)
            gelt, gelt_r = ab("l2_gel", 512)
            gwin, gwin_r = ab("l2_gwin", 4 * D)
            with b.task("dve", writes=[biasrow_r]) as tk:
                tk.emit(lambda e: e.memset(biasrow[:], 0.0))
            with b.task("pool", writes=[biasrow_r], dma_target=biasrow_r) as tk:
                tk.dma(biasrow[0:1, 0:D],
                       AP["L2_ffn_bs2"].rearrange("(o d) -> o d", o=1), biasrow_r)
            ph_t, pr_t = PF[6]
            for half in range(2):
                with b.task("pe", reads=[biasrow_r, ones_r], writes=[pr_t]) as tk:
                    tk.emit(lambda e, half=half: e.matmul(
                        ph_t[:], ones_sb[0:1, 0:128],
                        biasrow[0:1, half * 512:(half + 1) * 512], start=True, stop=True))
                with b.task("dve", reads=[pr_t], writes=[bs2bc_r]) as tk:
                    tk.emit(lambda e, half=half: e.tensor_copy(
                        bs2bc[:, half * 512:(half + 1) * 512], ph_t[:]))

            AR.mark()
            wtp, wtp_r = ab("l2_wtp", NT * 128)
            be2p, be2p_r = ab("l2_be2p", D)
            with b.task("dve", writes=[wtp_r]) as tk:
                tk.emit(lambda e: e.memset(wtp[:], 0.0))
            with b.task("dve", writes=[be2p_r]) as tk:
                tk.emit(lambda e: e.memset(be2p[:], 0.0))
            with b.task("pool", writes=[be2p_r], dma_target=be2p_r) as tk:
                tk.dma(be2p[0:12, :], AP["L2_ffn_be2"], be2p_r)
            for t in range(NT):
                with b.task("pe", reads=[wcomb_r, eye32_r], writes=[pr_t]) as tk:
                    tk.emit(lambda e, t=t: e.transpose(
                        ph_t[0:12, 0:128], wcomb[:, 16 * t:16 * t + 12], eye_f32[:]))
                with b.task("dve", reads=[pr_t], writes=[wtp_r]) as tk:
                    tk.emit(lambda e, t=t: e.tensor_copy(
                        wtp[0:12, t * 128:(t + 1) * 128], ph_t[0:12, 0:128]))
            for t in range(NT):
                for oh in range(2):
                    ph_y, pr_y = PF[2 + oh]
                    with b.task("pe", reads=[wtp_r, be2p_r], writes=[pr_y]) as tk:
                        tk.emit(lambda e, t=t, oh=oh, ph_y=ph_y: e.matmul(
                            ph_y[:], wtp[:, t * 128:(t + 1) * 128],
                            be2p[:, oh * 512:(oh + 1) * 512], start=True, stop=True))
                    with b.task("dve", reads=[pr_y, bs2bc_r], writes=[gwin_r]) as tk:
                        tk.emit(lambda e, ph_y=ph_y, oh=oh: e.tensor_tensor(
                            gwin[:, oh * 512:(oh + 1) * 512],
                            bs2bc[:, oh * 512:(oh + 1) * 512], ph_y[:], ALU.add))
                with b.task("sp", reads=[gwin_r], writes=[gsp_r], dma_target=gsp_r) as tk:
                    tk.dma(AP_gspill_tile(t), gwin[:, 0:D], gsp_r)
            AR.reset_to_mark()

            we1s, we1s_r = ab("l2_we1s", 2 * NC * 512)
            act_s, act_s_r = ab("l2_act", 24 * D)
            we2s, we2s_r = ab("l2_we2s", 2 * 4 * 512)

            def geglu_unit(eidx, W1ap, W1res, W2ap, W2res, hid2, has_bias):
                nhc = hid2 // 128
                nh5 = hid2 // 512
                for th in range(2):
                    for part in range(2):
                        for h5 in range(nh5):
                            buf = (h5 % 2) * NC * 512
                            with b.task("sp", reads=[W1res], writes=[we1s_r],
                                        dma_target=we1s_r) as tk:
                                stream_coltiles(tk, W1ap, we1s[:, buf:buf + NC * 512],
                                                we1s_r, D, part * hid2 + h5 * 512, 512)
                            for hc4 in range(4):
                                hc = h5 * 4 + hc4
                                for t5 in range(2):
                                    ph_a, pr_a = PF[(hc4 * 2 + t5) % 2]
                                    ef = None
                                    if has_bias:
                                        ef = (biasrow[:, part * hid2 + hc * 128:
                                                      part * hid2 + (hc + 1) * 128],
                                              ones_sb[:])
                                    with b.task("pe",
                                                reads=[we1s_r, xnT_r, biasrow_r, ones_r],
                                                writes=[pr_a]) as tk:
                                        pairs = [(we1s[:, buf + c * 512 + hc4 * 128:
                                                       buf + c * 512 + (hc4 + 1) * 128],
                                                  xnT_sb[:, c * S + th * 1024 + t5 * 512:
                                                         c * S + th * 1024 + t5 * 512 + 512])
                                                 for c in range(NC)]
                                        mm_group(tk, ph_a[:], pairs, extra_first=ef)
                                    if part == 0:
                                        with b.task("dve", reads=[pr_a],
                                                    writes=[act_s_r]) as tk:
                                            tk.emit(lambda e, ph_a=ph_a, hc=hc, t5=t5:
                                                    e.tensor_copy(
                                                        act_s[:, hc * D + t5 * 512:
                                                              hc * D + t5 * 512 + 512],
                                                        ph_a[:]))
                                    else:
                                        with b.task("act", reads=[pr_a],
                                                    writes=[gelt_r]) as tk:
                                            tk.emit(lambda e, ph_a=ph_a: e.activation(
                                                gelt[:], ph_a[:], AF.Gelu))
                                        with b.task("dve", reads=[gelt_r, act_s_r],
                                                    writes=[act_s_r]) as tk:
                                            tk.emit(lambda e, hc=hc, t5=t5: e.tensor_tensor(
                                                act_s[:, hc * D + t5 * 512:
                                                      hc * D + t5 * 512 + 512],
                                                act_s[:, hc * D + t5 * 512:
                                                      hc * D + t5 * 512 + 512],
                                                gelt[:], ALU.mult))
                    for tq in range(2):
                        with b.task("sp", reads=[gsp_r], writes=[gwin_r],
                                    dma_target=gwin_r) as tk:
                            for tt in range(4):
                                tk.dma(gwin[:, tt * D:(tt + 1) * D],
                                       AP_gspill_tile(th * 8 + tq * 4 + tt), gwin_r)
                        for oh in range(2):
                            ngrp = nhc // 4
                            for g4 in range(ngrp):
                                h0 = g4 * 4
                                buf2 = (g4 % 2) * 4 * 512
                                with b.task("sp", reads=[W2res], writes=[we2s_r],
                                            dma_target=we2s_r) as tk:
                                    tk.dma(we2s[:, buf2:buf2 + 4 * 512].rearrange(
                                               "p (h o) -> p h o", o=512),
                                           W2ap[h0 * 128:(h0 + 4) * 128,
                                                oh * 512:(oh + 1) * 512].rearrange(
                                               "(h p) o -> p h o", p=128),
                                           we2s_r)
                                for tt in range(4):
                                    ph_y, pr_y = PF[2 + tt]
                                    with b.task("pe", reads=[we2s_r, act_s_r],
                                                writes=[pr_y]) as tk:
                                        pairs = [(act_s[:, hcc * D + tq * 512 + tt * 128:
                                                        hcc * D + tq * 512 + (tt + 1) * 128],
                                                  we2s[:, buf2 + (hcc - h0) * 512:
                                                       buf2 + (hcc - h0 + 1) * 512])
                                                 for hcc in range(h0, h0 + 4)]
                                        mm_group(tk, ph_y[:], pairs,
                                                 first_start=(g4 == 0),
                                                 stop=(g4 == ngrp - 1))
                            for tt in range(4):
                                ph_y, pr_y = PF[2 + tt]
                                ti = th * 8 + tq * 4 + tt
                                gsl = gwin[:, tt * D + oh * 512: tt * D + oh * 512 + 512]
                                if eidx is None:
                                    with b.task("dve", reads=[pr_y, gwin_r],
                                                writes=[gwin_r]) as tk:
                                        tk.emit(lambda e, ph_y=ph_y, gsl=gsl:
                                                e.scalar_tensor_tensor(
                                                    gsl, ph_y[:], 1.0, gsl,
                                                    ALU.mult, ALU.add))
                                else:
                                    wcol = wcomb[:, 16 * ti + eidx:16 * ti + eidx + 1]
                                    with b.task("dve", reads=[pr_y, wcomb_r, gwin_r],
                                                writes=[gwin_r]) as tk:
                                        tk.emit(lambda e, ph_y=ph_y, gsl=gsl, wcol=wcol:
                                                e.scalar_tensor_tensor(
                                                    gsl, ph_y[:], wcol, gsl,
                                                    ALU.mult, ALU.add))
                        with b.task("sp", reads=[gwin_r], writes=[gsp_r],
                                    dma_target=gsp_r) as tk:
                            for tt in range(4):
                                tk.dma(AP_gspill_tile(th * 8 + tq * 4 + tt),
                                       gwin[:, tt * D:(tt + 1) * D], gsp_r)

            with b.task("dve", writes=[biasrow_r]) as tk:
                tk.emit(lambda e: e.memset(biasrow[:], 0.0))
            with b.task("pool", writes=[biasrow_r], dma_target=biasrow_r) as tk:
                tk.dma(biasrow[0:1, 0:2 * HS],
                       AP["L2_ffn_bs1"].rearrange("(o d) -> o d", o=1), biasrow_r)
            geglu_unit(None, WBF["L2_ffn_Ws1"], WBF_R["L2_ffn_Ws1"],
                       WBF["L2_ffn_Ws2"], WBF_R["L2_ffn_Ws2"], HS, True)
            We1 = AP["L2_ffn_We1"]
            We2 = AP["L2_ffn_We2"]
            be1 = AP["L2_ffn_be1"]
            for eidx in range(12):
                with b.task("pool", writes=[biasrow_r], dma_target=biasrow_r) as tk:
                    tk.dma(biasrow[0:1, 0:2 * HR], be1[eidx:eidx + 1, :], biasrow_r)
                geglu_unit(eidx, WBF["L2_ffn_We1"][eidx * D:(eidx + 1) * D, :],
                           WBF_R["L2_ffn_We1"],
                           WBF["L2_ffn_We2"][eidx * 3072:(eidx + 1) * 3072, :],
                           WBF_R["L2_ffn_We2"], HR, True)

        def ffn_l3_body(wcomb, wcomb_r):
            FF = 2048
            nhc = FF // 128
            w13s, w13s_r = ab("l3_w13s", 2 * 2 * NC * 512)
            act_s, act_s_r = ab("l3_act", 16 * D)
            we2s, we2s_r = ab("l3_we2s", 2 * 8 * 512)
            gwin, gwin_r = ab("l3_gwin", 4 * D)
            silt, silt_r = ab("l3_sil", 512)

            def swiglu_unit(eidx, W1ap, W3ap, W2ap, w_res, write_mode):
                for th in range(2):
                    for h5 in range(FF // 512):
                        buf = (h5 % 2) * 2 * NC * 512
                        with b.task("sp", reads=w_res, writes=[w13s_r],
                                    dma_target=w13s_r) as tk:
                            stream_coltiles(tk, W1ap, w13s[:, buf:buf + NC * 512],
                                            w13s_r, D, h5 * 512, 512)
                            stream_coltiles(tk, W3ap,
                                            w13s[:, buf + NC * 512:buf + 2 * NC * 512],
                                            w13s_r, D, h5 * 512, 512)
                        for hc4 in range(4):
                            hc = h5 * 4 + hc4
                            for t5 in range(2):
                                ph_a, pr_a = PF[0]
                                ph_g, pr_g = PF[1]
                                with b.task("pe", reads=[w13s_r, xnT_r],
                                            writes=[pr_a]) as tk:
                                    pairs = [(w13s[:, buf + c * 512 + hc4 * 128:
                                                   buf + c * 512 + (hc4 + 1) * 128],
                                              xnT_sb[:, c * S + th * 1024 + t5 * 512:
                                                     c * S + th * 1024 + t5 * 512 + 512])
                                             for c in range(NC)]
                                    mm_group(tk, ph_a[:], pairs)
                                with b.task("pe", reads=[w13s_r, xnT_r],
                                            writes=[pr_g]) as tk:
                                    pairs = [(w13s[:, buf + NC * 512 + c * 512 + hc4 * 128:
                                                   buf + NC * 512 + c * 512 + (hc4 + 1) * 128],
                                              xnT_sb[:, c * S + th * 1024 + t5 * 512:
                                                     c * S + th * 1024 + t5 * 512 + 512])
                                             for c in range(NC)]
                                    mm_group(tk, ph_g[:], pairs)
                                with b.task("act", reads=[pr_a], writes=[silt_r]) as tk:
                                    tk.emit(lambda e, ph_a=ph_a: e.activation(
                                        silt[:], ph_a[:], AF.Silu))
                                with b.task("dve", reads=[silt_r, pr_g],
                                            writes=[act_s_r]) as tk:
                                    tk.emit(lambda e, ph_g=ph_g, hc=hc, t5=t5: e.tensor_tensor(
                                        act_s[:, hc * D + t5 * 512: hc * D + t5 * 512 + 512],
                                        ph_g[:], silt[:], ALU.mult))
                    for tq in range(2):
                        if not write_mode:
                            with b.task("sp", reads=[gsp_r], writes=[gwin_r],
                                        dma_target=gwin_r) as tk:
                                for tt in range(4):
                                    tk.dma(gwin[:, tt * D:(tt + 1) * D],
                                           AP_gspill_tile(th * 8 + tq * 4 + tt), gwin_r)
                        for oh in range(2):
                            for g8 in range(2):
                                h0 = g8 * 8
                                buf2 = (g8 % 2) * 8 * 512
                                with b.task("sp", reads=w_res, writes=[we2s_r],
                                            dma_target=we2s_r) as tk:
                                    tk.dma(we2s[:, buf2:buf2 + 8 * 512].rearrange(
                                               "p (h o) -> p h o", o=512),
                                           W2ap[h0 * 128:(h0 + 8) * 128,
                                                oh * 512:(oh + 1) * 512].rearrange(
                                               "(h p) o -> p h o", p=128),
                                           we2s_r)
                                for tt in range(4):
                                    ph_y, pr_y = PF[2 + tt]
                                    with b.task("pe", reads=[we2s_r, act_s_r],
                                                writes=[pr_y]) as tk:
                                        pairs = [(act_s[:, hcc * D + tq * 512 + tt * 128:
                                                        hcc * D + tq * 512 + (tt + 1) * 128],
                                                  we2s[:, buf2 + (hcc - h0) * 512:
                                                       buf2 + (hcc - h0 + 1) * 512])
                                                 for hcc in range(h0, h0 + 8)]
                                        mm_group(tk, ph_y[:], pairs,
                                                 first_start=(g8 == 0),
                                                 stop=(g8 == 1))
                            for tt in range(4):
                                ph_y, pr_y = PF[2 + tt]
                                ti = th * 8 + tq * 4 + tt
                                gsl = gwin[:, tt * D + oh * 512: tt * D + oh * 512 + 512]
                                wcol = (wcomb[:, 16 * ti + eidx:16 * ti + eidx + 1]
                                        if eidx is not None else None)
                                with b.task("dve", reads=[pr_y, wcomb_r, gwin_r],
                                            writes=[gwin_r]) as tk:
                                    if write_mode:
                                        tk.emit(lambda e, ph_y=ph_y, gsl=gsl: e.tensor_scalar(
                                            gsl, ph_y[:], 1.0 / 3.0, None, ALU.mult))
                                    else:
                                        tk.emit(lambda e, ph_y=ph_y, gsl=gsl, wcol=wcol:
                                                e.scalar_tensor_tensor(
                                                    gsl, ph_y[:], wcol, gsl,
                                                    ALU.mult, ALU.add))
                        with b.task("sp", reads=[gwin_r], writes=[gsp_r],
                                    dma_target=gsp_r) as tk:
                            for tt in range(4):
                                tk.dma(AP_gspill_tile(th * 8 + tq * 4 + tt),
                                       gwin[:, tt * D:(tt + 1) * D], gsp_r)

            swiglu_unit(None, WBF["L3_ffn_Ws1"], WBF["L3_ffn_Ws3"],
                        WBF["L3_ffn_Ws2"],
                        [WBF_R["L3_ffn_Ws1"], WBF_R["L3_ffn_Ws3"],
                         WBF_R["L3_ffn_Ws2"]], True)
            for eidx in range(8):
                swiglu_unit(eidx,
                            WBF["L3_ffn_We1"][eidx * D:(eidx + 1) * D, :],
                            WBF["L3_ffn_We3"][eidx * D:(eidx + 1) * D, :],
                            WBF["L3_ffn_We2"][eidx * 2048:(eidx + 1) * 2048, :],
                            [WBF_R["L3_ffn_We1"], WBF_R["L3_ffn_We3"],
                             WBF_R["L3_ffn_We2"]], False)

        def attn_generic(lay):
            if lay == 1:
                Hs, hd, hpc, Wq = 32, 32, 3, 256
                wqkv_name, wproj_name = "L1_attn_Wqkv", "L1_attn_Wproj"
                causal = False
            else:
                Hs, hd, hpc, Wq = 16, 64, 2, 512
                wqkv_name = f"L{lay}_attn_w_qkv"
                wproj_name = "L2_attn_w_out" if lay == 2 else "L3_attn_w_proj"
                causal = (lay == 3)
            nkc = (Hs + hpc - 1) // hpc
            nsub = Wq // 128
            nqs = S // Wq
            scale = 1.0 / math.sqrt(hd)
            Wqkv = AP[wqkv_name]
            NV = 16

            def chunk_of(j):
                return j // hpc, (j % hpc) * hd

            vt, vt_r = ab(f"l{lay}_vt", NT * (NV * 65))
            AR.mark()
            wv_s, wv_s_r = ab(f"l{lay}_wvs", NC * D)
            WqkvB = WBF[wqkv_name]
            WqkvB_r = WBF_R[wqkv_name]
            with b.task("sp", reads=[WqkvB_r], writes=[wv_s_r],
                        dma_target=wv_s_r) as tk:
                stream_coltiles(tk, WqkvB, wv_s, wv_s_r, D, 2 * D, D)
            with b.task("dve", writes=[vt_r]) as tk:
                dst = vt.rearrange("p (t n) -> p t n", t=NT)[:, :, 0:NV * 65]
                dst = dst.rearrange("p t (h w) -> p t h w", w=65)[:, :, :, 64:65]
                tk.emit(lambda e, dst=dst: e.memset(dst, 1.0))
            for t in range(NT):
                for half in range(2):
                    ph, pr = PF[2 + half]
                    with b.task("pe", reads=[wv_s_r, xnT_r], writes=[pr]) as tk:
                        pairs = [(xnT_sb[:, c * S + t * 128: c * S + (t + 1) * 128],
                                  wv_s[:, c * D + half * 512: c * D + half * 512 + 512])
                                 for c in range(NC)]
                        mm_group(tk, ph[:], pairs)
                    with b.task("dve", reads=[pr], writes=[vt_r]) as tk:
                        dst = vt[:, t * NV * 65 + half * 8 * 65:
                                 t * NV * 65 + half * 8 * 65 + 8 * 65]
                        dst = dst.rearrange("p (h w) -> p h w", w=65)[:, :, 0:64]
                        src = ph.ap().rearrange("p (h w) -> p h w", w=64)
                        tk.emit(lambda e, dst=dst, src=src: e.tensor_copy(dst, src))
            AR.reset_to_mark()

            kt, kt_r = ab(f"l{lay}_kt", nkc * S)
            wk_s, wk_s_r = ab(f"l{lay}_wks", 2 * NC * 128)
            blko, blko_r = ab(f"l{lay}_blko", 16)
            cbf, cbf_r = ab(f"l{lay}_cbf", 48)
            cbf_row = cbf
            with b.task("dve", writes=[blko_r]) as tk:
                tk.emit(lambda e: e.memset(blko[:], 0.0))
            with b.task("dve", writes=[blko_r]) as tk:
                for jj in range(hpc):
                    tk.emit(lambda e, jj=jj: e.memset(
                        blko[jj * hd:(jj + 1) * hd, jj:jj + 1], 1.0))
            for m in range(nkc):
                width = min(hpc, Hs - m * hpc) * hd
                buf = (m % 2) * NC * 128
                with b.task("sp", reads=[WqkvB_r], writes=[wk_s_r],
                            dma_target=wk_s_r) as tk:
                    for c in range(NC):
                        tk.dma(wk_s[:, buf + c * 128: buf + c * 128 + width],
                               WqkvB[c * 128:(c + 1) * 128,
                                     D + m * hpc * hd: D + m * hpc * hd + width], wk_s_r)
                for ts4 in range(4):
                    ph, pr = PF[ts4 % 2]
                    with b.task("pe", reads=[wk_s_r, xnT_r], writes=[pr]) as tk:
                        pairs = [(wk_s[:, buf + c * 128: buf + c * 128 + width],
                                  xnT_sb[:, c * S + ts4 * 512: c * S + ts4 * 512 + 512])
                                 for c in range(NC)]
                        mm_group(tk, ph[0:width, :], pairs)
                    with b.task("act", reads=[pr], writes=[kt_r]) as tk:
                        tk.emit(lambda e, ph=ph, m=m, ts4=ts4, width=width: e.activation(
                            kt[0:width, m * S + ts4 * 512: m * S + ts4 * 512 + 512],
                            ph[0:width, :], AF.Identity))

            AR.mark()
            ksq, ksq_r = ab(f"l{lay}_ksq", 512)
            krow, krow_r = af(f"l{lay}_krow", 48)
            ncol, ncol_r = af(f"l{lay}_ncol", 16)
            ph_n, pr_n = PF[4]
            ph_t2, pr_t2 = PF[6]
            for m in range(nkc):
                nh = min(hpc, Hs - m * hpc)
                width = nh * hd
                for q4 in range(4):
                    with b.task("act", reads=[kt_r], writes=[ksq_r]) as tk:
                        tk.emit(lambda e, m=m, q4=q4, width=width: e.activation(
                            ksq[0:width, :], kt[0:width, m * S + q4 * 512: m * S + q4 * 512 + 512],
                            AF.Square))
                    with b.task("pe", reads=[ksq_r, blko_r], writes=[pr_n]) as tk:
                        tk.emit(lambda e, width=width, nh=nh: e.matmul(
                            ph_n[0:nh, 0:512], blko[0:width, 0:nh], ksq[0:width, :],
                            start=True, stop=True))
                    with b.task("dve", reads=[pr_n], writes=[ncol_r]) as tk:
                        tk.emit(lambda e, nh=nh, q4=q4: e.tensor_reduce(
                            ncol[0:nh, q4:q4 + 1], ph_n[0:nh, 0:512], AX.X, ALU.max))
                with b.task("dve", reads=[ncol_r], writes=[ncol_r]) as tk:
                    tk.emit(lambda e, nh=nh: e.tensor_reduce(
                        ncol[0:nh, 4:5], ncol[0:nh, 0:4], AX.X, ALU.max))
                with b.task("act", reads=[ncol_r], writes=[ncol_r]) as tk:
                    tk.emit(lambda e, nh=nh: e.activation(ncol[0:nh, 5:6], ncol[0:nh, 4:5],
                                                          AF.Sqrt))
                with b.task("pe", reads=[ncol_r, eye32_r], writes=[pr_t2]) as tk:
                    tk.emit(lambda e: e.transpose(
                        ph_t2[0:1, 0:128], ncol[:, 5:6], eye_f32[:]))
                with b.task("dve", reads=[pr_t2], writes=[krow_r]) as tk:
                    tk.emit(lambda e, m=m, nh=nh: e.tensor_copy(
                        krow[0:1, m * hpc:m * hpc + nh], ph_t2[0:1, 0:nh]))
            AR.reset_to_mark()

            if lay == 1:
                neglam, neglam_r = af("l1_neglam", 16)
                AF32.mark()
                lamv, lamv_r = af("l1_lam", 160)
                rmsbc, rmsbc_r = ab("l1_rmsbc", 64)
                AR.mark()
                rrow, rrow_r = ab("l1_rrow", 64)
                with b.task("pool", writes=[lamv_r], dma_target=lamv_r) as tk:
                    for i, nm in enumerate(["lambda_q1", "lambda_k1",
                                            "lambda_q2", "lambda_k2"]):
                        tk.dma(lamv[0:1, i * 32:(i + 1) * 32],
                               AP[f"L1_attn_{nm}"].rearrange("(o d) -> o d", o=1), lamv_r)
                with b.task("dve", reads=[lamv_r], writes=[lamv_r]) as tk:
                    tk.emit(lambda e: e.tensor_tensor(lamv[0:1, 128:160],
                                                      lamv[0:1, 0:32], lamv[0:1, 32:64],
                                                      ALU.mult))
                with b.task("dve", reads=[lamv_r], writes=[lamv_r]) as tk:
                    tk.emit(lambda e: e.tensor_reduce(lamv[0:1, 0:1], lamv[0:1, 128:160],
                                                      AX.X, ALU.add))
                with b.task("dve", reads=[lamv_r], writes=[lamv_r]) as tk:
                    tk.emit(lambda e: e.tensor_tensor(lamv[0:1, 128:160],
                                                      lamv[0:1, 64:96], lamv[0:1, 96:128],
                                                      ALU.mult))
                with b.task("dve", reads=[lamv_r], writes=[lamv_r]) as tk:
                    tk.emit(lambda e: e.tensor_reduce(lamv[0:1, 1:2], lamv[0:1, 128:160],
                                                      AX.X, ALU.add))
                with b.task("act", reads=[lamv_r], writes=[lamv_r]) as tk:
                    tk.emit(lambda e: e.activation(lamv[0:1, 2:3], lamv[0:1, 0:1], AF.Exp))
                with b.task("act", reads=[lamv_r], writes=[lamv_r]) as tk:
                    tk.emit(lambda e: e.activation(lamv[0:1, 3:4], lamv[0:1, 1:2], AF.Exp))
                with b.task("dve", reads=[lamv_r], writes=[lamv_r]) as tk:
                    tk.emit(lambda e: e.tensor_tensor(lamv[0:1, 4:5], lamv[0:1, 2:3],
                                                      lamv[0:1, 3:4], ALU.subtract))
                with b.task("dve", reads=[lamv_r], writes=[cbf_r]) as tk:
                    tk.emit(lambda e: e.tensor_scalar(cbf[0:1, 2:3], lamv[0:1, 4:5],
                                                      LAMBDA_INIT, -1.0, ALU.add, ALU.mult))
                with b.task("pe", reads=[cbf_r, ones_r], writes=[pr_t2]) as tk:
                    tk.emit(lambda e: e.matmul(ph_t2[:, 0:1], ones_sb[0:1, 0:128],
                                               cbf[0:1, 2:3], start=True, stop=True))
                with b.task("dve", reads=[pr_t2], writes=[neglam_r]) as tk:
                    tk.emit(lambda e: e.tensor_copy(neglam[:, 0:1], ph_t2[:, 0:1]))
                with b.task("pool", writes=[rrow_r], dma_target=rrow_r) as tk:
                    tk.dma(rrow[0:1, 0:64],
                           AP["L1_attn_rms_scale"].rearrange("(o d) -> o d", o=1), rrow_r)
                with b.task("pe", reads=[rrow_r, ones_r], writes=[pr_t2]) as tk:
                    tk.emit(lambda e: e.matmul(ph_t2[:, 0:64], ones_sb[0:1, 0:128],
                                               rrow[0:1, 0:64], start=True, stop=True))
                with b.task("act", reads=[pr_t2], writes=[rmsbc_r]) as tk:
                    tk.emit(lambda e: e.activation(rmsbc[:, 0:64], ph_t2[:, 0:64],
                                                   AF.Identity, scale=1.0 - LAMBDA_INIT))
                AF32.reset_to_mark()
                AR.reset_to_mark()

            qt, qt_r = ab(f"l{lay}_qt", nkc * Wq)
            qsq, qsq_r = ab(f"l{lay}_qsq", Wq)
            ptb, ptb_r = ab(f"l{lay}_ptb", 4 * Wq)
            ao_sb, ao_sb_r = ab(f"l{lay}_aosb", nsub * D)
            aoT, aoT_r = ab(f"l{lay}_aoT", D)
            wp_s, wp_s_r = ab(f"l{lay}_wps", 2 * D)
            qrow, qrow_r = af(f"l{lay}_qrow", 48)
            crow, crow_r = af(f"l{lay}_crow", 48)
            negc, negc_r = af(f"l{lay}_negc", 16)
            ncol2, ncol_r2 = af(f"l{lay}_ncol2", 16)
            ecol, ecol_r = af(f"l{lay}_ecol", 32)
            etmp, etmp_r = af(f"l{lay}_etmp", 256)
            tmpf, tmpf_r = af(f"l{lay}_tmpf", 512)
            Wproj = AP[wproj_name]
            for qs in range(nqs):
                for m in range(nkc):
                    nh = min(hpc, Hs - m * hpc)
                    width = nh * hd
                    buf = (m % 2) * NC * 128
                    with b.task("sp", reads=[WqkvB_r], writes=[wk_s_r],
                                dma_target=wk_s_r) as tk:
                        for c in range(NC):
                            tk.dma(wk_s[:, buf + c * 128: buf + c * 128 + width],
                                   WqkvB[c * 128:(c + 1) * 128,
                                         m * hpc * hd: m * hpc * hd + width], wk_s_r)
                    ph, pr = PF[5]
                    with b.task("pe", reads=[wk_s_r, xnT_r], writes=[pr]) as tk:
                        pairs = [(wk_s[:, buf + c * 128: buf + c * 128 + width],
                                  xnT_sb[:, c * S + qs * Wq: c * S + qs * Wq + Wq])
                                 for c in range(NC)]
                        mm_group(tk, ph[0:width, 0:Wq], pairs)
                    with b.task("act", reads=[pr], writes=[qt_r]) as tk:
                        tk.emit(lambda e, ph=ph, m=m, width=width: e.activation(
                            qt[0:width, m * Wq:(m + 1) * Wq], ph[0:width, 0:Wq],
                            AF.Identity, scale=scale))
                    with b.task("act", reads=[qt_r], writes=[qsq_r]) as tk:
                        tk.emit(lambda e, m=m, width=width: e.activation(
                            qsq[0:width, 0:Wq], qt[0:width, m * Wq:(m + 1) * Wq],
                            AF.Square))
                    with b.task("pe", reads=[qsq_r, blko_r], writes=[pr_n]) as tk:
                        tk.emit(lambda e, width=width, nh=nh: e.matmul(
                            ph_n[0:nh, 0:Wq], blko[0:width, 0:nh], qsq[0:width, 0:Wq],
                            start=True, stop=True))
                    with b.task("dve", reads=[pr_n], writes=[ncol_r2]) as tk:
                        tk.emit(lambda e, nh=nh: e.tensor_reduce(
                            ncol2[0:nh, 0:1], ph_n[0:nh, 0:Wq], AX.X, ALU.max))
                    with b.task("act", reads=[ncol_r2], writes=[ncol_r2]) as tk:
                        tk.emit(lambda e, nh=nh: e.activation(ncol2[0:nh, 1:2],
                                                              ncol2[0:nh, 0:1], AF.Sqrt))
                    with b.task("pe", reads=[ncol_r2, eye32_r], writes=[pr_t2]) as tk:
                        tk.emit(lambda e: e.transpose(
                            ph_t2[0:1, 0:128], ncol2[:, 1:2], eye_f32[:]))
                    with b.task("dve", reads=[pr_t2], writes=[qrow_r]) as tk:
                        tk.emit(lambda e, m=m, nh=nh: e.tensor_copy(
                            qrow[0:1, m * hpc:m * hpc + nh], ph_t2[0:1, 0:nh]))
                with b.task("dve", reads=[qrow_r, krow_r], writes=[crow_r]) as tk:
                    tk.emit(lambda e: e.tensor_tensor(crow[0:1, 0:Hs], qrow[0:1, 0:Hs],
                                                      krow[0:1, 0:Hs], ALU.mult))
                with b.task("dve", reads=[crow_r], writes=[cbf_r]) as tk:
                    tk.emit(lambda e: e.tensor_copy(cbf_row[0:1, 0:Hs], crow[0:1, 0:Hs]))

                kc_max = (qs * Wq + Wq) // 128 if causal else NT
                ngrp_h = NV if lay == 1 else NV // 2
                for vh in range(ngrp_h):
                    members = [vh, vh + 16] if lay == 1 else [2 * vh, 2 * vh + 1]
                    for mi, j in enumerate(members):
                        with b.task("pe", reads=[cbf_r, ones_r], writes=[pr_t2]) as tk:
                            tk.emit(lambda e, j=j: e.matmul(
                                ph_t2[:, 0:1], ones_sb[0:1, 0:128],
                                cbf_row[0:1, j:j + 1], start=True, stop=True))
                        with b.task("act", reads=[pr_t2], writes=[negc_r]) as tk:
                            tk.emit(lambda e, mi=mi: e.activation(
                                negc[:, mi:mi + 1], ph_t2[:, 0:1],
                                AF.Identity, scale=-1.0))
                    ao_banks = [PF[2], PF[3]]

                    def sbank(kc, mi):
                        if mi == 1:
                            return PF[(4, 5)[kc % 2]]
                        return PF[kc % 2]

                    def emit_score(kc):
                        for mi, j in enumerate(members):
                            m, base = chunk_of(j)
                            ph_s, pr_s = sbank(kc, mi)
                            with b.task("pe", reads=[kt_r, qt_r], writes=[pr_s]) as tk:
                                tk.emit(lambda e, m=m, base=base, kc=kc, ph_s=ph_s:
                                        e.matmul(
                                            ph_s[0:128, 0:Wq],
                                            kt[base:base + hd,
                                               m * S + kc * 128: m * S + (kc + 1) * 128],
                                            qt[base:base + hd, m * Wq:(m + 1) * Wq],
                                            start=True, stop=True))

                    emit_score(0)
                    for kc in range(kc_max):
                        if kc + 1 < kc_max:
                            emit_score(kc + 1)
                        for mi, j in enumerate(members):
                            ph_s, pr_s = sbank(kc, mi)
                            pslot = (kc % 2) * 2 * Wq + mi * Wq
                            if lay == 3:
                                for sub in range(nsub):
                                    gsub = qs * nsub + sub
                                    if kc > gsub:
                                        continue
                                    dlt = gsub - kc
                                    with b.task("dve", reads=[alibi_r, negc_r],
                                                writes=[ecol_r]) as tk:
                                        tk.emit(lambda e, j=j, dlt=dlt, sub=sub, mi=mi:
                                                e.tensor_scalar(
                                            ecol[:, 8 + 4 * mi + sub:9 + 4 * mi + sub],
                                            alibi_sb[:, 16 * j + dlt:16 * j + dlt + 1],
                                            negc[:, mi:mi + 1], None, ALU.add))
                                    with b.task("act", reads=[pr_s, ecol_r],
                                                writes=[ptb_r]) as tk:
                                        tk.emit(lambda e, ph_s=ph_s, pslot=pslot, sub=sub,
                                                mi=mi: e.activation(
                                            ptb[:, pslot + sub * 128:pslot + (sub + 1) * 128],
                                            ph_s[0:128, sub * 128:(sub + 1) * 128],
                                            AF.Exp, bias=ecol[:, 8 + 4 * mi + sub:
                                                            9 + 4 * mi + sub]))
                            else:
                                with b.task("act", reads=[pr_s, negc_r],
                                            writes=[ptb_r]) as tk:
                                    tk.emit(lambda e, ph_s=ph_s, pslot=pslot, mi=mi:
                                            e.activation(
                                                ptb[:, pslot:pslot + Wq], ph_s[0:128, 0:Wq],
                                                AF.Exp, bias=negc[:, mi:mi + 1]))
                            if causal and kc * 128 >= qs * Wq:
                                qoff = kc * 128 - qs * Wq
                                with b.task("dve", reads=[tri_r, ptb_r],
                                            writes=[ptb_r]) as tk:
                                    tk.emit(lambda e, pslot=pslot, qoff=qoff:
                                            e.tensor_tensor(
                                                ptb[:, pslot + qoff:pslot + qoff + 128],
                                                ptb[:, pslot + qoff:pslot + qoff + 128],
                                                tri_sb[:], ALU.mult))
                            ph_a, pr_a = ao_banks[mi]
                            with b.task("pe", reads=[ptb_r, vt_r], writes=[pr_a]) as tk:
                                for sub in range(nsub):
                                    if causal and kc > qs * nsub + sub:
                                        continue
                                    vm = vh if lay == 1 else j
                                    tk.emit(lambda e, ph_a=ph_a, pslot=pslot, sub=sub,
                                            kc=kc, vm=vm, kc_max=kc_max:
                                            e.matmul(
                                                ph_a[:, sub * 65:(sub + 1) * 65],
                                                ptb[:, pslot + sub * 128:
                                                    pslot + (sub + 1) * 128],
                                                vt[:, kc * NV * 65 + vm * 65:
                                                   kc * NV * 65 + (vm + 1) * 65],
                                                start=(kc == 0 and sub == 0),
                                                stop=(kc == kc_max - 1
                                                      and sub == nsub - 1)))
                    for sub in range(nsub):
                        if lay == 1:
                            ph1, pr1 = ao_banks[0]
                            ph2, pr2 = ao_banks[1]
                            with b.task("dve", reads=[pr1], writes=[ecol_r]) as tk:
                                tk.emit(lambda e, ph1=ph1, sub=sub: e.reciprocal(
                                    ecol[:, 2:3], ph1[:, sub * 65 + 64:sub * 65 + 65]))
                            with b.task("dve", reads=[pr2], writes=[ecol_r]) as tk:
                                tk.emit(lambda e, ph2=ph2, sub=sub: e.reciprocal(
                                    ecol[:, 3:4], ph2[:, sub * 65 + 64:sub * 65 + 65]))
                            with b.task("dve", reads=[ecol_r, neglam_r],
                                        writes=[ecol_r]) as tk:
                                tk.emit(lambda e: e.tensor_scalar(
                                    ecol[:, 4:5], ecol[:, 3:4], neglam[:, 0:1], None,
                                    ALU.mult))
                            with b.task("act", reads=[pr1, ecol_r], writes=[etmp_r]) as tk:
                                tk.emit(lambda e, ph1=ph1, sub=sub: e.activation(
                                    etmp[:, 0:64], ph1[:, sub * 65:sub * 65 + 64],
                                    AF.Identity, scale=ecol[:, 2:3]))
                            with b.task("dve", reads=[pr2, ecol_r, etmp_r],
                                        writes=[etmp_r]) as tk:
                                tk.emit(lambda e, ph2=ph2, sub=sub: e.scalar_tensor_tensor(
                                    etmp[:, 64:128], ph2[:, sub * 65:sub * 65 + 64],
                                    ecol[:, 4:5], etmp[:, 0:64], ALU.mult, ALU.add))
                            with b.task("dve", reads=[etmp_r], writes=[etmp_r]) as tk:
                                tk.emit(lambda e: e.tensor_tensor(
                                    etmp[:, 128:192], etmp[:, 64:128], etmp[:, 64:128],
                                    ALU.mult))
                            with b.task("dve", reads=[etmp_r], writes=[ecol_r]) as tk:
                                tk.emit(lambda e: e.tensor_reduce(
                                    ecol[:, 5:6], etmp[:, 128:192], AX.X, ALU.add))
                            with b.task("act", reads=[ecol_r, cmisc_r],
                                        writes=[ecol_r]) as tk:
                                tk.emit(lambda e: e.activation(
                                    ecol[:, 6:7], ecol[:, 5:6], AF.Sqrt,
                                    bias=cmisc_sb[:, 2:3], scale=1.0 / 64.0))
                            with b.task("dve", reads=[ecol_r], writes=[ecol_r]) as tk:
                                tk.emit(lambda e: e.reciprocal(ecol[:, 7:8], ecol[:, 6:7]))
                            with b.task("dve", reads=[etmp_r, ecol_r],
                                        writes=[etmp_r]) as tk:
                                tk.emit(lambda e: e.tensor_scalar(
                                    etmp[:, 192:256], etmp[:, 64:128], ecol[:, 7:8],
                                    None, ALU.mult))
                            with b.task("dve", reads=[etmp_r, rmsbc_r],
                                        writes=[ao_sb_r]) as tk:
                                tk.emit(lambda e, sub=sub, vh=vh: e.tensor_tensor(
                                    ao_sb[:, sub * D + vh * 64: sub * D + (vh + 1) * 64],
                                    etmp[:, 192:256], rmsbc[:, 0:64], ALU.mult))
                        else:
                            for mi, j in enumerate(members):
                                ph1, pr1 = ao_banks[mi]
                                with b.task("dve", reads=[pr1], writes=[ecol_r]) as tk:
                                    tk.emit(lambda e, ph1=ph1, sub=sub, mi=mi: e.reciprocal(
                                        ecol[:, 2 + mi:3 + mi],
                                        ph1[:, sub * 65 + 64:sub * 65 + 65]))
                                with b.task("act", reads=[pr1, ecol_r],
                                            writes=[ao_sb_r]) as tk:
                                    tk.emit(lambda e, ph1=ph1, sub=sub, j=j, mi=mi:
                                            e.activation(
                                        ao_sb[:, sub * D + j * 64: sub * D + (j + 1) * 64],
                                        ph1[:, sub * 65:sub * 65 + 64],
                                        AF.Identity, scale=ecol[:, 2 + mi:3 + mi]))
                for sub in range(nsub):
                    with b.task("pe", reads=[ao_sb_r, eye_r], writes=[pbf_r]) as tk:
                        for c in range(NC):
                            tk.emit(lambda e, sub=sub, c=c: e.transpose(
                                pbf_h[:, c * 128:(c + 1) * 128],
                                ao_sb[:, sub * D + c * 128: sub * D + (c + 1) * 128],
                                eye_sb[:]))
                    with b.task("dve", reads=[pbf_r], writes=[aoT_r]) as tk:
                        tk.emit(lambda e: e.tensor_copy(aoT[:, 0:D], pbf_h[:, 0:D]))
                    tglob = qs * nsub + sub
                    gslot = tglob % 2
                    with b.task("sp", reads=[gsp_r], writes=[gbuf_r],
                                dma_target=gbuf_r) as tk:
                        tk.dma(gbuf_sb[:, gslot * D:(gslot + 1) * D],
                               AP_gspill_tile(tglob), gbuf_r)
                    for c in range(NC):
                        buf = (c % 2) * D
                        with b.task("sp", reads=[WBF_R[wproj_name]], writes=[wp_s_r],
                                    dma_target=wp_s_r) as tk:
                            tk.dma(wp_s[:, buf:buf + D],
                                   WBF[wproj_name][c * 128:(c + 1) * 128, :], wp_s_r)
                        for half in range(2):
                            ph_y, pr_y = PF[half]
                            with b.task("pe", reads=[wp_s_r, aoT_r], writes=[pr_y]) as tk:
                                tk.emit(lambda e, c=c, buf=buf, half=half, ph_y=ph_y:
                                        e.matmul(
                                            ph_y[:],
                                            aoT[:, c * 128:(c + 1) * 128],
                                            wp_s[:, buf + half * 512:buf + half * 512 + 512],
                                            start=(c == 0), stop=(c == NC - 1)))
                    for half in range(2):
                        ph_y, pr_y = PF[half]
                        with b.task("dve", reads=[pr_y, gbuf_r], writes=[tmpf_r]) as tk:
                            tk.emit(lambda e, ph_y=ph_y, half=half, gslot=gslot:
                                    e.tensor_tensor(
                                        tmpf[:, 0:512], ph_y[:],
                                        gbuf_sb[:, gslot * D + half * 512:
                                                gslot * D + half * 512 + 512], ALU.mult))
                        with b.task("dve", reads=[tmpf_r, x_t[tglob]],
                                    writes=[x_t[tglob]]) as tk:
                            tk.emit(lambda e, tglob=tglob, half=half: e.tensor_tensor(
                                x_sb[:, tglob * D + half * 512: tglob * D + half * 512 + 512],
                                x_sb[:, tglob * D + half * 512: tglob * D + half * 512 + 512],
                                tmpf[:, 0:512], ALU.add))

        def l2_rms_transform():
            AR.mark()
            AF32.mark()
            r2t, r2t_r = af("l2_r2t", 128)
            r2flat, r2flat_r = ab("l2_r2flat", S)
            gp1, gp1_r = af("l2_gp1", 16)
            ph_t2, pr_t2 = PF[6]
            with b.task("pe", reads=[rstd2_r, eye32_r], writes=[pr_t2]) as tk:
                tk.emit(lambda e: e.transpose(ph_t2[0:16, 0:128], rstd2_sb[:, 0:16],
                                              eye_f32[:]))
            with b.task("dve", reads=[pr_t2], writes=[r2t_r]) as tk:
                tk.emit(lambda e: e.tensor_copy(r2t[0:16, 0:128], ph_t2[0:16, 0:128]))
            dscr_r = b.res("l2_dscr", dma=True)
            dflat = dscr.rearrange("p n -> (p n)")
            with b.task("sp", reads=[r2t_r], writes=[dscr_r], dma_target=dscr_r) as tk:
                tk.dma(dflat[0:S].rearrange("(t n) -> t n", n=128),
                       r2t[0:16, 0:128], dscr_r)
            with b.task("pool", reads=[dscr_r], writes=[r2flat_r],
                        dma_target=r2flat_r) as tk:
                tk.dma(r2flat[0:1, 0:S],
                       dflat[0:S].rearrange("(o x) -> o x", o=1), r2flat_r)
            with b.task("pool", writes=[gp1_r], dma_target=gp1_r) as tk:
                for c in range(NC):
                    tk.dma(gp1[:, c:c + 1],
                           AP["L2_attn_gamma"].rearrange("(c p) -> c p", p=128)[c].rearrange(
                               "(p o) -> p o", o=1), gp1_r)
            with b.task("dve", reads=[gp1_r], writes=[gp1_r]) as tk:
                tk.emit(lambda e: e.tensor_scalar(gp1[:, 8:16], gp1[:, 0:8], 1.0, None,
                                                  ALU.add))
            for ts4 in range(4):
                ph_b, pr_b = PF[4]
                with b.task("pe", reads=[r2flat_r, ones_r], writes=[pr_b]) as tk:
                    tk.emit(lambda e, ts4=ts4: e.matmul(
                        ph_b[:], ones_sb[0:1, 0:128],
                        r2flat[0:1, ts4 * 512:(ts4 + 1) * 512], start=True, stop=True))
                for c in range(NC):
                    with b.task("dve", reads=[pr_b, gp1_r, xnT_r], writes=[xnT_r]) as tk:
                        tk.emit(lambda e, c=c, ts4=ts4, ph_b=ph_b: e.scalar_tensor_tensor(
                            xnT_sb[:, c * S + ts4 * 512: c * S + ts4 * 512 + 512],
                            ph_b[:], gp1[:, 8 + c:9 + c],
                            xnT_sb[:, c * S + ts4 * 512: c * S + ts4 * 512 + 512],
                            ALU.mult, ALU.mult))
            AR.reset_to_mark()
            AF32.reset_to_mark()

        def debug_dump_g():
            gdb, gdb_r = ab("dbg_g", 2 * D)
            gf, gf_r = af("dbg_gf", 512)
            for t in range(NT):
                slot = t % 2
                with b.task("sp", reads=[gsp_r], writes=[gdb_r], dma_target=gdb_r) as tk:
                    tk.dma(gdb[:, slot * D:(slot + 1) * D], AP_gspill_tile(t), gdb_r)
                for half in range(2):
                    with b.task("dve", reads=[gdb_r], writes=[gf_r]) as tk:
                        tk.emit(lambda e, slot=slot, half=half: e.tensor_copy(
                            gf[:, 0:512],
                            gdb[:, slot * D + half * 512: slot * D + half * 512 + 512]))
                    with b.task("sp", reads=[gf_r], writes=[out_r], dma_target=out_r) as tk:
                        tk.dma(out.rearrange("(t p) d -> t p d", p=128)[t][:, half * 512:half * 512 + 512],
                               gf[:, 0:512], out_r)

        for lay in layers:
            ln_phase(lay)
            if lay == 0:
                layer0()
            else:
                if debug == "noffn":
                    with b.task("pool", reads=[x_r], writes=[gsp_r],
                                dma_target=gsp_r) as tk:
                        for t in range(NT):
                            tk.dma(AP_gspill_tile(t),
                                   x_sb[:, t * D:(t + 1) * D], gsp_r)
                else:
                    ffn_generic(lay)
                if debug == "w":
                    break
                if debug == "g":
                    debug_dump_g()
                    break
                if lay == 2:
                    l2_rms_transform()
                attn_generic(lay)
            reset_arenas()

        if debug is None:
            with b.task("sp", reads=x_t + [x_r], writes=[out_r], dma_target=out_r) as tk:
                for half in range(2):
                    tk.dma(out.rearrange("(h t p) d -> h p t d", h=2, p=128)[half],
                           x_sb[:, half * 8 * D:(half + 1) * 8 * D].rearrange(
                               "p (t d) -> p t d", d=D), out_r)
        b.final_wait("sp")

        with nc.Block() as block:
            b.replay(block)
        globals()["_LAST_B"] = b
    return nc



def _get_attention_slopes(n):
    def pow2_slopes(n):
        start = 2 ** (-(2 ** (-(math.log2(n) - 3))))
        return [start * start ** i for i in range(n)]
    if math.log2(n).is_integer():
        return pow2_slopes(n)
    c = 2 ** math.floor(math.log2(n))
    return pow2_slopes(c) + _get_attention_slopes(2 * c)[0::2][: n - c]


def make_consts():
    eye = np.eye(128, dtype=np.float32)
    kq = np.arange(128, dtype=np.float32)
    tri = (kq[:, None] <= kq[None, :]).astype(np.float32)
    slopes = np.asarray(_get_attention_slopes(16), dtype=np.float32)
    alibid = np.zeros((128, 256), dtype=np.float32)
    for h in range(16):
        for d in range(16):
            alibid[:, 16 * h + d] = slopes[h] * (kq - 128 * d - 63.5)
    onesrow = np.zeros((128, 512), dtype=np.float32)
    onesrow[0, :] = 1.0
    cmisc = np.zeros((128, 8), dtype=np.float32)
    cmisc[:, 0] = 1e-5
    cmisc[:, 1] = 1e-24
    cmisc[:, 2] = 1e-6
    cmisc[:, 3] = 1.0
    return {"c_eye": eye, "c_tri": tri, "c_alibid": alibid,
            "c_onesrow": onesrow, "c_misc": cmisc}


def kernel(**inputs):
    return _run(inputs, layers=(0, 1, 2, 3))


def _run(inputs, layers, x_override=None, debug=None):
    n = 8
    x_full = np.asarray(x_override if x_override is not None else inputs["L0_attn_X"],
                        dtype=np.float32)
    consts = make_consts()
    nc = build_program(list(layers), n_cores=n, debug=debug)
    in_maps = []
    for core in range(n):
        m = {"x_in": np.ascontiguousarray(x_full[core])}
        for k, v in inputs.items():
            if k == "L0_attn_X":
                continue
            m[k] = np.ascontiguousarray(np.asarray(v, dtype=np.float32))
        m.update(consts)
        in_maps.append(m)
    res = run_bass_kernel_spmd(nc, in_maps, core_ids=list(range(n)))
    out = np.stack([res.results[i]["out"] for i in range(n)], axis=0)
    return out.astype(np.float32)


# version 06
# speedup vs baseline: 1.1015x; 1.0178x over previous
import contextlib
import math
import numpy as np

import concourse.bass as bass
import concourse.mybir as mybir
from concourse.bass_utils import run_bass_kernel_spmd

F32 = mybir.dt.float32
BF16 = mybir.dt.bfloat16
AF = mybir.ActivationFunctionType
ALU = mybir.AluOpType
AX = mybir.AxisListType

S = 2048
D = 1024
NT = 16
NC = 8
EPS_LN = 1e-5
LAMBDA_INIT = 0.8 - 0.6 * math.exp(-0.3 * 1)


class Sem:
    __slots__ = ("name", "count", "handle", "_cm")
    def __init__(self, name):
        self.name = name
        self.count = 0
        self.handle = None


class Res:
    __slots__ = ("name", "writer", "readers", "dma_sem", "init_waits")
    def __init__(self, name, dma_sem=None, init_waits=None):
        self.name = name
        self.writer = None
        self.readers = []
        self.dma_sem = dma_sem
        self.init_waits = dict(init_waits) if init_waits else {}


class Queue:
    __slots__ = ("name", "items", "sem", "hw")
    def __init__(self, name, sem):
        self.name = name
        self.items = []
        self.sem = sem
        self.hw = {}


class Task:
    def __init__(self, b, q, milestone_sem):
        self.b = b
        self.q = q
        self.milestone_sem = milestone_sem
        self.n = 0

    def emit(self, fn):
        self.q.items.append(("instr", fn, None, 0))
        self.n += 1

    def dma(self, out, in_, target_res, **kw):
        if target_res.dma_sem is None:
            target_res.dma_sem = self.b.new_sem("d_" + target_res.name)
        sem = target_res.dma_sem
        sem.count += 16
        self.q.items.append(
            ("instr", lambda e: e.dma_start(out=out, in_=in_, **kw), sem, 16))
        self.n += 1
        if not hasattr(self, "_dma_targets"):
            self._dma_targets = {}
        self._dma_targets[target_res] = sem.count
        self._dma_val = sem.count
        self._dma_sem = sem


class B:
    def __init__(self, nc):
        self.nc = nc
        self.sems = []
        self.queues = {}
        for qn in ("pe", "act", "dve", "pool", "sp"):
            self.queues[qn] = Queue(qn, self.new_sem("q_" + qn))
        self.tombstone = {}

    def new_sem(self, name):
        s = Sem(name)
        self.sems.append(s)
        return s

    def res(self, name, dma=False):
        return Res(name, dma_sem=None, init_waits=self.tombstone)

    def _wait(self, q, sem, val):
        if val <= 0:
            return
        if q.hw.get(sem, 0) >= val:
            return
        q.hw[sem] = val
        q.items.append(("wait", sem, val))

    @contextlib.contextmanager
    def task(self, qn, reads=(), writes=(), dma_target=None):
        q = self.queues[qn]
        waits = {}
        def need(sem, val):
            if sem is not None and val > waits.get(sem, 0):
                waits[sem] = val
        for r in reads:
            if r.writer:
                need(*r.writer)
            for sem, val in r.init_waits.items():
                need(sem, val)
            r.init_waits = {}
        for w in writes:
            if w.writer:
                need(*w.writer)
            for rd in w.readers:
                need(*rd)
            for sem, val in w.init_waits.items():
                need(sem, val)
            w.init_waits = {}
        for sem, val in sorted(waits.items(), key=lambda kv: kv[0].name):
            self._wait(q, sem, val)
        t = Task(self, q, q.sem)
        yield t
        if t.n == 0:
            return
        if hasattr(t, "_dma_sem"):
            all_ms = list(t._dma_targets.items())
            for w in writes:
                if w in t._dma_targets:
                    w.writer = (w.dma_sem, t._dma_targets[w])
                else:
                    w.writer = (t._dma_sem, t._dma_val)
                w.readers = []
            for r in reads:
                for res, val in all_ms:
                    r.readers.append((res.dma_sem, val))
        else:
            q.sem.count += 1
            ms = (q.sem, q.sem.count)
            for i in range(len(q.items) - 1, -1, -1):
                kind = q.items[i][0]
                if kind == "instr":
                    _, fn, s0, a0 = q.items[i]
                    assert s0 is None, "engine task mixing with dma incs"
                    q.items[i] = ("instr", fn, q.sem, 1)
                    break
            for w in writes:
                w.writer = ms
                w.readers = []
            for r in reads:
                r.readers.append(ms)

    def set_tombstone(self):
        self.tombstone = {s: s.count for s in self.sems if s.count > 0}

    def final_wait(self, qn):
        q = self.queues[qn]
        for s in self.sems:
            if s.count > 0:
                self._wait(q, s, s.count)

    def replay(self, block):
        handles = {}
        for s in self.sems:
            cm = self.nc.semaphore(s.name)
            s._cm = cm
            s.handle = cm.__enter__()
        def run(engine, q):
            for it in q.items:
                if it[0] == "wait":
                    engine.wait_ge(it[1].handle, it[2])
                else:
                    inst = it[1](engine)
                    if it[2] is not None:
                        inst.then_inc(it[2].handle, it[3])
        @block.tensor
        def _(e):
            run(e, self.queues["pe"])
        @block.scalar
        def _(e):
            run(e, self.queues["act"])
        @block.vector
        def _(e):
            run(e, self.queues["dve"])
        @block.gpsimd
        def _(e):
            run(e, self.queues["pool"])
        @block.sync
        def _(e):
            run(e, self.queues["sp"])


class Arena:
    def __init__(self, b, handle, cols):
        self.b = b
        self.h = handle
        self.cols = cols
        self.off = 0
        self.marks = []

    def alloc(self, name, n):
        assert self.off + n <= self.cols, \
            f"arena overflow: {name} needs {n} at {self.off}/{self.cols}"
        v = self.h[:, self.off:self.off + n]
        self.off += n
        return v, self.b.res(name)

    def mark(self):
        self.marks.append(self.off)

    def reset_to_mark(self):
        self.off = self.marks.pop()
        self.b.set_tombstone()

    def reset(self):
        self.off = 0
        self.marks = []
        self.b.set_tombstone()



def build_program(layers, n_cores=8, debug=None):
    nc = bass.Bass("TRN2", target_bir_lowering=False)
    AP = {}

    def din(name, shape, dtype=F32):
        AP[name] = nc.dram_tensor(name, list(shape), dtype, kind="ExternalInput").ap()
        return AP[name]

    din("x_in", (S, D))
    din("L0_attn_Wq", (D, D)); din("L0_attn_bq", (D,))
    din("L0_attn_Wk", (D, D)); din("L0_attn_bk", (D,))
    din("L0_attn_Wv", (D, D)); din("L0_attn_bv", (D,))
    din("L0_ffn_weight", (D, D))
    din("L1_attn_Wqkv", (D, 3 * D)); din("L1_attn_Wproj", (D, D))
    din("L1_attn_lambda_q1", (32,)); din("L1_attn_lambda_k1", (32,))
    din("L1_attn_lambda_q2", (32,)); din("L1_attn_lambda_k2", (32,))
    din("L1_attn_rms_scale", (64,))
    din("L1_ffn_gate_w", (8, D)); din("L1_ffn_expert_bias", (8,))
    din("L1_ffn_w_up", (8, 1408, D)); din("L1_ffn_w_down", (8, D, 1408))
    din("L1_ffn_shared_w_up", (1408, D)); din("L1_ffn_shared_w_down", (D, 1408))
    din("L2_attn_gamma", (D,)); din("L2_attn_w_qkv", (D, 3 * D)); din("L2_attn_w_out", (D, D))
    din("L2_ffn_Ws1", (D, 4096)); din("L2_ffn_bs1", (4096,))
    din("L2_ffn_Ws2", (2048, D)); din("L2_ffn_bs2", (D,))
    din("L2_ffn_We1", (12, D, 6144)); din("L2_ffn_be1", (12, 6144))
    din("L2_ffn_We2", (12, 3072, D)); din("L2_ffn_be2", (12, D))
    din("L2_ffn_Wr", (D, 12)); din("L2_ffn_br", (12,))
    din("L3_attn_w_qkv", (D, 3 * D)); din("L3_attn_w_proj", (D, D))
    din("L3_ffn_Wg", (D, 8))
    din("L3_ffn_Ws1", (D, 2048)); din("L3_ffn_Ws3", (D, 2048)); din("L3_ffn_Ws2", (2048, D))
    din("L3_ffn_We1", (8, D, 2048)); din("L3_ffn_We3", (8, D, 2048)); din("L3_ffn_We2", (8, 2048, D))
    din("c_eye", (128, 128))
    din("c_tri", (128, 128))
    din("c_alibid", (128, 256))
    din("c_onesrow", (128, 512))
    din("c_misc", (128, 8))
    out = nc.dram_tensor("out", [S, D], F32, kind="ExternalOutput").ap()
    g_spill = nc.dram_tensor("g_spill", [S, D], BF16, kind="Internal").ap()
    dscr = nc.dram_tensor("dscratch", [128, 2048], F32, kind="Internal").ap()

    b = B(nc)
    st = contextlib.ExitStack()
    with st:
        def sb(name, shape, dtype=BF16):
            h = st.enter_context(nc.sbuf_tensor(name, list(shape), dtype))
            r = b.res(name, dma=True)
            return h, r

        x_sb, x_r = sb("x", (128, NT * D), F32)
        x_t = [b.res(f"x{t}") for t in range(NT)]
        xnT_sb, xnT_r = sb("xnT", (128, NC * S), BF16)
        eye_sb, eye_r = sb("eye", (128, 128), BF16)
        eye_f32, eye32_r = sb("eye32", (128, 128), F32)
        tri_sb, tri_r = sb("tri", (128, 128), BF16)
        alibi_sb, alibi_r = sb("alibid", (128, 256), F32)
        ones_sb, ones_r = sb("onesrow", (128, 512), BF16)
        cols_sb, cols_r = sb("cols", (128, 64), F32)
        rows_sb, rows_r = sb("rows", (128, 128), F32)
        mvav_sb, mvav_r = sb("mvav", (128, 16), F32)
        bn6_sb, bn6_r = sb("bn6", (128, 12), F32)
        rstd2_sb, rstd2_r = sb("rstd2", (128, 16), F32)
        gbuf_sb, gbuf_r = sb("gbuf", (128, 2 * D), BF16)
        cmisc_sb, cmisc_r = sb("cmisc", (128, 8), F32)
        gsp_r = b.res("g_spill_dram", dma=True)
        out_r = b.res("out_dram", dma=True)
        af32_h = st.enter_context(nc.sbuf_tensor("af32", [128, 1280], F32))
        abf_h = st.enter_context(nc.sbuf_tensor("abf", [128, 50560], BF16))
        AR = Arena(b, abf_h, 50560)
        AF32 = Arena(b, af32_h, 1280)

        def ab(name, n):
            return AR.alloc(name, n)

        def af(name, n):
            return AF32.alloc(name, n)

        def reset_arenas():
            AR.reset()
            AF32.reset()

        PF = []
        for i in range(7):
            h = st.enter_context(nc.psum_tensor(f"pf{i}", [128, 512], F32))
            PF.append((h, b.res(f"pf{i}")))
        pbf_h = st.enter_context(nc.psum_tensor("pbf", [128, 1024], BF16))
        pbf_r = b.res("pbf")

        with b.task("sp", writes=[x_r] + x_t) as t:
            for half in range(2):
                t.dma(x_sb[:, half * 8 * D:(half + 1) * 8 * D].rearrange(
                          "p (t d) -> p t d", d=D),
                      AP["x_in"].rearrange("(h t p) d -> h p t d", h=2, p=128)[half],
                      x_r)
        with b.task("pool", writes=[eye_r, tri_r, alibi_r, ones_r, cmisc_r]) as t:
            t.dma(eye_sb[:], AP["c_eye"], eye_r)
            t.dma(eye_f32[:], AP["c_eye"], eye32_r)
            t.dma(tri_sb[:], AP["c_tri"], tri_r)
            t.dma(alibi_sb[:], AP["c_alibid"], alibi_r)
            t.dma(ones_sb[:], AP["c_onesrow"], ones_r)
            t.dma(cmisc_sb[:], AP["c_misc"], cmisc_r)

        for t in x_t:
            t.writer = x_r.writer

        PRECAST = [
            ("L0_attn_Wv", D, D), ("L0_attn_Wk", D, D), ("L0_attn_Wq", D, D),
            ("L0_ffn_weight", D, D),
            ("L1_ffn_shared_w_up", 1408, D), ("L1_ffn_shared_w_down", D, 1408),
            ("L1_ffn_w_up", 8 * 1408, D), ("L1_ffn_w_down", 8 * D, 1408),
            ("L1_attn_Wqkv", D, 3 * D), ("L1_attn_Wproj", D, D),
            ("L2_ffn_Ws1", D, 4096), ("L2_ffn_Ws2", 2048, D),
            ("L2_ffn_We1", 12 * D, 6144), ("L2_ffn_We2", 12 * 3072, D),
            ("L2_attn_w_qkv", D, 3 * D), ("L2_attn_w_out", D, D),
            ("L3_ffn_Ws1", D, 2048), ("L3_ffn_Ws3", D, 2048), ("L3_ffn_Ws2", 2048, D),
            ("L3_ffn_We1", 8 * D, 2048), ("L3_ffn_We3", 8 * D, 2048),
            ("L3_ffn_We2", 8 * 2048, D),
            ("L3_attn_w_qkv", D, 3 * D), ("L3_attn_w_proj", D, D),
        ]
        WBF = {}
        WBF_R = {}
        for name, rows, cols in PRECAST:
            WBF[name] = nc.dram_tensor("wbf_" + name, [rows, cols], BF16,
                                       kind="Internal").ap()
            WBF_R[name] = b.res("wbf_" + name, dma=True)
        CHUNK_ROWS = 2 * 1024 * 1024

        def emit_precast(name):
            rows, cols = next((r, c) for n, r, c in PRECAST if n == name)
            src = AP[name]
            if len(src.shape) == 3:
                src = src.rearrange("g n k -> (g n) k")
            step = max(1, CHUNK_ROWS // cols)
            with b.task("pool", writes=[WBF_R[name]], dma_target=WBF_R[name]) as tk:
                for r0 in range(0, rows, step):
                    r1 = min(r0 + step, rows)
                    tk.dma(WBF[name][r0:r1, :], src[r0:r1, :], WBF_R[name])

        for name, _, _ in PRECAST:
            emit_precast(name)

        def ln_phase(lay):
            AR.mark()
            xn_scr, xn_scr_r = ab(f"l{lay}_xnscr", 2 * D)
            for t in range(NT):
                xt = x_sb[:, t * D:(t + 1) * D]
                with b.task("dve", reads=[x_t[t]], writes=[bn6_r]) as tk:
                    tk.emit(lambda e, xt=xt: e.bn_stats(bn6_sb[:, 0:6], xt[:, 0:512]))
                    tk.emit(lambda e, xt=xt: e.bn_stats(bn6_sb[:, 6:12], xt[:, 512:1024]))
                with b.task("dve", reads=[bn6_r], writes=[mvav_r]) as tk:
                    tk.emit(lambda e: e.bn_aggr(mvav_sb[:, 0:2], bn6_sb[:]))
                with b.task("act", reads=[mvav_r, cmisc_r], writes=[mvav_r]) as tk:
                    tk.emit(lambda e: e.activation(mvav_sb[:, 2:3], mvav_sb[:, 1:2],
                                                   AF.Sqrt, bias=cmisc_sb[:, 0:1]))
                with b.task("dve", reads=[mvav_r], writes=[mvav_r]) as tk:
                    tk.emit(lambda e: e.reciprocal(mvav_sb[:, 3:4], mvav_sb[:, 2:3]))
                with b.task("dve", reads=[mvav_r], writes=[mvav_r]) as tk:
                    tk.emit(lambda e: e.tensor_scalar(
                        mvav_sb[:, 4:5], mvav_sb[:, 0:1], mvav_sb[:, 3:4], -1.0,
                        ALU.mult, ALU.mult))
                if lay == 2:
                    with b.task("act", reads=[mvav_r, cmisc_r], writes=[mvav_r]) as tk:
                        tk.emit(lambda e: e.activation(mvav_sb[:, 5:6], mvav_sb[:, 1:2],
                                                       AF.Sqrt, bias=cmisc_sb[:, 1:2]))
                    with b.task("dve", reads=[mvav_r], writes=[mvav_r]) as tk:
                        tk.emit(lambda e: e.reciprocal(mvav_sb[:, 6:7], mvav_sb[:, 5:6]))
                    with b.task("dve", reads=[mvav_r], writes=[rstd2_r]) as tk:
                        tk.emit(lambda e, t=t: e.tensor_scalar(
                            rstd2_sb[:, t:t + 1], mvav_sb[:, 2:3], mvav_sb[:, 6:7], None,
                            ALU.mult))
                slot = t % 2
                xns = xn_scr[:, slot * D:(slot + 1) * D]
                with b.task("act", reads=[mvav_r, x_t[t]], writes=[xn_scr_r]) as tk:
                    tk.emit(lambda e, xt=xt, xns=xns: e.activation(
                        xns, xt, AF.Identity,
                        bias=mvav_sb[:, 4:5], scale=mvav_sb[:, 3:4]))
                with b.task("pe", reads=[xn_scr_r, eye_r], writes=[pbf_r]) as tk:
                    for c in range(NC):
                        tk.emit(lambda e, xns=xns, c=c: e.transpose(
                            pbf_h[:, c * 128:(c + 1) * 128],
                            xns[:, c * 128:(c + 1) * 128], eye_sb[:]))
                with b.task("dve", reads=[pbf_r], writes=[xnT_r]) as tk:
                    dst = xnT_sb.ap().rearrange("p (c n) -> p c n", c=NC)[:, :, t * 128:(t + 1) * 128]
                    src = pbf_h.ap().rearrange("p (c n) -> p c n", c=NC)
                    tk.emit(lambda e, dst=dst, src=src: e.tensor_copy(dst, src))
            AR.reset_to_mark()

        def spill_g(lay, src_sb, src_r, t):
            with b.task("sp", reads=[src_r], writes=[gsp_r], dma_target=gsp_r) as tk:
                tk.dma(AP_gspill_tile(t), src_sb, gsp_r)

        def AP_gspill_tile(t):
            return g_spill.rearrange("(t p) d -> t p d", p=128)[t]

        def mm_group(tk, psum_ap, pairs, extra_first=None, first_start=True,
                     stop=True):
            first = first_start
            if extra_first is not None:
                lh, rh = extra_first
                tk.emit(lambda e, lh=lh, rh=rh, f=first: e.matmul(
                    psum_ap, lh, rh, start=f, stop=False))
                first = False
            for i, (lh, rh) in enumerate(pairs):
                last = stop and (i == len(pairs) - 1)
                tk.emit(lambda e, lh=lh, rh=rh, f=first, l=last: e.matmul(
                    psum_ap, lh, rh, start=f, stop=l))
                first = False

        def layer0():
            AR.mark()
            w_raw, w_raw_r = ab("l0_wraw", D)
            swt, swt_r = ab("l0_swt", NC * D)
            gtile, gtile_r = ab("l0_gt", 2 * D)
            W = WBF["L0_ffn_weight"].rearrange("(c p) i -> c p i", p=128)
            for c in range(NC):
                with b.task("sp", reads=[WBF_R["L0_ffn_weight"]], writes=[w_raw_r],
                            dma_target=w_raw_r) as tk:
                    tk.dma(w_raw[:], W[c], w_raw_r)
                with b.task("act", reads=[w_raw_r], writes=[w_raw_r]) as tk:
                    tk.emit(lambda e: e.activation(w_raw[:], w_raw[:], AF.Sign))
                with b.task("pe", reads=[w_raw_r, eye_r], writes=[pbf_r]) as tk:
                    for ic in range(NC):
                        tk.emit(lambda e, ic=ic: e.transpose(
                            pbf_h[:, ic * 128:(ic + 1) * 128],
                            w_raw[:, ic * 128:(ic + 1) * 128], eye_sb[:]))
                with b.task("dve", reads=[pbf_r], writes=[swt_r]) as tk:
                    dst = swt.rearrange("p (ic n) -> p ic n", ic=NC)[:, :, c * 128:(c + 1) * 128]
                    src = pbf_h.ap().rearrange("p (ic n) -> p ic n", ic=NC)
                    tk.emit(lambda e, dst=dst, src=src: e.tensor_copy(dst, src))
            for t in range(NT):
                slot = t % 2
                pb = [PF[slot * 2], PF[slot * 2 + 1]]
                for half in range(2):
                    ph, pr = pb[half]
                    with b.task("pe", reads=[swt_r, xnT_r], writes=[pr]) as tk:
                        pairs = [(xnT_sb[:, c * S + t * 128: c * S + (t + 1) * 128],
                                  swt[:, c * D + half * 512: c * D + half * 512 + 512])
                                 for c in range(NC)]
                        mm_group(tk, ph[:], pairs)
                for half in range(2):
                    ph, pr = pb[half]
                    with b.task("act", reads=[pr], writes=[gtile_r]) as tk:
                        tk.emit(lambda e, ph=ph, slot=slot, half=half: e.activation(
                            gtile[:, slot * D + half * 512: slot * D + half * 512 + 512],
                            ph[:], AF.Silu))
                with b.task("sp", reads=[gtile_r], dma_target=gsp_r, writes=[gsp_r]) as tk:
                    tk.dma(AP_gspill_tile(t), gtile[:, slot * D:(slot + 1) * D], gsp_r)
            AR.reset_to_mark()


            def transpose_w(name, wt_sb_, wt_r_, raw, raw_r):
                Wsrc = WBF[name].rearrange("(c p) i -> c p i", p=128)
                for c in range(NC):
                    with b.task("sp", reads=[WBF_R[name]], writes=[raw_r],
                                dma_target=raw_r) as tk:
                        tk.dma(raw[:], Wsrc[c], raw_r)
                    with b.task("pe", reads=[raw_r, eye_r], writes=[pbf_r]) as tk:
                        for ic in range(NC):
                            tk.emit(lambda e, ic=ic: e.transpose(
                                pbf_h[:, ic * 128:(ic + 1) * 128],
                                raw[:, ic * 128:(ic + 1) * 128], eye_sb[:]))
                    with b.task("dve", reads=[pbf_r], writes=[wt_r_]) as tk:
                        dst = wt_sb_.rearrange("p (ic n) -> p ic n", ic=NC)[:, :, c * 128:(c + 1) * 128]
                        src = pbf_h.ap().rearrange("p (ic n) -> p ic n", ic=NC)
                        tk.emit(lambda e, dst=dst, src=src: e.tensor_copy(dst, src))

            def load_bias_row(pad, pad_r, name, off, width):
                with b.task("dve", writes=[pad_r]) as tk:
                    tk.emit(lambda e: e.memset(pad[:], 0.0))
                with b.task("pool", writes=[pad_r], dma_target=pad_r) as tk:
                    tk.dma(pad[0:1, 0:width], AP[name].rearrange("(o d) -> o d", o=1), pad_r)

            vt, vt_r = ab("l0_vt", NT * 1040)
            AR.mark()
            wv_t, wv_t_r = ab("l0_wvT", NC * D)
            wraw2, wraw2_r = ab("l0_wraw2", D)
            bv_pad, bv_pad_r = ab("l0_bvpad", D)
            load_bias_row(bv_pad, bv_pad_r, "L0_attn_bv", 0, D)
            transpose_w("L0_attn_Wv", wv_t, wv_t_r, wraw2, wraw2_r)
            with b.task("dve", writes=[vt_r]) as tk:
                dst = vt.rearrange("p (t n) -> p t n", t=NT)[:, :, 1039:1040]
                tk.emit(lambda e, dst=dst: e.memset(dst, 1.0))
            for t in range(NT):
                for half in range(2):
                    ph, pr = PF[2 + half]
                    with b.task("pe", reads=[wv_t_r, xnT_r, ones_r, bv_pad_r], writes=[pr]) as tk:
                        pairs = [(xnT_sb[:, c * S + t * 128: c * S + (t + 1) * 128],
                                  wv_t[:, c * D + half * 512: c * D + half * 512 + 512])
                                 for c in range(NC)]
                        mm_group(tk, ph[:], pairs,
                                 extra_first=(ones_sb[:, 0:128],
                                              bv_pad[:, half * 512: half * 512 + 512]))
                    with b.task("dve", reads=[pr], writes=[vt_r]) as tk:
                        tk.emit(lambda e, ph=ph, t=t, half=half: e.tensor_copy(
                            vt[:, t * 1040 + half * 512: t * 1040 + half * 512 + 512], ph[:]))
            AR.reset_to_mark()

            kt, kt_r = ab("l0_kt", NC * S)
            AR.mark()
            wk_t, wk_t_r = ab("l0_wkT", NC * D)
            wraw3, wraw3_r = ab("l0_wraw3", D)
            bk_pad, bk_pad_r = ab("l0_bkpad", D)
            load_bias_row(bk_pad, bk_pad_r, "L0_attn_bk", 0, D)
            transpose_w("L0_attn_Wk", wk_t, wk_t_r, wraw3, wraw3_r)
            for m in range(NC):
                for ts4 in range(4):
                    ph, pr = PF[ts4 % 2]
                    with b.task("pe", reads=[wk_t_r, xnT_r, ones_r, bk_pad_r], writes=[pr]) as tk:
                        pairs = [(wk_t[:, c * D + m * 128: c * D + (m + 1) * 128],
                                  xnT_sb[:, c * S + ts4 * 512: c * S + ts4 * 512 + 512])
                                 for c in range(NC)]
                        mm_group(tk, ph[:], pairs,
                                 extra_first=(bk_pad[:, m * 128:(m + 1) * 128],
                                              ones_sb[:]))
                    with b.task("act", reads=[pr], writes=[kt_r]) as tk:
                        tk.emit(lambda e, ph=ph, m=m, ts4=ts4: e.activation(
                            kt[:, m * S + ts4 * 512: m * S + ts4 * 512 + 512], ph[:],
                            AF.Identity))
            AR.reset_to_mark()

            wq_t, wq_t_r = ab("l0_wqT", NC * D)
            bq_pad, bq_pad_r = ab("l0_bqpad", D)
            load_bias_row(bq_pad, bq_pad_r, "L0_attn_bq", 0, D)
            AR.mark()
            wraw4, wraw4_r = ab("l0_wraw4", D)
            transpose_w("L0_attn_Wq", wq_t, wq_t_r, wraw4, wraw4_r)
            AR.reset_to_mark()

            ktsq, ktsq_r = ab("l0_ktsq", S)
            onescol, onescol_r = ab("l0_onescol", 16)
            cbf, cbf_r = ab("l0_cbf", 16)
            with b.task("dve", writes=[onescol_r]) as tk:
                tk.emit(lambda e: e.memset(onescol[:], 1.0))

            attn_l0(kt, kt_r, vt, vt_r, wq_t, wq_t_r, bq_pad, bq_pad_r,
                    onescol, onescol_r, ktsq, ktsq_r, cbf, cbf_r)

        def attn_l0(kt, kt_r, vt, vt_r, wq_t, wq_t_r, bias_pad, bias_pad_r,
                    onescol, onescol_r, ktsq, ktsq_r, cbf, cbf_r):
            qt, qt_r = ab("l0_qt", NC * 128)
            qtsq, qtsq_r = ab("l0_qtsq", 128)
            ptb, ptb_r = ab("l0_pt", 2 * 128)
            tmpf, tmpf_r = af("l0_tmp", 512)
            negc, negc_r = af("l0_negc", 16)
            knorm, knorm_r = af("l0_knorm", 16)

            ph_n, pr_n = PF[4]
            for q4 in range(4):
                for c in range(NC):
                    with b.task("act", reads=[kt_r], writes=[ktsq_r]) as tk:
                        tk.emit(lambda e, c=c, q4=q4: e.activation(
                            ktsq[:, 0:512], kt[:, c * S + q4 * 512: c * S + q4 * 512 + 512],
                            AF.Square))
                    with b.task("pe", reads=[ktsq_r, onescol_r], writes=[pr_n]) as tk:
                        tk.emit(lambda e, c=c: e.matmul(
                            ph_n[0:1, 0:512], onescol[:, 0:1], ktsq[:, 0:512],
                            start=(c == 0), stop=(c == NC - 1)))
                with b.task("dve", reads=[pr_n], writes=[knorm_r]) as tk:
                    tk.emit(lambda e, q4=q4: e.tensor_reduce(
                        knorm[0:1, q4:q4 + 1], ph_n[0:1, 0:512], AX.X, ALU.max))
            with b.task("dve", reads=[knorm_r], writes=[knorm_r]) as tk:
                tk.emit(lambda e: e.tensor_reduce(knorm[0:1, 4:5], knorm[0:1, 0:4],
                                                  AX.X, ALU.max))
            with b.task("act", reads=[knorm_r], writes=[knorm_r]) as tk:
                tk.emit(lambda e: e.activation(knorm[0:1, 5:6], knorm[0:1, 4:5], AF.Sqrt))

            for t in range(NT):
                for c in range(NC):
                    ph, pr = PF[5]
                    with b.task("pe", reads=[wq_t_r, xnT_r, ones_r, bias_pad_r], writes=[pr]) as tk:
                        pairs = [(wq_t[:, ci * D + c * 128: ci * D + (c + 1) * 128],
                                  xnT_sb[:, ci * S + t * 128: ci * S + (t + 1) * 128])
                                 for ci in range(NC)]
                        mm_group(tk, ph[:, 0:128], pairs,
                                 extra_first=(bias_pad[:, c * 128:(c + 1) * 128],
                                              ones_sb[:, 0:128]))
                    with b.task("act", reads=[pr], writes=[qt_r]) as tk:
                        tk.emit(lambda e, ph=ph, c=c: e.activation(
                            qt[:, c * 128:(c + 1) * 128], ph[:, 0:128],
                            AF.Identity, scale=1.0 / 32.0))
                ph_n2, pr_n2 = PF[4]
                for c in range(NC):
                    with b.task("act", reads=[qt_r], writes=[qtsq_r]) as tk:
                        tk.emit(lambda e, c=c: e.activation(
                            qtsq[:], qt[:, c * 128:(c + 1) * 128], AF.Square))
                    with b.task("pe", reads=[qtsq_r, onescol_r], writes=[pr_n2]) as tk:
                        tk.emit(lambda e, c=c: e.matmul(
                            ph_n2[0:1, 0:128], onescol[:, 0:1], qtsq[:],
                            start=(c == 0), stop=(c == NC - 1)))
                with b.task("dve", reads=[pr_n2], writes=[knorm_r]) as tk:
                    tk.emit(lambda e: e.tensor_reduce(knorm[0:1, 6:7], ph_n2[0:1, 0:128],
                                                      AX.X, ALU.max))
                with b.task("act", reads=[knorm_r], writes=[knorm_r]) as tk:
                    tk.emit(lambda e: e.activation(knorm[0:1, 7:8], knorm[0:1, 6:7], AF.Sqrt))
                with b.task("dve", reads=[knorm_r], writes=[cbf_r]) as tk:
                    tk.emit(lambda e: e.tensor_tensor(cbf[0:1, 0:1], knorm[0:1, 7:8],
                                                      knorm[0:1, 5:6], ALU.mult))
                ph_b, pr_b = PF[6]
                with b.task("pe", reads=[cbf_r, ones_r], writes=[pr_b]) as tk:
                    tk.emit(lambda e: e.matmul(ph_b[:, 0:1], ones_sb[0:1, 0:128],
                                               cbf[0:1, 0:1], start=True, stop=True))
                with b.task("act", reads=[pr_b], writes=[negc_r]) as tk:
                    tk.emit(lambda e: e.activation(negc[:, 0:1], ph_b[:, 0:1],
                                                   AF.Identity, scale=-1.0))

                aoh = [PF[2], PF[3]]

                def emit_score0(kc):
                    ph_s, pr_s = PF[kc % 2]
                    with b.task("pe", reads=[kt_r, qt_r], writes=[pr_s]) as tk:
                        pairs = [(kt[:, c * S + kc * 128: c * S + (kc + 1) * 128],
                                  qt[:, c * 128:(c + 1) * 128]) for c in range(NC)]
                        mm_group(tk, ph_s[:, 0:128], pairs)

                emit_score0(0)
                for kc in range(t + 1):
                    if kc + 1 < t + 1:
                        emit_score0(kc + 1)
                    ph_s, pr_s = PF[kc % 2]
                    slot = kc % 2
                    with b.task("act", reads=[pr_s, negc_r], writes=[ptb_r]) as tk:
                        tk.emit(lambda e, ph_s=ph_s, slot=slot: e.activation(
                            ptb[:, slot * 128:(slot + 1) * 128], ph_s[:, 0:128],
                            AF.Exp, bias=negc[:, 0:1]))
                    if kc == t:
                        with b.task("dve", reads=[tri_r, ptb_r], writes=[ptb_r]) as tk:
                            tk.emit(lambda e, slot=slot: e.tensor_tensor(
                                ptb[:, slot * 128:(slot + 1) * 128],
                                ptb[:, slot * 128:(slot + 1) * 128], tri_sb[:], ALU.mult))
                    for half in range(2):
                        ph_a, pr_a = aoh[half]
                        with b.task("pe", reads=[ptb_r, vt_r], writes=[pr_a]) as tk:
                            tk.emit(lambda e, kc=kc, slot=slot, half=half, ph_a=ph_a,
                                    t=t: e.matmul(
                                ph_a[:], ptb[:, slot * 128:(slot + 1) * 128],
                                vt[:, kc * 1040 + half * 512: kc * 1040 + half * 512 + 512],
                                start=(kc == 0), stop=(kc == t)))
                    with b.task("pe", reads=[ptb_r, vt_r], writes=[pr_b]) as tk:
                        tk.emit(lambda e, kc=kc, slot=slot, t=t: e.matmul(
                            ph_b[:, 8:9], ptb[:, slot * 128:(slot + 1) * 128],
                            vt[:, kc * 1040 + 1039: kc * 1040 + 1040],
                            start=(kc == 0), stop=(kc == t)))
                with b.task("dve", reads=[pr_b], writes=[negc_r]) as tk:
                    tk.emit(lambda e: e.reciprocal(negc[:, 0:1], ph_b[:, 8:9]))
                gslot = t % 2
                with b.task("sp", reads=[gsp_r], writes=[gbuf_r], dma_target=gbuf_r) as tk:
                    tk.dma(gbuf_sb[:, gslot * D:(gslot + 1) * D], AP_gspill_tile(t), gbuf_r)
                for half in range(2):
                    ph_a, pr_a = aoh[half]
                    with b.task("dve", reads=[pr_a, negc_r, gbuf_r], writes=[tmpf_r]) as tk:
                        tk.emit(lambda e, ph_a=ph_a, half=half, gslot=gslot: e.scalar_tensor_tensor(
                            tmpf[:], ph_a[:], negc[:, 0:1],
                            gbuf_sb[:, gslot * D + half * 512: gslot * D + half * 512 + 512],
                            ALU.mult, ALU.mult))
                    with b.task("dve", reads=[tmpf_r, x_t[t]], writes=[x_t[t]]) as tk:
                        tk.emit(lambda e, t=t, half=half: e.tensor_tensor(
                            x_sb[:, t * D + half * 512: t * D + half * 512 + 512],
                            x_sb[:, t * D + half * 512: t * D + half * 512 + 512],
                            tmpf[:], ALU.add))


        def ffn_generic(lay):
            AR.mark()
            AF32.mark()
            nE = {1: 8, 2: 12, 3: 8}[lay]
            wcomb, wcomb_r = af(f"l{lay}_wcomb", 256)
            ccols, ccols_r = af(f"l{lay}_fc", 64)
            if debug == "w":
                scrdump, scrdump_r = af("dbg_scr", 128)
                rawdump, rawdump_r = af("dbg_raw", 128)
                with b.task("dve", writes=[wcomb_r, scrdump_r, rawdump_r]) as tk:
                    tk.emit(lambda e: e.memset(wcomb[:, 0:256], 0.0))
                    tk.emit(lambda e: e.memset(scrdump[:, 0:128], 0.0))
                    tk.emit(lambda e: e.memset(rawdump[:, 0:128], 0.0))
            if lay == 1:
                gw_raw, gw_raw_r = ab("l1_gwraw", D)
                gwt, gwt_r = ab("l1_gwt", 64)
                eb_bc, eb_bc_r = ab("l1_ebbc", 16)
                ebias_f, ebias_f_r = af("l1_ebias", 16)
                with b.task("dve", writes=[gw_raw_r]) as tk:
                    tk.emit(lambda e: e.memset(gw_raw[:], 0.0))
                with b.task("pool", writes=[gw_raw_r], dma_target=gw_raw_r) as tk:
                    tk.dma(gw_raw[0:8, :], AP["L1_ffn_gate_w"], gw_raw_r)
                with b.task("pe", reads=[gw_raw_r, eye_r], writes=[pbf_r]) as tk:
                    for c in range(NC):
                        tk.emit(lambda e, c=c: e.transpose(
                            pbf_h[:, c * 128:(c + 1) * 128],
                            gw_raw[:, c * 128:(c + 1) * 128], eye_sb[:]))
                with b.task("dve", reads=[pbf_r], writes=[gwt_r]) as tk:
                    src = pbf_h.ap().rearrange("p (c n) -> p c n", c=NC)[:, :, 0:8]
                    dst = gwt.rearrange("p (c n) -> p c n", n=8)
                    tk.emit(lambda e, src=src, dst=dst: e.tensor_copy(dst, src))
                ebrow, ebrow_r = ab("l1_ebrow", 16)
                with b.task("dve", writes=[ebrow_r]) as tk:
                    tk.emit(lambda e: e.memset(ebrow[:], 0.0))
                with b.task("pool", writes=[ebrow_r], dma_target=ebrow_r) as tk:
                    tk.dma(ebrow[0:1, 0:8],
                           AP["L1_ffn_expert_bias"].rearrange("(o d) -> o d", o=1), ebrow_r)
                ph_t, pr_t = PF[6]
                with b.task("pe", reads=[ebrow_r, ones_r], writes=[pr_t]) as tk:
                    tk.emit(lambda e: e.matmul(ph_t[:, 0:8], ones_sb[0:1, 0:128],
                                               ebrow[0:1, 0:8], start=True, stop=True))
                with b.task("dve", reads=[pr_t], writes=[ebias_f_r]) as tk:
                    tk.emit(lambda e: e.tensor_copy(ebias_f[:, 0:8], ph_t[:, 0:8]))
            elif lay == 2:
                wr_bf, wr_bf_r = ab("l2_wr", NC * 16)
                br_pad, br_pad_r = ab("l2_brpad", 16)
                with b.task("pool", writes=[wr_bf_r], dma_target=wr_bf_r) as tk:
                    W = AP["L2_ffn_Wr"].rearrange("(c p) e -> c p e", p=128)
                    for c in range(NC):
                        tk.dma(wr_bf[:, c * 16:c * 16 + 12], W[c], wr_bf_r)
                with b.task("dve", writes=[br_pad_r]) as tk:
                    tk.emit(lambda e: e.memset(br_pad[:], 0.0))
                with b.task("pool", writes=[br_pad_r], dma_target=br_pad_r) as tk:
                    tk.dma(br_pad[0:1, 0:12],
                           AP["L2_ffn_br"].rearrange("(o d) -> o d", o=1), br_pad_r)
            else:
                wg_bf, wg_bf_r = ab("l3_wg", NC * 16)
                with b.task("pool", writes=[wg_bf_r], dma_target=wg_bf_r) as tk:
                    W = AP["L3_ffn_Wg"].rearrange("(c p) e -> c p e", p=128)
                    for c in range(NC):
                        tk.dma(wg_bf[:, c * 16:c * 16 + 8], W[c], wg_bf_r)

            for t in range(NT):
                ph_g, pr_g = PF[4 + (t % 2)]
                ne = nE if lay != 2 else 12
                with b.task("pe", reads=[xnT_r] + (
                        [gwt_r] if lay == 1 else [wr_bf_r, br_pad_r, ones_r] if lay == 2
                        else [wg_bf_r]), writes=[pr_g]) as tk:
                    if lay == 1:
                        pairs = [(xnT_sb[:, c * S + t * 128:c * S + (t + 1) * 128],
                                  gwt[:, c * 8:(c + 1) * 8]) for c in range(NC)]
                        mm_group(tk, ph_g[:, 0:8], pairs)
                    elif lay == 2:
                        pairs = [(xnT_sb[:, c * S + t * 128:c * S + (t + 1) * 128],
                                  wr_bf[:, c * 16:c * 16 + 12]) for c in range(NC)]
                        mm_group(tk, ph_g[:, 0:12], pairs,
                                 extra_first=(ones_sb[:, 0:128], br_pad[:, 0:12]))
                    else:
                        pairs = [(xnT_sb[:, c * S + t * 128:c * S + (t + 1) * 128],
                                  wg_bf[:, c * 16:c * 16 + 8]) for c in range(NC)]
                        mm_group(tk, ph_g[:, 0:8], pairs)
                wc = wcomb[:, 16 * t:16 * t + ne]
                cc = ccols
                if lay == 2:
                    with b.task("act", reads=[pr_g], writes=[ccols_r]) as tk:
                        tk.emit(lambda e, ph_g=ph_g: e.activation(
                            cc[:, 16:28], ph_g[:, 0:12], AF.Identity, scale=1.0 / 1.5))
                    with b.task("dve", reads=[ccols_r], writes=[ccols_r]) as tk:
                        tk.emit(lambda e: e.tensor_reduce(cc[:, 0:1], cc[:, 16:28], AX.X, ALU.max))
                    with b.task("dve", reads=[ccols_r], writes=[ccols_r]) as tk:
                        tk.emit(lambda e: e.tensor_scalar(cc[:, 32:44], cc[:, 16:28], cc[:, 0:1],
                                                          None, ALU.is_ge))
                    with b.task("dve", reads=[ccols_r], writes=[ccols_r]) as tk:
                        tk.emit(lambda e: e.scalar_tensor_tensor(
                            cc[:, 48:60], cc[:, 32:44], -1e30, cc[:, 16:28], ALU.mult, ALU.add))
                    with b.task("dve", reads=[ccols_r], writes=[ccols_r]) as tk:
                        tk.emit(lambda e: e.tensor_reduce(cc[:, 1:2], cc[:, 48:60], AX.X, ALU.max))
                    with b.task("dve", reads=[ccols_r], writes=[ccols_r]) as tk:
                        tk.emit(lambda e: e.tensor_scalar(cc[:, 48:60], cc[:, 48:60], cc[:, 1:2],
                                                          None, ALU.is_ge))
                    with b.task("dve", reads=[ccols_r], writes=[ccols_r]) as tk:
                        tk.emit(lambda e: e.tensor_tensor(cc[:, 2:3], cc[:, 1:2], cc[:, 0:1],
                                                          ALU.subtract))
                    with b.task("act", reads=[ccols_r], writes=[ccols_r]) as tk:
                        tk.emit(lambda e: e.activation(cc[:, 3:4], cc[:, 2:3], AF.Exp))
                    with b.task("dve", reads=[ccols_r], writes=[ccols_r]) as tk:
                        tk.emit(lambda e: e.tensor_scalar(cc[:, 4:5], cc[:, 3:4], 1.0, None,
                                                          ALU.add))
                    with b.task("dve", reads=[ccols_r], writes=[ccols_r]) as tk:
                        tk.emit(lambda e: e.reciprocal(cc[:, 5:6], cc[:, 4:5]))
                    with b.task("dve", reads=[ccols_r], writes=[ccols_r]) as tk:
                        tk.emit(lambda e: e.tensor_tensor(cc[:, 6:7], cc[:, 3:4], cc[:, 5:6],
                                                          ALU.mult))
                    with b.task("dve", reads=[ccols_r], writes=[ccols_r]) as tk:
                        tk.emit(lambda e: e.tensor_scalar(cc[:, 32:44], cc[:, 32:44], cc[:, 5:6],
                                                          None, ALU.mult))
                    with b.task("dve", reads=[ccols_r], writes=[wcomb_r]) as tk:
                        tk.emit(lambda e, wc=wc: e.scalar_tensor_tensor(
                            wc, cc[:, 48:60], cc[:, 6:7], cc[:, 32:44], ALU.mult, ALU.add))
                else:
                    with b.task("act", reads=[pr_g], writes=[ccols_r]) as tk:
                        tk.emit(lambda e, ne=ne, ph_g=ph_g: e.activation(
                            cc[:, 16:16 + ne], ph_g[:, 0:ne], AF.Sigmoid))
                    if debug == "w":
                        with b.task("dve", reads=[ccols_r], writes=[scrdump_r]) as tk:
                            tk.emit(lambda e, t=t: e.tensor_copy(
                                scrdump[:, 8 * t:8 * t + 8], cc[:, 16:24]))
                        with b.task("dve", reads=[pr_g], writes=[rawdump_r]) as tk:
                            tk.emit(lambda e, t=t, ph_g=ph_g: e.tensor_copy(
                                rawdump[:, 8 * t:8 * t + 8], ph_g[:, 0:8]))
                    if lay == 1:
                        with b.task("dve", reads=[ccols_r, ebias_f_r], writes=[ccols_r]) as tk:
                            tk.emit(lambda e: e.tensor_tensor(cc[:, 32:40], cc[:, 16:24],
                                                              ebias_f[:, 0:8], ALU.add))
                        sel_src = (32, 40)
                    else:
                        with b.task("dve", reads=[ccols_r], writes=[ccols_r]) as tk:
                            tk.emit(lambda e: e.tensor_copy(cc[:, 32:40], cc[:, 16:24]))
                        sel_src = (32, 40)
                    a0, a1 = sel_src
                    with b.task("dve", reads=[ccols_r], writes=[ccols_r]) as tk:
                        tk.emit(lambda e: e.tensor_reduce(cc[:, 0:1], cc[:, a0:a1], AX.X, ALU.max))
                    with b.task("dve", reads=[ccols_r], writes=[ccols_r]) as tk:
                        tk.emit(lambda e: e.tensor_scalar(cc[:, 48:56], cc[:, a0:a1], cc[:, 0:1],
                                                          None, ALU.is_ge))
                    with b.task("dve", reads=[ccols_r], writes=[ccols_r]) as tk:
                        tk.emit(lambda e: e.scalar_tensor_tensor(
                            cc[:, 56:64], cc[:, 48:56], -1e30, cc[:, a0:a1], ALU.mult, ALU.add))
                    with b.task("dve", reads=[ccols_r], writes=[ccols_r]) as tk:
                        tk.emit(lambda e: e.tensor_reduce(cc[:, 1:2], cc[:, 56:64], AX.X, ALU.max))
                    with b.task("dve", reads=[ccols_r], writes=[ccols_r]) as tk:
                        tk.emit(lambda e: e.tensor_scalar(cc[:, 56:64], cc[:, 56:64], cc[:, 1:2],
                                                          None, ALU.is_ge))
                    with b.task("dve", reads=[ccols_r], writes=[ccols_r]) as tk:
                        tk.emit(lambda e: e.tensor_tensor(cc[:, 48:56], cc[:, 48:56],
                                                          cc[:, 56:64], ALU.add))
                    if lay == 1:
                        with b.task("dve", reads=[ccols_r], writes=[ccols_r]) as tk:
                            tk.emit(lambda e: e.tensor_tensor(cc[:, 56:64], cc[:, 16:24],
                                                              cc[:, 16:24], ALU.mult))
                        with b.task("dve", reads=[ccols_r], writes=[wcomb_r]) as tk:
                            tk.emit(lambda e, wc=wc: e.tensor_tensor(
                                wc, cc[:, 48:56], cc[:, 56:64], ALU.mult))
                    else:
                        with b.task("dve", reads=[ccols_r], writes=[ccols_r]) as tk:
                            tk.emit(lambda e: e.tensor_tensor(cc[:, 2:3], cc[:, 0:1],
                                                              cc[:, 1:2], ALU.add))
                        with b.task("dve", reads=[ccols_r], writes=[ccols_r]) as tk:
                            tk.emit(lambda e: e.reciprocal(cc[:, 3:4], cc[:, 2:3]))
                        with b.task("dve", reads=[ccols_r], writes=[ccols_r]) as tk:
                            tk.emit(lambda e: e.tensor_tensor(cc[:, 56:64], cc[:, 48:56],
                                                              cc[:, 16:24], ALU.mult))
                        with b.task("dve", reads=[ccols_r], writes=[wcomb_r]) as tk:
                            tk.emit(lambda e, wc=wc: e.tensor_scalar(
                                wc, cc[:, 56:64], cc[:, 3:4], 1.0 / 3.0, ALU.mult, ALU.mult))
            if debug == "w":
                with b.task("sp", reads=[wcomb_r, scrdump_r, rawdump_r],
                            writes=[out_r], dma_target=out_r) as tk:
                    tk.dma(out[0:128, 0:256], wcomb[:, 0:256], out_r)
                    tk.dma(out[0:128, 256:384], scrdump[:, 0:128], out_r)
                    tk.dma(out[0:128, 384:512], rawdump[:, 0:128], out_r)
                return
            if lay == 1:
                AR.reset_to_mark()
                AR.mark()

            if lay == 1:
                ffn_l1_body(wcomb, wcomb_r)
            elif lay == 2:
                ffn_l2_body(wcomb, wcomb_r)
            else:
                ffn_l3_body(wcomb, wcomb_r)
            AR.reset_to_mark()

            AR.mark()
            gt2, gt2_r = ab(f"l{lay}_gsil", 2 * D)
            for t in range(NT):
                slot = t % 2
                with b.task("sp", reads=[gsp_r], writes=[gt2_r], dma_target=gt2_r) as tk:
                    tk.dma(gt2[:, slot * D:(slot + 1) * D], AP_gspill_tile(t), gt2_r)
                with b.task("act", reads=[gt2_r], writes=[gt2_r]) as tk:
                    tk.emit(lambda e, slot=slot: e.activation(
                        gt2[:, slot * D:(slot + 1) * D], gt2[:, slot * D:(slot + 1) * D],
                        AF.Silu))
                with b.task("sp", reads=[gt2_r], writes=[gsp_r], dma_target=gsp_r) as tk:
                    tk.dma(AP_gspill_tile(t), gt2[:, slot * D:(slot + 1) * D], gsp_r)
            AR.reset_to_mark()
            AF32.reset_to_mark()

        def stream_coltiles(tk, name_ap, slot, slot_r, d_rows, col0, ncols):
            for c in range(NC):
                tk.dma(slot[:, c * ncols:(c + 1) * ncols],
                       name_ap[c * 128:(c + 1) * 128, col0:col0 + ncols], slot_r)

        def ffn_l1_body(wcomb, wcomb_r):
            HID = 1408
            NHC = 11
            wup_t, wup_t_r = ab("l1_wupT", NC * HID)
            wdn_t, wdn_t_r = ab("l1_wdnT", NHC * D)
            raw, raw_r = ab("l1_raw", 2 * HID)
            act_s, act_s_r = ab("l1_act", NHC * 512)
            gwin, gwin_r = ab("l1_gwin", 4 * D)

            def transpose_torch2(ap_mat, src_r, rows, cols, dst, dst_r, dst_stride):
                nrc = rows // 128
                ncc = cols // 128
                for rc in range(nrc):
                    slot = rc % 2
                    with b.task("sp", reads=[src_r], writes=[raw_r],
                                dma_target=raw_r) as tk:
                        tk.dma(raw[:, slot * HID:slot * HID + cols],
                               ap_mat[rc * 128:(rc + 1) * 128, :], raw_r)
                    for g0 in range(0, ncc, 8):
                        g1 = min(g0 + 8, ncc)
                        with b.task("pe", reads=[raw_r, eye_r], writes=[pbf_r]) as tk:
                            for cc in range(g0, g1):
                                tk.emit(lambda e, cc=cc, slot=slot: e.transpose(
                                    pbf_h[:, (cc % 8) * 128:(cc % 8) * 128 + 128],
                                    raw[:, slot * HID + cc * 128:slot * HID + (cc + 1) * 128],
                                    eye_sb[:]))
                        with b.task("dve", reads=[pbf_r], writes=[dst_r]) as tk:
                            for cc in range(g0, g1):
                                tk.emit(lambda e, cc=cc, rc=rc: e.tensor_copy(
                                    dst[:, cc * dst_stride + rc * 128:
                                        cc * dst_stride + (rc + 1) * 128],
                                    pbf_h[:, (cc % 8) * 128:(cc % 8) * 128 + 128]))

            def one_unit(eidx, wup_ap, wup_res, wdn_ap, wdn_res, write_mode):
                transpose_torch2(wup_ap, wup_res, HID, D, wup_t, wup_t_r, HID)
                transpose_torch2(wdn_ap, wdn_res, D, HID, wdn_t, wdn_t_r, D)
                for ts4 in range(4):
                    for hc in range(NHC):
                        ph_u, pr_u = PF[hc % 4]
                        with b.task("pe", reads=[wup_t_r, xnT_r], writes=[pr_u]) as tk:
                            pairs = [(wup_t[:, c * HID + hc * 128: c * HID + (hc + 1) * 128],
                                      xnT_sb[:, c * S + ts4 * 512: c * S + ts4 * 512 + 512])
                                     for c in range(NC)]
                            mm_group(tk, ph_u[:], pairs)
                        with b.task("act", reads=[pr_u], writes=[act_s_r]) as tk:
                            tk.emit(lambda e, ph_u=ph_u, hc=hc: e.activation(
                                act_s[:, hc * 512:(hc + 1) * 512], ph_u[:], AF.Relu))
                        with b.task("dve", reads=[act_s_r], writes=[act_s_r]) as tk:
                            tk.emit(lambda e, hc=hc: e.tensor_tensor(
                                act_s[:, hc * 512:(hc + 1) * 512],
                                act_s[:, hc * 512:(hc + 1) * 512],
                                act_s[:, hc * 512:(hc + 1) * 512], ALU.mult))
                    if not write_mode:
                        with b.task("sp", reads=[gsp_r], writes=[gwin_r],
                                    dma_target=gwin_r) as tk:
                            for tt in range(4):
                                tk.dma(gwin[:, tt * D:(tt + 1) * D],
                                       AP_gspill_tile(ts4 * 4 + tt), gwin_r)
                    for oh in range(2):
                        for tt in range(4):
                            ph_y, pr_y = PF[2 + tt]
                            with b.task("pe", reads=[wdn_t_r, act_s_r], writes=[pr_y]) as tk:
                                pairs = []
                                for hc in range(NHC):
                                    pairs.append(
                                        (act_s[:, hc * 512 + tt * 128: hc * 512 + (tt + 1) * 128],
                                         wdn_t[:, hc * D + oh * 512: hc * D + oh * 512 + 512]))
                                mm_group(tk, ph_y[:], pairs)
                            wcol = (wcomb[:, 16 * (ts4 * 4 + tt) + eidx:
                                          16 * (ts4 * 4 + tt) + eidx + 1]
                                    if eidx is not None else None)
                            with b.task("dve",
                                        reads=[pr_y, wcomb_r, gwin_r],
                                        writes=[gwin_r]) as tk:
                                gsl = gwin[:, tt * D + oh * 512: tt * D + oh * 512 + 512]
                                if write_mode:
                                    tk.emit(lambda e, ph_y=ph_y, gsl=gsl: e.tensor_scalar(
                                        gsl, ph_y[:], 1.0, None, ALU.mult))
                                else:
                                    tk.emit(lambda e, ph_y=ph_y, gsl=gsl, wcol=wcol:
                                            e.scalar_tensor_tensor(
                                                gsl, ph_y[:], wcol, gsl, ALU.mult, ALU.add))
                    with b.task("sp", reads=[gwin_r], writes=[gsp_r],
                                dma_target=gsp_r) as tk:
                        for tt in range(4):
                            tk.dma(AP_gspill_tile(ts4 * 4 + tt),
                                   gwin[:, tt * D:(tt + 1) * D], gsp_r)

            one_unit(None, WBF["L1_ffn_shared_w_up"], WBF_R["L1_ffn_shared_w_up"],
                     WBF["L1_ffn_shared_w_down"], WBF_R["L1_ffn_shared_w_down"], True)
            for eidx in range(8):
                one_unit(eidx,
                         WBF["L1_ffn_w_up"][eidx * 1408:(eidx + 1) * 1408, :],
                         WBF_R["L1_ffn_w_up"],
                         WBF["L1_ffn_w_down"][eidx * D:(eidx + 1) * D, :],
                         WBF_R["L1_ffn_w_down"],
                         False)

        def ffn_l2_body(wcomb, wcomb_r):
            HS, HR = 2048, 3072
            bs2bc, bs2bc_r = ab("l2_bs2bc", D)
            biasrow, biasrow_r = ab("l2_biasrow", 2 * HR)
            gelt, gelt_r = ab("l2_gel", 512)
            gwin, gwin_r = ab("l2_gwin", 4 * D)
            with b.task("dve", writes=[biasrow_r]) as tk:
                tk.emit(lambda e: e.memset(biasrow[:], 0.0))
            with b.task("pool", writes=[biasrow_r], dma_target=biasrow_r) as tk:
                tk.dma(biasrow[0:1, 0:D],
                       AP["L2_ffn_bs2"].rearrange("(o d) -> o d", o=1), biasrow_r)
            ph_t, pr_t = PF[6]
            for half in range(2):
                with b.task("pe", reads=[biasrow_r, ones_r], writes=[pr_t]) as tk:
                    tk.emit(lambda e, half=half: e.matmul(
                        ph_t[:], ones_sb[0:1, 0:128],
                        biasrow[0:1, half * 512:(half + 1) * 512], start=True, stop=True))
                with b.task("dve", reads=[pr_t], writes=[bs2bc_r]) as tk:
                    tk.emit(lambda e, half=half: e.tensor_copy(
                        bs2bc[:, half * 512:(half + 1) * 512], ph_t[:]))

            AR.mark()
            wtp, wtp_r = ab("l2_wtp", NT * 128)
            be2p, be2p_r = ab("l2_be2p", D)
            with b.task("dve", writes=[wtp_r]) as tk:
                tk.emit(lambda e: e.memset(wtp[:], 0.0))
            with b.task("dve", writes=[be2p_r]) as tk:
                tk.emit(lambda e: e.memset(be2p[:], 0.0))
            with b.task("pool", writes=[be2p_r], dma_target=be2p_r) as tk:
                tk.dma(be2p[0:12, :], AP["L2_ffn_be2"], be2p_r)
            for t in range(NT):
                with b.task("pe", reads=[wcomb_r, eye32_r], writes=[pr_t]) as tk:
                    tk.emit(lambda e, t=t: e.transpose(
                        ph_t[0:12, 0:128], wcomb[:, 16 * t:16 * t + 12], eye_f32[:]))
                with b.task("dve", reads=[pr_t], writes=[wtp_r]) as tk:
                    tk.emit(lambda e, t=t: e.tensor_copy(
                        wtp[0:12, t * 128:(t + 1) * 128], ph_t[0:12, 0:128]))
            for t in range(NT):
                for oh in range(2):
                    ph_y, pr_y = PF[2 + oh]
                    with b.task("pe", reads=[wtp_r, be2p_r], writes=[pr_y]) as tk:
                        tk.emit(lambda e, t=t, oh=oh, ph_y=ph_y: e.matmul(
                            ph_y[:], wtp[:, t * 128:(t + 1) * 128],
                            be2p[:, oh * 512:(oh + 1) * 512], start=True, stop=True))
                    with b.task("dve", reads=[pr_y, bs2bc_r], writes=[gwin_r]) as tk:
                        tk.emit(lambda e, ph_y=ph_y, oh=oh: e.tensor_tensor(
                            gwin[:, oh * 512:(oh + 1) * 512],
                            bs2bc[:, oh * 512:(oh + 1) * 512], ph_y[:], ALU.add))
                with b.task("sp", reads=[gwin_r], writes=[gsp_r], dma_target=gsp_r) as tk:
                    tk.dma(AP_gspill_tile(t), gwin[:, 0:D], gsp_r)
            AR.reset_to_mark()

            we1s, we1s_r = ab("l2_we1s", 2 * NC * 512)
            act_s, act_s_r = ab("l2_act", 24 * D)
            we2s, we2s_r = ab("l2_we2s", 2 * 4 * 512)

            def geglu_unit(eidx, W1ap, W1res, W2ap, W2res, hid2, has_bias):
                nhc = hid2 // 128
                nh5 = hid2 // 512
                for th in range(2):
                    for part in range(2):
                        for h5 in range(nh5):
                            buf = (h5 % 2) * NC * 512
                            with b.task("sp", reads=[W1res], writes=[we1s_r],
                                        dma_target=we1s_r) as tk:
                                stream_coltiles(tk, W1ap, we1s[:, buf:buf + NC * 512],
                                                we1s_r, D, part * hid2 + h5 * 512, 512)
                            for hc4 in range(4):
                                hc = h5 * 4 + hc4
                                for t5 in range(2):
                                    ph_a, pr_a = PF[(hc4 * 2 + t5) % 4]
                                    ef = None
                                    if has_bias:
                                        ef = (biasrow[:, part * hid2 + hc * 128:
                                                      part * hid2 + (hc + 1) * 128],
                                              ones_sb[:])
                                    with b.task("pe",
                                                reads=[we1s_r, xnT_r, biasrow_r, ones_r],
                                                writes=[pr_a]) as tk:
                                        pairs = [(we1s[:, buf + c * 512 + hc4 * 128:
                                                       buf + c * 512 + (hc4 + 1) * 128],
                                                  xnT_sb[:, c * S + th * 1024 + t5 * 512:
                                                         c * S + th * 1024 + t5 * 512 + 512])
                                                 for c in range(NC)]
                                        mm_group(tk, ph_a[:], pairs, extra_first=ef)
                                    if part == 0:
                                        with b.task("dve", reads=[pr_a],
                                                    writes=[act_s_r]) as tk:
                                            tk.emit(lambda e, ph_a=ph_a, hc=hc, t5=t5:
                                                    e.tensor_copy(
                                                        act_s[:, hc * D + t5 * 512:
                                                              hc * D + t5 * 512 + 512],
                                                        ph_a[:]))
                                    else:
                                        with b.task("act", reads=[pr_a],
                                                    writes=[gelt_r]) as tk:
                                            tk.emit(lambda e, ph_a=ph_a: e.activation(
                                                gelt[:], ph_a[:], AF.Gelu))
                                        with b.task("dve", reads=[gelt_r, act_s_r],
                                                    writes=[act_s_r]) as tk:
                                            tk.emit(lambda e, hc=hc, t5=t5: e.tensor_tensor(
                                                act_s[:, hc * D + t5 * 512:
                                                      hc * D + t5 * 512 + 512],
                                                act_s[:, hc * D + t5 * 512:
                                                      hc * D + t5 * 512 + 512],
                                                gelt[:], ALU.mult))
                    for tq in range(2):
                        with b.task("sp", reads=[gsp_r], writes=[gwin_r],
                                    dma_target=gwin_r) as tk:
                            for tt in range(4):
                                tk.dma(gwin[:, tt * D:(tt + 1) * D],
                                       AP_gspill_tile(th * 8 + tq * 4 + tt), gwin_r)
                        for oh in range(2):
                            ngrp = nhc // 4
                            for g4 in range(ngrp):
                                h0 = g4 * 4
                                buf2 = (g4 % 2) * 4 * 512
                                with b.task("sp", reads=[W2res], writes=[we2s_r],
                                            dma_target=we2s_r) as tk:
                                    tk.dma(we2s[:, buf2:buf2 + 4 * 512].rearrange(
                                               "p (h o) -> p h o", o=512),
                                           W2ap[h0 * 128:(h0 + 4) * 128,
                                                oh * 512:(oh + 1) * 512].rearrange(
                                               "(h p) o -> p h o", p=128),
                                           we2s_r)
                                for tt in range(4):
                                    ph_y, pr_y = PF[2 + tt]
                                    with b.task("pe", reads=[we2s_r, act_s_r],
                                                writes=[pr_y]) as tk:
                                        pairs = [(act_s[:, hcc * D + tq * 512 + tt * 128:
                                                        hcc * D + tq * 512 + (tt + 1) * 128],
                                                  we2s[:, buf2 + (hcc - h0) * 512:
                                                       buf2 + (hcc - h0 + 1) * 512])
                                                 for hcc in range(h0, h0 + 4)]
                                        mm_group(tk, ph_y[:], pairs,
                                                 first_start=(g4 == 0),
                                                 stop=(g4 == ngrp - 1))
                            for tt in range(4):
                                ph_y, pr_y = PF[2 + tt]
                                ti = th * 8 + tq * 4 + tt
                                gsl = gwin[:, tt * D + oh * 512: tt * D + oh * 512 + 512]
                                if eidx is None:
                                    with b.task("dve", reads=[pr_y, gwin_r],
                                                writes=[gwin_r]) as tk:
                                        tk.emit(lambda e, ph_y=ph_y, gsl=gsl:
                                                e.scalar_tensor_tensor(
                                                    gsl, ph_y[:], 1.0, gsl,
                                                    ALU.mult, ALU.add))
                                else:
                                    wcol = wcomb[:, 16 * ti + eidx:16 * ti + eidx + 1]
                                    with b.task("dve", reads=[pr_y, wcomb_r, gwin_r],
                                                writes=[gwin_r]) as tk:
                                        tk.emit(lambda e, ph_y=ph_y, gsl=gsl, wcol=wcol:
                                                e.scalar_tensor_tensor(
                                                    gsl, ph_y[:], wcol, gsl,
                                                    ALU.mult, ALU.add))
                        with b.task("sp", reads=[gwin_r], writes=[gsp_r],
                                    dma_target=gsp_r) as tk:
                            for tt in range(4):
                                tk.dma(AP_gspill_tile(th * 8 + tq * 4 + tt),
                                       gwin[:, tt * D:(tt + 1) * D], gsp_r)

            with b.task("dve", writes=[biasrow_r]) as tk:
                tk.emit(lambda e: e.memset(biasrow[:], 0.0))
            with b.task("pool", writes=[biasrow_r], dma_target=biasrow_r) as tk:
                tk.dma(biasrow[0:1, 0:2 * HS],
                       AP["L2_ffn_bs1"].rearrange("(o d) -> o d", o=1), biasrow_r)
            geglu_unit(None, WBF["L2_ffn_Ws1"], WBF_R["L2_ffn_Ws1"],
                       WBF["L2_ffn_Ws2"], WBF_R["L2_ffn_Ws2"], HS, True)
            We1 = AP["L2_ffn_We1"]
            We2 = AP["L2_ffn_We2"]
            be1 = AP["L2_ffn_be1"]
            for eidx in range(12):
                with b.task("pool", writes=[biasrow_r], dma_target=biasrow_r) as tk:
                    tk.dma(biasrow[0:1, 0:2 * HR], be1[eidx:eidx + 1, :], biasrow_r)
                geglu_unit(eidx, WBF["L2_ffn_We1"][eidx * D:(eidx + 1) * D, :],
                           WBF_R["L2_ffn_We1"],
                           WBF["L2_ffn_We2"][eidx * 3072:(eidx + 1) * 3072, :],
                           WBF_R["L2_ffn_We2"], HR, True)

        def ffn_l3_body(wcomb, wcomb_r):
            FF = 2048
            nhc = FF // 128
            w13s, w13s_r = ab("l3_w13s", 2 * 2 * NC * 512)
            act_s, act_s_r = ab("l3_act", 16 * D)
            we2s, we2s_r = ab("l3_we2s", 2 * 8 * 512)
            gwin, gwin_r = ab("l3_gwin", 4 * D)
            silt, silt_r = ab("l3_sil", 512)

            def swiglu_unit(eidx, W1ap, W3ap, W2ap, w_res, write_mode):
                for th in range(2):
                    for h5 in range(FF // 512):
                        buf = (h5 % 2) * 2 * NC * 512
                        with b.task("sp", reads=w_res, writes=[w13s_r],
                                    dma_target=w13s_r) as tk:
                            stream_coltiles(tk, W1ap, w13s[:, buf:buf + NC * 512],
                                            w13s_r, D, h5 * 512, 512)
                            stream_coltiles(tk, W3ap,
                                            w13s[:, buf + NC * 512:buf + 2 * NC * 512],
                                            w13s_r, D, h5 * 512, 512)
                        for hc4 in range(4):
                            hc = h5 * 4 + hc4
                            for t5 in range(2):
                                rot = (hc4 * 2 + t5) % 2
                                ph_a, pr_a = PF[2 * rot]
                                ph_g, pr_g = PF[2 * rot + 1]
                                with b.task("pe", reads=[w13s_r, xnT_r],
                                            writes=[pr_a]) as tk:
                                    pairs = [(w13s[:, buf + c * 512 + hc4 * 128:
                                                   buf + c * 512 + (hc4 + 1) * 128],
                                              xnT_sb[:, c * S + th * 1024 + t5 * 512:
                                                     c * S + th * 1024 + t5 * 512 + 512])
                                             for c in range(NC)]
                                    mm_group(tk, ph_a[:], pairs)
                                with b.task("pe", reads=[w13s_r, xnT_r],
                                            writes=[pr_g]) as tk:
                                    pairs = [(w13s[:, buf + NC * 512 + c * 512 + hc4 * 128:
                                                   buf + NC * 512 + c * 512 + (hc4 + 1) * 128],
                                              xnT_sb[:, c * S + th * 1024 + t5 * 512:
                                                     c * S + th * 1024 + t5 * 512 + 512])
                                             for c in range(NC)]
                                    mm_group(tk, ph_g[:], pairs)
                                with b.task("act", reads=[pr_a], writes=[silt_r]) as tk:
                                    tk.emit(lambda e, ph_a=ph_a: e.activation(
                                        silt[:], ph_a[:], AF.Silu))
                                with b.task("dve", reads=[silt_r, pr_g],
                                            writes=[act_s_r]) as tk:
                                    tk.emit(lambda e, ph_g=ph_g, hc=hc, t5=t5: e.tensor_tensor(
                                        act_s[:, hc * D + t5 * 512: hc * D + t5 * 512 + 512],
                                        ph_g[:], silt[:], ALU.mult))
                    for tq in range(2):
                        if not write_mode:
                            with b.task("sp", reads=[gsp_r], writes=[gwin_r],
                                        dma_target=gwin_r) as tk:
                                for tt in range(4):
                                    tk.dma(gwin[:, tt * D:(tt + 1) * D],
                                           AP_gspill_tile(th * 8 + tq * 4 + tt), gwin_r)
                        for oh in range(2):
                            for g8 in range(2):
                                h0 = g8 * 8
                                buf2 = (g8 % 2) * 8 * 512
                                with b.task("sp", reads=w_res, writes=[we2s_r],
                                            dma_target=we2s_r) as tk:
                                    tk.dma(we2s[:, buf2:buf2 + 8 * 512].rearrange(
                                               "p (h o) -> p h o", o=512),
                                           W2ap[h0 * 128:(h0 + 8) * 128,
                                                oh * 512:(oh + 1) * 512].rearrange(
                                               "(h p) o -> p h o", p=128),
                                           we2s_r)
                                for tt in range(4):
                                    ph_y, pr_y = PF[2 + tt]
                                    with b.task("pe", reads=[we2s_r, act_s_r],
                                                writes=[pr_y]) as tk:
                                        pairs = [(act_s[:, hcc * D + tq * 512 + tt * 128:
                                                        hcc * D + tq * 512 + (tt + 1) * 128],
                                                  we2s[:, buf2 + (hcc - h0) * 512:
                                                       buf2 + (hcc - h0 + 1) * 512])
                                                 for hcc in range(h0, h0 + 8)]
                                        mm_group(tk, ph_y[:], pairs,
                                                 first_start=(g8 == 0),
                                                 stop=(g8 == 1))
                            for tt in range(4):
                                ph_y, pr_y = PF[2 + tt]
                                ti = th * 8 + tq * 4 + tt
                                gsl = gwin[:, tt * D + oh * 512: tt * D + oh * 512 + 512]
                                wcol = (wcomb[:, 16 * ti + eidx:16 * ti + eidx + 1]
                                        if eidx is not None else None)
                                with b.task("dve", reads=[pr_y, wcomb_r, gwin_r],
                                            writes=[gwin_r]) as tk:
                                    if write_mode:
                                        tk.emit(lambda e, ph_y=ph_y, gsl=gsl: e.tensor_scalar(
                                            gsl, ph_y[:], 1.0 / 3.0, None, ALU.mult))
                                    else:
                                        tk.emit(lambda e, ph_y=ph_y, gsl=gsl, wcol=wcol:
                                                e.scalar_tensor_tensor(
                                                    gsl, ph_y[:], wcol, gsl,
                                                    ALU.mult, ALU.add))
                        with b.task("sp", reads=[gwin_r], writes=[gsp_r],
                                    dma_target=gsp_r) as tk:
                            for tt in range(4):
                                tk.dma(AP_gspill_tile(th * 8 + tq * 4 + tt),
                                       gwin[:, tt * D:(tt + 1) * D], gsp_r)

            swiglu_unit(None, WBF["L3_ffn_Ws1"], WBF["L3_ffn_Ws3"],
                        WBF["L3_ffn_Ws2"],
                        [WBF_R["L3_ffn_Ws1"], WBF_R["L3_ffn_Ws3"],
                         WBF_R["L3_ffn_Ws2"]], True)
            for eidx in range(8):
                swiglu_unit(eidx,
                            WBF["L3_ffn_We1"][eidx * D:(eidx + 1) * D, :],
                            WBF["L3_ffn_We3"][eidx * D:(eidx + 1) * D, :],
                            WBF["L3_ffn_We2"][eidx * 2048:(eidx + 1) * 2048, :],
                            [WBF_R["L3_ffn_We1"], WBF_R["L3_ffn_We3"],
                             WBF_R["L3_ffn_We2"]], False)

        def attn_generic(lay):
            if lay == 1:
                Hs, hd, hpc, Wq = 32, 32, 3, 256
                wqkv_name, wproj_name = "L1_attn_Wqkv", "L1_attn_Wproj"
                causal = False
            else:
                Hs, hd, hpc, Wq = 16, 64, 2, 512
                wqkv_name = f"L{lay}_attn_w_qkv"
                wproj_name = "L2_attn_w_out" if lay == 2 else "L3_attn_w_proj"
                causal = (lay == 3)
            nkc = (Hs + hpc - 1) // hpc
            nsub = Wq // 128
            nqs = S // Wq
            scale = 1.0 / math.sqrt(hd)
            Wqkv = AP[wqkv_name]
            NV = 16

            def chunk_of(j):
                return j // hpc, (j % hpc) * hd

            vt, vt_r = ab(f"l{lay}_vt", NT * (NV * 65))
            AR.mark()
            wv_s, wv_s_r = ab(f"l{lay}_wvs", NC * D)
            WqkvB = WBF[wqkv_name]
            WqkvB_r = WBF_R[wqkv_name]
            with b.task("sp", reads=[WqkvB_r], writes=[wv_s_r],
                        dma_target=wv_s_r) as tk:
                stream_coltiles(tk, WqkvB, wv_s, wv_s_r, D, 2 * D, D)
            with b.task("dve", writes=[vt_r]) as tk:
                dst = vt.rearrange("p (t n) -> p t n", t=NT)[:, :, 0:NV * 65]
                dst = dst.rearrange("p t (h w) -> p t h w", w=65)[:, :, :, 64:65]
                tk.emit(lambda e, dst=dst: e.memset(dst, 1.0))
            for t in range(NT):
                for half in range(2):
                    ph, pr = PF[2 + half]
                    with b.task("pe", reads=[wv_s_r, xnT_r], writes=[pr]) as tk:
                        pairs = [(xnT_sb[:, c * S + t * 128: c * S + (t + 1) * 128],
                                  wv_s[:, c * D + half * 512: c * D + half * 512 + 512])
                                 for c in range(NC)]
                        mm_group(tk, ph[:], pairs)
                    with b.task("dve", reads=[pr], writes=[vt_r]) as tk:
                        dst = vt[:, t * NV * 65 + half * 8 * 65:
                                 t * NV * 65 + half * 8 * 65 + 8 * 65]
                        dst = dst.rearrange("p (h w) -> p h w", w=65)[:, :, 0:64]
                        src = ph.ap().rearrange("p (h w) -> p h w", w=64)
                        tk.emit(lambda e, dst=dst, src=src: e.tensor_copy(dst, src))
            AR.reset_to_mark()

            kt, kt_r = ab(f"l{lay}_kt", nkc * S)
            wk_s, wk_s_r = ab(f"l{lay}_wks", 2 * NC * 128)
            blko, blko_r = ab(f"l{lay}_blko", 16)
            cbf, cbf_r = ab(f"l{lay}_cbf", 48)
            cbf_row = cbf
            with b.task("dve", writes=[blko_r]) as tk:
                tk.emit(lambda e: e.memset(blko[:], 0.0))
            with b.task("dve", writes=[blko_r]) as tk:
                for jj in range(hpc):
                    tk.emit(lambda e, jj=jj: e.memset(
                        blko[jj * hd:(jj + 1) * hd, jj:jj + 1], 1.0))
            for m in range(nkc):
                width = min(hpc, Hs - m * hpc) * hd
                buf = (m % 2) * NC * 128
                with b.task("sp", reads=[WqkvB_r], writes=[wk_s_r],
                            dma_target=wk_s_r) as tk:
                    for c in range(NC):
                        tk.dma(wk_s[:, buf + c * 128: buf + c * 128 + width],
                               WqkvB[c * 128:(c + 1) * 128,
                                     D + m * hpc * hd: D + m * hpc * hd + width], wk_s_r)
                for ts4 in range(4):
                    ph, pr = PF[ts4 % 2]
                    with b.task("pe", reads=[wk_s_r, xnT_r], writes=[pr]) as tk:
                        pairs = [(wk_s[:, buf + c * 128: buf + c * 128 + width],
                                  xnT_sb[:, c * S + ts4 * 512: c * S + ts4 * 512 + 512])
                                 for c in range(NC)]
                        mm_group(tk, ph[0:width, :], pairs)
                    with b.task("act", reads=[pr], writes=[kt_r]) as tk:
                        tk.emit(lambda e, ph=ph, m=m, ts4=ts4, width=width: e.activation(
                            kt[0:width, m * S + ts4 * 512: m * S + ts4 * 512 + 512],
                            ph[0:width, :], AF.Identity))

            AR.mark()
            ksq, ksq_r = ab(f"l{lay}_ksq", 512)
            krow, krow_r = af(f"l{lay}_krow", 48)
            ncol, ncol_r = af(f"l{lay}_ncol", 16)
            ph_n, pr_n = PF[4]
            ph_t2, pr_t2 = PF[6]
            for m in range(nkc):
                nh = min(hpc, Hs - m * hpc)
                width = nh * hd
                for q4 in range(4):
                    with b.task("act", reads=[kt_r], writes=[ksq_r]) as tk:
                        tk.emit(lambda e, m=m, q4=q4, width=width: e.activation(
                            ksq[0:width, :], kt[0:width, m * S + q4 * 512: m * S + q4 * 512 + 512],
                            AF.Square))
                    with b.task("pe", reads=[ksq_r, blko_r], writes=[pr_n]) as tk:
                        tk.emit(lambda e, width=width, nh=nh: e.matmul(
                            ph_n[0:nh, 0:512], blko[0:width, 0:nh], ksq[0:width, :],
                            start=True, stop=True))
                    with b.task("dve", reads=[pr_n], writes=[ncol_r]) as tk:
                        tk.emit(lambda e, nh=nh, q4=q4: e.tensor_reduce(
                            ncol[0:nh, q4:q4 + 1], ph_n[0:nh, 0:512], AX.X, ALU.max))
                with b.task("dve", reads=[ncol_r], writes=[ncol_r]) as tk:
                    tk.emit(lambda e, nh=nh: e.tensor_reduce(
                        ncol[0:nh, 4:5], ncol[0:nh, 0:4], AX.X, ALU.max))
                with b.task("act", reads=[ncol_r], writes=[ncol_r]) as tk:
                    tk.emit(lambda e, nh=nh: e.activation(ncol[0:nh, 5:6], ncol[0:nh, 4:5],
                                                          AF.Sqrt))
                with b.task("pe", reads=[ncol_r, eye32_r], writes=[pr_t2]) as tk:
                    tk.emit(lambda e: e.transpose(
                        ph_t2[0:1, 0:128], ncol[:, 5:6], eye_f32[:]))
                with b.task("dve", reads=[pr_t2], writes=[krow_r]) as tk:
                    tk.emit(lambda e, m=m, nh=nh: e.tensor_copy(
                        krow[0:1, m * hpc:m * hpc + nh], ph_t2[0:1, 0:nh]))
            AR.reset_to_mark()

            if lay == 1:
                neglam, neglam_r = af("l1_neglam", 16)
                AF32.mark()
                lamv, lamv_r = af("l1_lam", 160)
                rmsbc, rmsbc_r = ab("l1_rmsbc", 64)
                AR.mark()
                rrow, rrow_r = ab("l1_rrow", 64)
                with b.task("pool", writes=[lamv_r], dma_target=lamv_r) as tk:
                    for i, nm in enumerate(["lambda_q1", "lambda_k1",
                                            "lambda_q2", "lambda_k2"]):
                        tk.dma(lamv[0:1, i * 32:(i + 1) * 32],
                               AP[f"L1_attn_{nm}"].rearrange("(o d) -> o d", o=1), lamv_r)
                with b.task("dve", reads=[lamv_r], writes=[lamv_r]) as tk:
                    tk.emit(lambda e: e.tensor_tensor(lamv[0:1, 128:160],
                                                      lamv[0:1, 0:32], lamv[0:1, 32:64],
                                                      ALU.mult))
                with b.task("dve", reads=[lamv_r], writes=[lamv_r]) as tk:
                    tk.emit(lambda e: e.tensor_reduce(lamv[0:1, 0:1], lamv[0:1, 128:160],
                                                      AX.X, ALU.add))
                with b.task("dve", reads=[lamv_r], writes=[lamv_r]) as tk:
                    tk.emit(lambda e: e.tensor_tensor(lamv[0:1, 128:160],
                                                      lamv[0:1, 64:96], lamv[0:1, 96:128],
                                                      ALU.mult))
                with b.task("dve", reads=[lamv_r], writes=[lamv_r]) as tk:
                    tk.emit(lambda e: e.tensor_reduce(lamv[0:1, 1:2], lamv[0:1, 128:160],
                                                      AX.X, ALU.add))
                with b.task("act", reads=[lamv_r], writes=[lamv_r]) as tk:
                    tk.emit(lambda e: e.activation(lamv[0:1, 2:3], lamv[0:1, 0:1], AF.Exp))
                with b.task("act", reads=[lamv_r], writes=[lamv_r]) as tk:
                    tk.emit(lambda e: e.activation(lamv[0:1, 3:4], lamv[0:1, 1:2], AF.Exp))
                with b.task("dve", reads=[lamv_r], writes=[lamv_r]) as tk:
                    tk.emit(lambda e: e.tensor_tensor(lamv[0:1, 4:5], lamv[0:1, 2:3],
                                                      lamv[0:1, 3:4], ALU.subtract))
                with b.task("dve", reads=[lamv_r], writes=[cbf_r]) as tk:
                    tk.emit(lambda e: e.tensor_scalar(cbf[0:1, 2:3], lamv[0:1, 4:5],
                                                      LAMBDA_INIT, -1.0, ALU.add, ALU.mult))
                with b.task("pe", reads=[cbf_r, ones_r], writes=[pr_t2]) as tk:
                    tk.emit(lambda e: e.matmul(ph_t2[:, 0:1], ones_sb[0:1, 0:128],
                                               cbf[0:1, 2:3], start=True, stop=True))
                with b.task("dve", reads=[pr_t2], writes=[neglam_r]) as tk:
                    tk.emit(lambda e: e.tensor_copy(neglam[:, 0:1], ph_t2[:, 0:1]))
                with b.task("pool", writes=[rrow_r], dma_target=rrow_r) as tk:
                    tk.dma(rrow[0:1, 0:64],
                           AP["L1_attn_rms_scale"].rearrange("(o d) -> o d", o=1), rrow_r)
                with b.task("pe", reads=[rrow_r, ones_r], writes=[pr_t2]) as tk:
                    tk.emit(lambda e: e.matmul(ph_t2[:, 0:64], ones_sb[0:1, 0:128],
                                               rrow[0:1, 0:64], start=True, stop=True))
                with b.task("act", reads=[pr_t2], writes=[rmsbc_r]) as tk:
                    tk.emit(lambda e: e.activation(rmsbc[:, 0:64], ph_t2[:, 0:64],
                                                   AF.Identity, scale=1.0 - LAMBDA_INIT))
                AF32.reset_to_mark()
                AR.reset_to_mark()

            qt, qt_r = ab(f"l{lay}_qt", nkc * Wq)
            qsq, qsq_r = ab(f"l{lay}_qsq", Wq)
            ptb, ptb_r = ab(f"l{lay}_ptb", 4 * Wq)
            ao_sb, ao_sb_r = ab(f"l{lay}_aosb", nsub * D)
            aoT, aoT_r = ab(f"l{lay}_aoT", D)
            wp_s, wp_s_r = ab(f"l{lay}_wps", 2 * D)
            qrow, qrow_r = af(f"l{lay}_qrow", 48)
            crow, crow_r = af(f"l{lay}_crow", 48)
            negc, negc_r = af(f"l{lay}_negc", 16)
            ncol2, ncol_r2 = af(f"l{lay}_ncol2", 16)
            ecol, ecol_r = af(f"l{lay}_ecol", 32)
            etmp, etmp_r = af(f"l{lay}_etmp", 256)
            tmpf, tmpf_r = af(f"l{lay}_tmpf", 512)
            Wproj = AP[wproj_name]
            for qs in range(nqs):
                for m in range(nkc):
                    nh = min(hpc, Hs - m * hpc)
                    width = nh * hd
                    buf = (m % 2) * NC * 128
                    with b.task("sp", reads=[WqkvB_r], writes=[wk_s_r],
                                dma_target=wk_s_r) as tk:
                        for c in range(NC):
                            tk.dma(wk_s[:, buf + c * 128: buf + c * 128 + width],
                                   WqkvB[c * 128:(c + 1) * 128,
                                         m * hpc * hd: m * hpc * hd + width], wk_s_r)
                    ph, pr = PF[5]
                    with b.task("pe", reads=[wk_s_r, xnT_r], writes=[pr]) as tk:
                        pairs = [(wk_s[:, buf + c * 128: buf + c * 128 + width],
                                  xnT_sb[:, c * S + qs * Wq: c * S + qs * Wq + Wq])
                                 for c in range(NC)]
                        mm_group(tk, ph[0:width, 0:Wq], pairs)
                    with b.task("act", reads=[pr], writes=[qt_r]) as tk:
                        tk.emit(lambda e, ph=ph, m=m, width=width: e.activation(
                            qt[0:width, m * Wq:(m + 1) * Wq], ph[0:width, 0:Wq],
                            AF.Identity, scale=scale))
                    with b.task("act", reads=[qt_r], writes=[qsq_r]) as tk:
                        tk.emit(lambda e, m=m, width=width: e.activation(
                            qsq[0:width, 0:Wq], qt[0:width, m * Wq:(m + 1) * Wq],
                            AF.Square))
                    with b.task("pe", reads=[qsq_r, blko_r], writes=[pr_n]) as tk:
                        tk.emit(lambda e, width=width, nh=nh: e.matmul(
                            ph_n[0:nh, 0:Wq], blko[0:width, 0:nh], qsq[0:width, 0:Wq],
                            start=True, stop=True))
                    with b.task("dve", reads=[pr_n], writes=[ncol_r2]) as tk:
                        tk.emit(lambda e, nh=nh: e.tensor_reduce(
                            ncol2[0:nh, 0:1], ph_n[0:nh, 0:Wq], AX.X, ALU.max))
                    with b.task("act", reads=[ncol_r2], writes=[ncol_r2]) as tk:
                        tk.emit(lambda e, nh=nh: e.activation(ncol2[0:nh, 1:2],
                                                              ncol2[0:nh, 0:1], AF.Sqrt))
                    with b.task("pe", reads=[ncol_r2, eye32_r], writes=[pr_t2]) as tk:
                        tk.emit(lambda e: e.transpose(
                            ph_t2[0:1, 0:128], ncol2[:, 1:2], eye_f32[:]))
                    with b.task("dve", reads=[pr_t2], writes=[qrow_r]) as tk:
                        tk.emit(lambda e, m=m, nh=nh: e.tensor_copy(
                            qrow[0:1, m * hpc:m * hpc + nh], ph_t2[0:1, 0:nh]))
                with b.task("dve", reads=[qrow_r, krow_r], writes=[crow_r]) as tk:
                    tk.emit(lambda e: e.tensor_tensor(crow[0:1, 0:Hs], qrow[0:1, 0:Hs],
                                                      krow[0:1, 0:Hs], ALU.mult))
                with b.task("dve", reads=[crow_r], writes=[cbf_r]) as tk:
                    tk.emit(lambda e: e.tensor_copy(cbf_row[0:1, 0:Hs], crow[0:1, 0:Hs]))

                kc_max = (qs * Wq + Wq) // 128 if causal else NT
                ngrp_h = NV if lay == 1 else NV // 2
                for vh in range(ngrp_h):
                    members = [vh, vh + 16] if lay == 1 else [2 * vh, 2 * vh + 1]
                    for mi, j in enumerate(members):
                        with b.task("pe", reads=[cbf_r, ones_r], writes=[pr_t2]) as tk:
                            tk.emit(lambda e, j=j: e.matmul(
                                ph_t2[:, 0:1], ones_sb[0:1, 0:128],
                                cbf_row[0:1, j:j + 1], start=True, stop=True))
                        with b.task("act", reads=[pr_t2], writes=[negc_r]) as tk:
                            tk.emit(lambda e, mi=mi: e.activation(
                                negc[:, mi:mi + 1], ph_t2[:, 0:1],
                                AF.Identity, scale=-1.0))
                    ao_banks = [PF[2], PF[3]]

                    def sbank(kc, mi):
                        if mi == 1:
                            return PF[(4, 5)[kc % 2]]
                        return PF[kc % 2]

                    def emit_score(kc):
                        for mi, j in enumerate(members):
                            m, base = chunk_of(j)
                            ph_s, pr_s = sbank(kc, mi)
                            with b.task("pe", reads=[kt_r, qt_r], writes=[pr_s]) as tk:
                                tk.emit(lambda e, m=m, base=base, kc=kc, ph_s=ph_s:
                                        e.matmul(
                                            ph_s[0:128, 0:Wq],
                                            kt[base:base + hd,
                                               m * S + kc * 128: m * S + (kc + 1) * 128],
                                            qt[base:base + hd, m * Wq:(m + 1) * Wq],
                                            start=True, stop=True))

                    emit_score(0)
                    for kc in range(kc_max):
                        if kc + 1 < kc_max:
                            emit_score(kc + 1)
                        for mi, j in enumerate(members):
                            ph_s, pr_s = sbank(kc, mi)
                            pslot = (kc % 2) * 2 * Wq + mi * Wq
                            if lay == 3:
                                for sub in range(nsub):
                                    gsub = qs * nsub + sub
                                    if kc > gsub:
                                        continue
                                    dlt = gsub - kc
                                    with b.task("dve", reads=[alibi_r, negc_r],
                                                writes=[ecol_r]) as tk:
                                        tk.emit(lambda e, j=j, dlt=dlt, sub=sub, mi=mi:
                                                e.tensor_scalar(
                                            ecol[:, 8 + 4 * mi + sub:9 + 4 * mi + sub],
                                            alibi_sb[:, 16 * j + dlt:16 * j + dlt + 1],
                                            negc[:, mi:mi + 1], None, ALU.add))
                                    with b.task("act", reads=[pr_s, ecol_r],
                                                writes=[ptb_r]) as tk:
                                        tk.emit(lambda e, ph_s=ph_s, pslot=pslot, sub=sub,
                                                mi=mi: e.activation(
                                            ptb[:, pslot + sub * 128:pslot + (sub + 1) * 128],
                                            ph_s[0:128, sub * 128:(sub + 1) * 128],
                                            AF.Exp, bias=ecol[:, 8 + 4 * mi + sub:
                                                            9 + 4 * mi + sub]))
                            else:
                                with b.task("act", reads=[pr_s, negc_r],
                                            writes=[ptb_r]) as tk:
                                    tk.emit(lambda e, ph_s=ph_s, pslot=pslot, mi=mi:
                                            e.activation(
                                                ptb[:, pslot:pslot + Wq], ph_s[0:128, 0:Wq],
                                                AF.Exp, bias=negc[:, mi:mi + 1]))
                            if causal and kc * 128 >= qs * Wq:
                                qoff = kc * 128 - qs * Wq
                                with b.task("dve", reads=[tri_r, ptb_r],
                                            writes=[ptb_r]) as tk:
                                    tk.emit(lambda e, pslot=pslot, qoff=qoff:
                                            e.tensor_tensor(
                                                ptb[:, pslot + qoff:pslot + qoff + 128],
                                                ptb[:, pslot + qoff:pslot + qoff + 128],
                                                tri_sb[:], ALU.mult))
                            ph_a, pr_a = ao_banks[mi]
                            with b.task("pe", reads=[ptb_r, vt_r], writes=[pr_a]) as tk:
                                for sub in range(nsub):
                                    if causal and kc > qs * nsub + sub:
                                        continue
                                    vm = vh if lay == 1 else j
                                    tk.emit(lambda e, ph_a=ph_a, pslot=pslot, sub=sub,
                                            kc=kc, vm=vm, kc_max=kc_max:
                                            e.matmul(
                                                ph_a[:, sub * 65:(sub + 1) * 65],
                                                ptb[:, pslot + sub * 128:
                                                    pslot + (sub + 1) * 128],
                                                vt[:, kc * NV * 65 + vm * 65:
                                                   kc * NV * 65 + (vm + 1) * 65],
                                                start=(kc == 0 and sub == 0),
                                                stop=(kc == kc_max - 1
                                                      and sub == nsub - 1)))
                    for sub in range(nsub):
                        if lay == 1:
                            ph1, pr1 = ao_banks[0]
                            ph2, pr2 = ao_banks[1]
                            with b.task("dve", reads=[pr1], writes=[ecol_r]) as tk:
                                tk.emit(lambda e, ph1=ph1, sub=sub: e.reciprocal(
                                    ecol[:, 2:3], ph1[:, sub * 65 + 64:sub * 65 + 65]))
                            with b.task("dve", reads=[pr2], writes=[ecol_r]) as tk:
                                tk.emit(lambda e, ph2=ph2, sub=sub: e.reciprocal(
                                    ecol[:, 3:4], ph2[:, sub * 65 + 64:sub * 65 + 65]))
                            with b.task("dve", reads=[ecol_r, neglam_r],
                                        writes=[ecol_r]) as tk:
                                tk.emit(lambda e: e.tensor_scalar(
                                    ecol[:, 4:5], ecol[:, 3:4], neglam[:, 0:1], None,
                                    ALU.mult))
                            with b.task("act", reads=[pr1, ecol_r], writes=[etmp_r]) as tk:
                                tk.emit(lambda e, ph1=ph1, sub=sub: e.activation(
                                    etmp[:, 0:64], ph1[:, sub * 65:sub * 65 + 64],
                                    AF.Identity, scale=ecol[:, 2:3]))
                            with b.task("dve", reads=[pr2, ecol_r, etmp_r],
                                        writes=[etmp_r]) as tk:
                                tk.emit(lambda e, ph2=ph2, sub=sub: e.scalar_tensor_tensor(
                                    etmp[:, 64:128], ph2[:, sub * 65:sub * 65 + 64],
                                    ecol[:, 4:5], etmp[:, 0:64], ALU.mult, ALU.add))
                            with b.task("dve", reads=[etmp_r], writes=[etmp_r]) as tk:
                                tk.emit(lambda e: e.tensor_tensor(
                                    etmp[:, 128:192], etmp[:, 64:128], etmp[:, 64:128],
                                    ALU.mult))
                            with b.task("dve", reads=[etmp_r], writes=[ecol_r]) as tk:
                                tk.emit(lambda e: e.tensor_reduce(
                                    ecol[:, 5:6], etmp[:, 128:192], AX.X, ALU.add))
                            with b.task("act", reads=[ecol_r, cmisc_r],
                                        writes=[ecol_r]) as tk:
                                tk.emit(lambda e: e.activation(
                                    ecol[:, 6:7], ecol[:, 5:6], AF.Sqrt,
                                    bias=cmisc_sb[:, 2:3], scale=1.0 / 64.0))
                            with b.task("dve", reads=[ecol_r], writes=[ecol_r]) as tk:
                                tk.emit(lambda e: e.reciprocal(ecol[:, 7:8], ecol[:, 6:7]))
                            with b.task("dve", reads=[etmp_r, ecol_r],
                                        writes=[etmp_r]) as tk:
                                tk.emit(lambda e: e.tensor_scalar(
                                    etmp[:, 192:256], etmp[:, 64:128], ecol[:, 7:8],
                                    None, ALU.mult))
                            with b.task("dve", reads=[etmp_r, rmsbc_r],
                                        writes=[ao_sb_r]) as tk:
                                tk.emit(lambda e, sub=sub, vh=vh: e.tensor_tensor(
                                    ao_sb[:, sub * D + vh * 64: sub * D + (vh + 1) * 64],
                                    etmp[:, 192:256], rmsbc[:, 0:64], ALU.mult))
                        else:
                            for mi, j in enumerate(members):
                                ph1, pr1 = ao_banks[mi]
                                with b.task("dve", reads=[pr1], writes=[ecol_r]) as tk:
                                    tk.emit(lambda e, ph1=ph1, sub=sub, mi=mi: e.reciprocal(
                                        ecol[:, 2 + mi:3 + mi],
                                        ph1[:, sub * 65 + 64:sub * 65 + 65]))
                                with b.task("act", reads=[pr1, ecol_r],
                                            writes=[ao_sb_r]) as tk:
                                    tk.emit(lambda e, ph1=ph1, sub=sub, j=j, mi=mi:
                                            e.activation(
                                        ao_sb[:, sub * D + j * 64: sub * D + (j + 1) * 64],
                                        ph1[:, sub * 65:sub * 65 + 64],
                                        AF.Identity, scale=ecol[:, 2 + mi:3 + mi]))
                for sub in range(nsub):
                    with b.task("pe", reads=[ao_sb_r, eye_r], writes=[pbf_r]) as tk:
                        for c in range(NC):
                            tk.emit(lambda e, sub=sub, c=c: e.transpose(
                                pbf_h[:, c * 128:(c + 1) * 128],
                                ao_sb[:, sub * D + c * 128: sub * D + (c + 1) * 128],
                                eye_sb[:]))
                    with b.task("dve", reads=[pbf_r], writes=[aoT_r]) as tk:
                        tk.emit(lambda e: e.tensor_copy(aoT[:, 0:D], pbf_h[:, 0:D]))
                    tglob = qs * nsub + sub
                    gslot = tglob % 2
                    with b.task("sp", reads=[gsp_r], writes=[gbuf_r],
                                dma_target=gbuf_r) as tk:
                        tk.dma(gbuf_sb[:, gslot * D:(gslot + 1) * D],
                               AP_gspill_tile(tglob), gbuf_r)
                    for c in range(NC):
                        buf = (c % 2) * D
                        with b.task("sp", reads=[WBF_R[wproj_name]], writes=[wp_s_r],
                                    dma_target=wp_s_r) as tk:
                            tk.dma(wp_s[:, buf:buf + D],
                                   WBF[wproj_name][c * 128:(c + 1) * 128, :], wp_s_r)
                        for half in range(2):
                            ph_y, pr_y = PF[half]
                            with b.task("pe", reads=[wp_s_r, aoT_r], writes=[pr_y]) as tk:
                                tk.emit(lambda e, c=c, buf=buf, half=half, ph_y=ph_y:
                                        e.matmul(
                                            ph_y[:],
                                            aoT[:, c * 128:(c + 1) * 128],
                                            wp_s[:, buf + half * 512:buf + half * 512 + 512],
                                            start=(c == 0), stop=(c == NC - 1)))
                    for half in range(2):
                        ph_y, pr_y = PF[half]
                        with b.task("dve", reads=[pr_y, gbuf_r], writes=[tmpf_r]) as tk:
                            tk.emit(lambda e, ph_y=ph_y, half=half, gslot=gslot:
                                    e.tensor_tensor(
                                        tmpf[:, 0:512], ph_y[:],
                                        gbuf_sb[:, gslot * D + half * 512:
                                                gslot * D + half * 512 + 512], ALU.mult))
                        with b.task("dve", reads=[tmpf_r, x_t[tglob]],
                                    writes=[x_t[tglob]]) as tk:
                            tk.emit(lambda e, tglob=tglob, half=half: e.tensor_tensor(
                                x_sb[:, tglob * D + half * 512: tglob * D + half * 512 + 512],
                                x_sb[:, tglob * D + half * 512: tglob * D + half * 512 + 512],
                                tmpf[:, 0:512], ALU.add))

        def l2_rms_transform():
            AR.mark()
            AF32.mark()
            r2t, r2t_r = af("l2_r2t", 128)
            r2flat, r2flat_r = ab("l2_r2flat", S)
            gp1, gp1_r = af("l2_gp1", 16)
            ph_t2, pr_t2 = PF[6]
            with b.task("pe", reads=[rstd2_r, eye32_r], writes=[pr_t2]) as tk:
                tk.emit(lambda e: e.transpose(ph_t2[0:16, 0:128], rstd2_sb[:, 0:16],
                                              eye_f32[:]))
            with b.task("dve", reads=[pr_t2], writes=[r2t_r]) as tk:
                tk.emit(lambda e: e.tensor_copy(r2t[0:16, 0:128], ph_t2[0:16, 0:128]))
            dscr_r = b.res("l2_dscr", dma=True)
            dflat = dscr.rearrange("p n -> (p n)")
            with b.task("sp", reads=[r2t_r], writes=[dscr_r], dma_target=dscr_r) as tk:
                tk.dma(dflat[0:S].rearrange("(t n) -> t n", n=128),
                       r2t[0:16, 0:128], dscr_r)
            with b.task("pool", reads=[dscr_r], writes=[r2flat_r],
                        dma_target=r2flat_r) as tk:
                tk.dma(r2flat[0:1, 0:S],
                       dflat[0:S].rearrange("(o x) -> o x", o=1), r2flat_r)
            with b.task("pool", writes=[gp1_r], dma_target=gp1_r) as tk:
                for c in range(NC):
                    tk.dma(gp1[:, c:c + 1],
                           AP["L2_attn_gamma"].rearrange("(c p) -> c p", p=128)[c].rearrange(
                               "(p o) -> p o", o=1), gp1_r)
            with b.task("dve", reads=[gp1_r], writes=[gp1_r]) as tk:
                tk.emit(lambda e: e.tensor_scalar(gp1[:, 8:16], gp1[:, 0:8], 1.0, None,
                                                  ALU.add))
            for ts4 in range(4):
                ph_b, pr_b = PF[4]
                with b.task("pe", reads=[r2flat_r, ones_r], writes=[pr_b]) as tk:
                    tk.emit(lambda e, ts4=ts4: e.matmul(
                        ph_b[:], ones_sb[0:1, 0:128],
                        r2flat[0:1, ts4 * 512:(ts4 + 1) * 512], start=True, stop=True))
                for c in range(NC):
                    with b.task("dve", reads=[pr_b, gp1_r, xnT_r], writes=[xnT_r]) as tk:
                        tk.emit(lambda e, c=c, ts4=ts4, ph_b=ph_b: e.scalar_tensor_tensor(
                            xnT_sb[:, c * S + ts4 * 512: c * S + ts4 * 512 + 512],
                            ph_b[:], gp1[:, 8 + c:9 + c],
                            xnT_sb[:, c * S + ts4 * 512: c * S + ts4 * 512 + 512],
                            ALU.mult, ALU.mult))
            AR.reset_to_mark()
            AF32.reset_to_mark()

        def debug_dump_g():
            gdb, gdb_r = ab("dbg_g", 2 * D)
            gf, gf_r = af("dbg_gf", 512)
            for t in range(NT):
                slot = t % 2
                with b.task("sp", reads=[gsp_r], writes=[gdb_r], dma_target=gdb_r) as tk:
                    tk.dma(gdb[:, slot * D:(slot + 1) * D], AP_gspill_tile(t), gdb_r)
                for half in range(2):
                    with b.task("dve", reads=[gdb_r], writes=[gf_r]) as tk:
                        tk.emit(lambda e, slot=slot, half=half: e.tensor_copy(
                            gf[:, 0:512],
                            gdb[:, slot * D + half * 512: slot * D + half * 512 + 512]))
                    with b.task("sp", reads=[gf_r], writes=[out_r], dma_target=out_r) as tk:
                        tk.dma(out.rearrange("(t p) d -> t p d", p=128)[t][:, half * 512:half * 512 + 512],
                               gf[:, 0:512], out_r)

        for lay in layers:
            ln_phase(lay)
            if lay == 0:
                layer0()
            else:
                if debug == "noffn":
                    with b.task("pool", reads=[x_r], writes=[gsp_r],
                                dma_target=gsp_r) as tk:
                        for t in range(NT):
                            tk.dma(AP_gspill_tile(t),
                                   x_sb[:, t * D:(t + 1) * D], gsp_r)
                else:
                    ffn_generic(lay)
                if debug == "w":
                    break
                if debug == "g":
                    debug_dump_g()
                    break
                if lay == 2:
                    l2_rms_transform()
                attn_generic(lay)
            reset_arenas()

        if debug is None:
            with b.task("sp", reads=x_t + [x_r], writes=[out_r], dma_target=out_r) as tk:
                for half in range(2):
                    tk.dma(out.rearrange("(h t p) d -> h p t d", h=2, p=128)[half],
                           x_sb[:, half * 8 * D:(half + 1) * 8 * D].rearrange(
                               "p (t d) -> p t d", d=D), out_r)
        b.final_wait("sp")

        with nc.Block() as block:
            b.replay(block)
        globals()["_LAST_B"] = b
    return nc



def _get_attention_slopes(n):
    def pow2_slopes(n):
        start = 2 ** (-(2 ** (-(math.log2(n) - 3))))
        return [start * start ** i for i in range(n)]
    if math.log2(n).is_integer():
        return pow2_slopes(n)
    c = 2 ** math.floor(math.log2(n))
    return pow2_slopes(c) + _get_attention_slopes(2 * c)[0::2][: n - c]


def make_consts():
    eye = np.eye(128, dtype=np.float32)
    kq = np.arange(128, dtype=np.float32)
    tri = (kq[:, None] <= kq[None, :]).astype(np.float32)
    slopes = np.asarray(_get_attention_slopes(16), dtype=np.float32)
    alibid = np.zeros((128, 256), dtype=np.float32)
    for h in range(16):
        for d in range(16):
            alibid[:, 16 * h + d] = slopes[h] * (kq - 128 * d - 63.5)
    onesrow = np.zeros((128, 512), dtype=np.float32)
    onesrow[0, :] = 1.0
    cmisc = np.zeros((128, 8), dtype=np.float32)
    cmisc[:, 0] = 1e-5
    cmisc[:, 1] = 1e-24
    cmisc[:, 2] = 1e-6
    cmisc[:, 3] = 1.0
    return {"c_eye": eye, "c_tri": tri, "c_alibid": alibid,
            "c_onesrow": onesrow, "c_misc": cmisc}


def kernel(**inputs):
    return _run(inputs, layers=(0, 1, 2, 3))


def _run(inputs, layers, x_override=None, debug=None):
    n = 8
    x_full = np.asarray(x_override if x_override is not None else inputs["L0_attn_X"],
                        dtype=np.float32)
    consts = make_consts()
    nc = build_program(list(layers), n_cores=n, debug=debug)
    in_maps = []
    for core in range(n):
        m = {"x_in": np.ascontiguousarray(x_full[core])}
        for k, v in inputs.items():
            if k == "L0_attn_X":
                continue
            m[k] = np.ascontiguousarray(np.asarray(v, dtype=np.float32))
        m.update(consts)
        in_maps.append(m)
    res = run_bass_kernel_spmd(nc, in_maps, core_ids=list(range(n)))
    out = np.stack([res.results[i]["out"] for i in range(n)], axis=0)
    return out.astype(np.float32)
